# version 18
# baseline (speedup 1.0000x reference)
import numpy as np
from contextlib import ExitStack
import concourse.bass as bass
import concourse.mybir as mybir
from concourse.bass_utils import run_bass_kernel_spmd

F32 = mybir.dt.float32
BF16 = mybir.dt.bfloat16
I32 = mybir.dt.int32
AF = mybir.ActivationFunctionType
ALU = mybir.AluOpType

S = 4096
DM = 1024
H = 16
NB = 64
FF = 3072
NT = S // 128
NCH = S // 512
QSCALE = 0.125
DQ = float(np.float32(1.0 / 127.0))
S2 = DQ * DQ
EPS = 1e-5


def attends(qb):
    if qb in (0, NB - 1):
        return list(range(NB))
    if qb == 1:
        return [0, 1, 2, NB - 1]
    if qb == NB - 2:
        return [0, NB - 3, NB - 2, NB - 1]
    return sorted({0, qb - 1, qb, qb + 1, NB - 1})


PAIRS = [(0, NB - 1)] + [(2 * t - 1, 2 * t) for t in range(1, NB // 2)]
NP_ = len(PAIRS)


def pair_runs(t):
    if t == 0:
        return [(0, NB)]
    lo, hi = max(1, 2 * t - 2), min(NB - 2, 2 * t + 1)
    return [(0, 1), (lo, hi - lo + 1), (NB - 1, 1)]


def _build_tables():
    seg_off, seg_w = [], []
    off = 0
    for t in range(NP_):
        seg_off.append(off)
        w = sum(n for _, n in pair_runs(t)) * 64
        seg_w.append(w)
        off += w
    arena_cols = off

    memsets = []
    for t in range(NP_):
        jt, jb = PAIRS[t]
        rects = []
        for half, j in ((0, jt), (1, jb)):
            c = 0
            bad = []
            for qs, n in pair_runs(t):
                for k in range(n):
                    if j not in attends(qs + k):
                        bad.append([c + k * 64, c + k * 64 + 64])
                c += n * 64
            merged = []
            for lo, hi in bad:
                if merged and merged[-1][1] == lo:
                    merged[-1][1] = hi
                else:
                    merged.append([lo, hi])
            rects += [(half, lo, hi) for lo, hi in merged]
        memsets.append(rects)

    ctx_lists = []
    total_w = 0
    for C in range(NCH):
        entries = []
        for t in range(NP_):
            c = 0
            for qs, n in pair_runs(t):
                lo, hi = max(qs, 8 * C), min(qs + n, 8 * C + 8)
                if lo < hi:
                    entries.append((t, seg_off[t] + c + (lo - qs) * 64,
                                    (lo - 8 * C) * 64, (hi - lo) * 64))
                c += n * 64
        ctx_lists.append(entries)
        total_w += sum(e[3] for e in entries)
    assert total_w == arena_cols, (total_w, arena_cols)
    return seg_off, seg_w, arena_cols, memsets, ctx_lists


SEG_OFF, SEG_W, ARENA_COLS, MEMSETS, CTX_LISTS = _build_tables()
SGROUPS = 8 + (NP_ - 1)
MS_INC_PER_U = sum(1 for t in range(1, NP_) if MEMSETS[t])


def _v_plan():
    plans = []
    for c in range(NCH):
        groups = []
        pA = (0, 64, 0 if c == 0 else 64, 0 if c == 0 else 4 * c)
        pB = (448, 64, 64 if c == 7 else 0, 0 if c == 7 else 4 * c + 4)
        if c in (0, 7):
            groups += [[pA], [pB]]
        else:
            groups += [[pA, pB]]
        for w in (1, 2, 3):
            groups.append([(w * 128 - 64, 128, 0, 4 * c + w)])
        plans.append(groups)
    return plans


V_PLAN = _v_plan()
V_COPIES = [[sum(2 if p[1] == 128 else 1 for p in g) for g in plan]
            for plan in V_PLAN]
V_COPY_PREFIX = []
_acc = 0
for _cnts in V_COPIES:
    for _cnt in _cnts:
        V_COPY_PREFIX.append(_acc)
        _acc += _cnt
V_COPY_TOTAL = _acc
VG_PER_CHUNK = [len(p) for p in V_PLAN]
VG_PREFIX = [0]
for _n in VG_PER_CHUNK:
    VG_PREFIX.append(VG_PREFIX[-1] + _n)


def build_nc(debug=(), phases=6):
    nc = bass.Bass("TRN2", target_bir_lowering=False)

    def scratch(name, shape, dtype):
        kind = "ExternalOutput" if name in debug else "Internal"
        return nc.dram_tensor(name, shape, dtype, kind=kind).ap()

    x_in = nc.dram_tensor("x_in", [S, DM], F32, kind="ExternalInput").ap()
    wqT = nc.dram_tensor("wqT", [DM, DM], F32, kind="ExternalInput").ap()
    wkT = nc.dram_tensor("wkT", [DM, DM], F32, kind="ExternalInput").ap()
    wvT = nc.dram_tensor("wvT", [DM, DM], F32, kind="ExternalInput").ap()
    bq_d = nc.dram_tensor("bq", [DM], F32, kind="ExternalInput").ap()
    bk_d = nc.dram_tensor("bk", [DM], F32, kind="ExternalInput").ap()
    bv_d = nc.dram_tensor("bv", [DM], F32, kind="ExternalInput").ap()
    wgT_d = nc.dram_tensor("wgT", [DM, FF], I32, kind="ExternalInput").ap()
    wuT_d = nc.dram_tensor("wuT", [DM, FF], I32, kind="ExternalInput").ap()
    wdT_d = nc.dram_tensor("wdT", [FF, DM], I32, kind="ExternalInput").ap()
    out_d = nc.dram_tensor("out", [S, DM], F32, kind="ExternalOutput").ap()

    xn1_d = scratch("xn1_d", [S, DM], BF16)
    qkt_d = scratch("qkt_d", [2, 8, 128, S], BF16)
    ctxT_d = scratch("ctxT_d", [1040, S], BF16)
    ln2_d = scratch("ln2_d", [S, DM], BF16)
    x1_d = scratch("x1_d", [S, DM], F32)
    xn2_d = scratch("xn2_d", [S, DM], BF16)

    es = ExitStack()
    es_qkv = ExitStack()
    es_v = ExitStack()
    es_ffn = ExitStack()
    sems = {}

    def sem(name):
        sems[name] = es.enter_context(nc.semaphore(name))

    for n in ("s_bb s_wq s_bvr s_xld0 s_xld1 s_xld2 s_xld3 s_bn1 s_va s_sq s_vb s_ln1 "
              "s_sp10 s_sp11 "
              "s_trl0 s_trl1 s_qg s_kg s_vg s_qev s_kev s_vev "
              "s_qsp0 s_qsp1 s_ksp0 s_ksp1 "
              "s_ql0 s_ql1 s_spe s_exp s_ms s_cpe s_cev0 s_cev1 "
              "s_cs00 s_cs01 s_cs10 s_cs11 "
              "s_ctl0 s_ctl1 s_ctl2 s_ctl3 s_ctl4 s_ctl5 s_ctl6 s_ctl7 s_bn3 s_v3a s_s3a s_v3b s_s3b s_sp30 s_sp31 "
              "s_xl0 s_xl1 s_xl2 s_xl3 s_xl4 s_xl5 s_xl6 s_xl7 s_bn3b s_v4a s_s4a s_v4b s_s4b s_sp3b0 s_sp3b1 "
              "s_wgu s_wua s_wub s_wuc s_wud s_x2t0 s_x2t1 s_x1f0 s_x1f1 s_gu s_sg s_hu s_dpe s_yraw "
              "s_bn4 s_v5a s_s5a s_v5b s_s5b s_oady s_out0 s_out1 s_dph s_wd0 s_wd1 s_wd2 s_wd3 s_wg0 s_wg1 s_wg2 s_wg3").split():
        sem(n)
    sm = type("SM", (), sems)

    bv_rep = es.enter_context(nc.sbuf_tensor("bv_rep", [128, DM], F32, side="right"))
    wq_sb = es_qkv.enter_context(nc.sbuf_tensor("wq_sb", [128, 8 * DM], BF16))
    wk_sb = es_qkv.enter_context(nc.sbuf_tensor("wk_sb", [128, 8 * DM], BF16))
    wv_sb = es_qkv.enter_context(nc.sbuf_tensor("wv_sb", [128, 8 * DM], BF16))
    bqsT = es_qkv.enter_context(nc.sbuf_tensor("bqsT", [128, 8], F32))
    bkT = es_qkv.enter_context(nc.sbuf_tensor("bkT", [128, 8], F32))
    hold = {}

    s_wu_t = (sm.s_wua, sm.s_wub, sm.s_wuc)
    s_wd_t = (sm.s_wd0, sm.s_wd1, sm.s_wd2, sm.s_wd3)
    s_wg_t = (sm.s_wg0, sm.s_wg1, sm.s_wg2, sm.s_wg3)

    def ffn_silu(s, c, hc):
        ghc = 24 * c + hc
        s.wait_ge(sm.s_sp30, ghc + 1)
        if c >= 1:
            s.wait_ge(sm.s_dph, 48 * (c - 1) + 24 + hc + 1)
        s.activation(hold["hswb"][:, hc * 512:(hc + 1) * 512],
                     hold["gp"][:, (ghc % 2) * 512:(ghc % 2 + 1) * 512],
                     AF.Silu, scale=DQ).then_inc(sm.s_sg, 1)

    def ffn_tt(v, c, hc):
        ghc = 24 * c + hc
        v.wait_ge(sm.s_sg, ghc + 1)
        v.wait_ge(sm.s_gu, ghc + 1)
        for half in (0, 1):
            o = half * 256
            v.tensor_tensor(
                hold["hswb"][:, hc * 512 + o:hc * 512 + o + 256],
                hold["hswb"][:, hc * 512 + o:hc * 512 + o + 256],
                hold["upp"][:, (ghc % 2) * 512 + o:(ghc % 2) * 512 + o + 256],
                ALU.mult).then_inc(sm.s_hu, 1)

    def ffn_ycopy_t(v, gt):
        pas = (gt % 4) // 2
        v.wait_ge(sm.s_dph, 24 * (2 * (gt // 4) + pas + 1))
        if gt >= 2:
            v.wait_ge(sm.s_s5b, gt - 1)
        if gt >= 4 and gt % 4 >= 2:
            ysrc = (hold["gp"], hold["upp"])[gt % 2][:, 0:DM]
        else:
            ysrc = hold["yp"][:, (gt % 2) * DM:(gt % 2 + 1) * DM]
        v.tensor_copy(hold["y_sb"][:, (gt % 2) * DM:(gt % 2 + 1) * DM],
                      ysrc).then_inc(sm.s_yraw, 1)

    def ffn_gateup(te, c):
        par = c % 2
        te.wait_ge((sm.s_x2t0, sm.s_x2t1)[par], 16 * 8 * (c // 2 + 1))
        if c >= 2:
            te.wait_ge(sm.s_yraw, 4 * c)
        x2T = hold["x2T"]

        def xsl(dc):
            slot = (8 * c + dc) % 12
            return x2T[:, slot * 512:(slot + 1) * 512]
        for hc in range(24):
            ghc = 24 * c + hc
            if ghc >= 2:
                te.wait_ge(sm.s_hu, 2 * ghc - 2)
            te.wait_ge(s_wg_t[hc // 6], 16)
            te.wait_ge(s_wu_t[hc % 3], 16 * (8 * c + hc // 3 + 1))
            gsl = hold["gp"][:, (ghc % 2) * 512:(ghc % 2 + 1) * 512]
            usl = hold["upp"][:, (ghc % 2) * 512:(ghc % 2 + 1) * 512]
            for dc in range(8):
                mm = te.matmul(gsl,
                               hold["wg_sb"][:, dc * FF + hc * 128:dc * FF + (hc + 1) * 128],
                               xsl(dc), start=(dc == 0), stop=(dc == 7))
            mm.then_inc(sm.s_sp30, 1)
            for dc in range(8):
                mm = te.matmul(usl,
                               hold["wu_sl"][:, (hc % 3) * 1024 + dc * 128:
                                             (hc % 3) * 1024 + (dc + 1) * 128],
                               xsl(dc), start=(dc == 0), stop=(dc == 7))
            mm.then_inc(sm.s_gu, 1)

    def ffn_down(te, c, pas):
        te.wait_ge(sm.s_sg, 24 * (c + 1))
        te.wait_ge(sm.s_hu, 48 * (c + 1))
        if pas == 0:
            if c == 1:
                te.wait_ge(sm.s_yraw, 4)
            elif c >= 2:
                te.wait_ge(sm.s_yraw, 4 * c - 2)
        elif c == 0:
            te.wait_ge(sm.s_yraw, 2)
        for hc in range(24):
            te.wait_ge(s_wd_t[hc // 6], 16)
            for k in (0, 1):
                tt = 2 * pas + k
                if pas == 1 and c >= 1:
                    tgt = (hold["gp"], hold["upp"])[k]
                    off = 0
                else:
                    tgt = hold["yp"]
                    off = k * DM
                for half in (0, 1):
                    mm = te.matmul(
                        tgt[:, off + half * 512:off + half * 512 + 512],
                        hold["hswb"][:, hc * 512 + tt * 128:
                                      hc * 512 + tt * 128 + 128],
                        hold["wd_sb"][:, hc * DM + half * 512:hc * DM + (half + 1) * 512],
                        start=(hc == 0), stop=(hc == 23))
            mm.then_inc(sm.s_dph, 1)

    def wu_loads(g, c, hc0, hc1):
        for hc in range(hc0, hc1):
            ghc = 24 * c + hc
            if ghc >= 3:
                g.wait_ge(sm.s_gu, ghc - 2)
            sl = hc % 3
            g.dma_start(
                hold["wu_sl"][:, sl * 1024:(sl + 1) * 1024].rearrange(
                    "p (dc e) -> p dc e", dc=8),
                wuT_d[:, hc * 128:(hc + 1) * 128].rearrange(
                    "(dc p) e -> p dc e", p=128)).then_inc(s_wu_t[sl], 16)

    def x2t_gate(c, dc):
        s = (8 * c + dc) % 12
        for cc in (c - 1, c - 2):
            if cc >= 0 and (s - 8 * cc) % 12 < 8:
                return 24 * (cc + 1)
        return 0

    def x2t_loads(sy, c, dcs=None):
        if dcs is None:
            dcs = sorted(range(8), key=lambda d: x2t_gate(c, d))
        sy.wait_ge(sm.s_sp3b0, 64 * (c + 1))
        sy.wait_ge(sm.s_sp3b1, 64 * (c + 1))
        gcur = 0
        for dc in dcs:
            gq = x2t_gate(c, dc)
            if gq > gcur:
                sy.wait_ge(sm.s_gu, gq)
                gcur = gq
            slot = (8 * c + dc) % 12
            sy.dma_start_transpose(
                hold["x2T"][:, slot * 512:(slot + 1) * 512],
                xn2_d[c * 512:(c + 1) * 512, dc * 128:(dc + 1) * 128]).then_inc(
                (sm.s_x2t0, sm.s_x2t1)[c % 2], 16)

    def _p1a():
        NI = NT // 2
        with (
            nc.sbuf_tensor("xbuf", [128, 8 * DM], F32) as xbuf,
            nc.sbuf_tensor("xn1buf", [128, 4 * DM], BF16) as xn1buf,
            nc.sbuf_tensor("bv_row", [1, DM], BF16) as bv_row,
            nc.sbuf_tensor("ones1", [1, 128], BF16) as ones1,
            nc.sbuf_tensor("bst1", [128, 2 * 24], F32) as bst1,
            nc.sbuf_tensor("agg1", [128, 2 * 4], F32) as agg1,
            nc.sbuf_tensor("sc1", [128, 2 * 8], F32) as sc1,
            nc.psum_tensor("bvp", [128, DM], F32) as bvp,
            nc.Block() as blk,
        ):
            s_xl = (sm.s_xld0, sm.s_xld1, sm.s_xld2, sm.s_xld3)

            @blk.gpsimd
            def _(g):
                g.dma_start(bv_row[:], bv_d[None, :]).then_inc(sm.s_bb, 16)
                for u in range(8):
                    g.dma_start(bqsT[:, u:u + 1],
                                bq_d[u * 128:(u + 1) * 128][:, None]).then_inc(sm.s_bb, 16)
                for u in range(8):
                    g.dma_start(bkT[:, u:u + 1],
                                bk_d[u * 128:(u + 1) * 128][:, None]).then_inc(sm.s_bb, 16)
                for w_d, w_sb in ((wqT, wq_sb), (wkT, wk_sb), (wvT, wv_sb)):
                    g.dma_start(w_sb[:].rearrange("p (dc c) -> p dc c", dc=8),
                                w_d.rearrange("(dc p) c -> p dc c", p=128)).then_inc(sm.s_wq, 16)

            @blk.sync
            def _(sy):
                for it in range(NI):
                    if it >= 4:
                        sy.wait_ge(sm.s_ln1, 2 * it - 6)
                    for k in (0, 1):
                        t = 2 * it + k
                        sy.dma_start(xbuf[:, (t % 8) * DM:(t % 8 + 1) * DM],
                                     x_in[t * 128:(t + 1) * 128, :]).then_inc(
                            s_xl[it % 4], 16)

            @blk.tensor
            def _(te):
                te.wait_ge(sm.s_bb, 272)
                te.wait_ge(sm.s_bvr, 1)
                te.matmul(bvp[:, 0:512], ones1[:], bv_row[:, 0:512], start=True, stop=True)
                te.matmul(bvp[:, 512:1024], ones1[:], bv_row[:, 512:1024],
                          start=True, stop=True).then_inc(sm.s_bvr, 1)

            @blk.vector
            def _(v):
                hz = [0]

                def hop(instr):
                    instr.then_inc(sm.s_bn1, 1)
                    hz[0] += 1
                    v.wait_ge(sm.s_bn1, hz[0])
                v.memset(ones1[:], 1.0).then_inc(sm.s_bvr, 1)
                v.wait_ge(sm.s_bvr, 2)
                v.tensor_copy(bv_rep[:], bvp[:])
                v.wait_ge(sm.s_bb, 272)
                v.tensor_scalar_mul(bqsT[:], bqsT[:], QSCALE)
                for it in range(NI + 1):
                    pb = it % 2
                    if it < NI:
                        v.wait_ge(s_xl[it % 4], 32 * (it // 4 + 1))
                        for k in (0, 1):
                            t = 2 * it + k
                            sl = (t % 8) * DM
                            v.bn_stats(bst1[:, pb * 24 + k * 12:pb * 24 + k * 12 + 6],
                                       xbuf[:, sl:sl + 512])
                            ins = v.bn_stats(bst1[:, pb * 24 + k * 12 + 6:pb * 24 + k * 12 + 12],
                                             xbuf[:, sl + 512:sl + 1024])
                        hop(ins)
                        v.bn_aggr(agg1[:, pb * 4:pb * 4 + 2], bst1[:, pb * 24:pb * 24 + 12])
                        hop(v.bn_aggr(agg1[:, pb * 4 + 2:pb * 4 + 4],
                                      bst1[:, pb * 24 + 12:pb * 24 + 24]))
                        v.tensor_scalar_add(
                            sc1[:, pb * 8:pb * 8 + 2],
                            agg1[:, pb * 4 + 1:pb * 4 + 4:2],
                            EPS).then_inc(sm.s_va, 1)
                    if it >= 1:
                        qb = (it - 1) % 2
                        v.wait_ge(sm.s_sq, it)
                        hop(v.reciprocal(sc1[:, qb * 8 + 4:qb * 8 + 6],
                                         sc1[:, qb * 8 + 2:qb * 8 + 4]))
                        v.scalar_tensor_tensor(
                            sc1[:, qb * 8 + 6:qb * 8 + 8],
                            agg1[:, qb * 4:qb * 4 + 3:2], -1.0,
                            sc1[:, qb * 8 + 4:qb * 8 + 6],
                            ALU.mult, ALU.mult).then_inc(sm.s_vb, 1)

            @blk.scalar
            def _(s):
                for it in range(NI + 1):
                    pb = it % 2
                    if it < NI:
                        s.wait_ge(sm.s_va, it + 1)
                        s.activation(sc1[:, pb * 8 + 2:pb * 8 + 4],
                                     sc1[:, pb * 8:pb * 8 + 2], AF.Sqrt).then_inc(sm.s_sq, 1)
                    if it >= 1:
                        qb = (it - 1) % 2
                        s.wait_ge(sm.s_vb, it)
                        if it - 1 >= 2:
                            s.wait_ge((sm.s_sp10, sm.s_sp11)[qb], 32 * ((it - 1) // 2))
                        for k in (0, 1):
                            t = 2 * (it - 1) + k
                            s.activation(xn1buf[:, (t % 4) * DM:(t % 4 + 1) * DM],
                                         xbuf[:, (t % 8) * DM:(t % 8 + 1) * DM], AF.Identity,
                                         bias=sc1[:, qb * 8 + 6 + k:qb * 8 + 7 + k],
                                         scale=sc1[:, qb * 8 + 4 + k:qb * 8 + 5 + k]).then_inc(sm.s_ln1, 1)
                            s.wait_ge(sm.s_ln1, t + 1)
                            s.dma_start(xn1_d[t * 128:(t + 1) * 128, :],
                                        xn1buf[:, (t % 4) * DM:(t % 4 + 1) * DM]).then_inc(
                                (sm.s_sp10, sm.s_sp11)[(t // 2) % 2], 16)

    def _p1b():
        v_sb = hold["v_sb"]
        qstage = hold.get("qstage")
        kstage = hold.get("kstage")
        if qstage is None:
            qstage = es_v.enter_context(
                nc.sbuf_tensor("qstage", [128, 2 * 512], BF16, side="right"))
            kstage = es_v.enter_context(
                nc.sbuf_tensor("kstage", [128, 2 * 512], BF16, side="right"))
        with (
            nc.sbuf_tensor("xn1T", [128, 2 * 8 * 512], BF16) as xn1T,
            nc.psum_tensor("qp", [128, 2 * 512], F32) as qp,
            nc.psum_tensor("kp", [128, 2 * 512], F32) as kp,
            nc.psum_tensor("vp", [128, 2 * 1024], F32) as vp,
            nc.Block() as blk,
        ):
            def xn1T_sl(par, dc, lo=0, w=512):
                base = par * 8 * 512 + dc * 512
                return xn1T[:, base + lo: base + lo + w]

            @blk.vector
            def _(v):
                v.memset(v_sb[:, 64:32 * 1040:65], 1.0).then_inc(sm.s_vev, 1)
                gv = 0
                for c in range(NCH):
                    for u in range(8):
                        gq = c * 8 + u
                        v.wait_ge(sm.s_qg, gq + 1)
                        if gq >= 2:
                            v.wait_ge((sm.s_qsp0, sm.s_qsp1)[gq % 2], 16 * (gq // 2))
                        par = gq % 2
                        v.tensor_scalar(qstage[:, par * 512:(par + 1) * 512],
                                        qp[:, par * 512:(par + 1) * 512],
                                        QSCALE, bqsT[:, u:u + 1],
                                        ALU.mult, ALU.add).then_inc(sm.s_qev, 1)
                    for grp in V_PLAN[c]:
                        v.wait_ge(sm.s_vg, gv + 1)
                        par = gv % 2
                        psl = vp[:, par * 1024:(par + 1) * 1024]
                        for (lhs_lo, lhs_w, plo, slot) in grp:
                            rlos = (0, 64) if lhs_w == 128 else (plo,)
                            for rlo in rlos:
                                dst = v_sb[rlo:rlo + 64,
                                           slot * 1040:slot * 1040 + 1040]
                                dst = dst.rearrange("p (h e) -> p h e", e=65)[:, :, 0:64]
                                v.tensor_copy(dst, psl[rlo:rlo + 64, :].rearrange(
                                    "p (h e) -> p h e", e=64)).then_inc(sm.s_vev, 1)
                        gv += 1

            @blk.sync
            def _(sy):
                for c in range(NCH):
                    sy.wait_ge(sm.s_sp10, 32 * (c + 1))
                    sy.wait_ge(sm.s_sp11, 32 * (c + 1))
                    if c >= 2:
                        sy.wait_ge(sm.s_vg, VG_PREFIX[c - 1])
                    for dc in range(8):
                        sy.dma_start_transpose(
                            xn1T_sl(c % 2, dc),
                            xn1_d[c * 512:(c + 1) * 512, dc * 128:(dc + 1) * 128]
                        ).then_inc((sm.s_trl0, sm.s_trl1)[c % 2], 16)
                if "qt_buf" in hold:
                    for u in (0, 1):
                        cnt = len([g for g in range(64)
                                   if g % 2 == u % 2 and g <= 56 + u])
                        sy.wait_ge((sm.s_qsp0, sm.s_qsp1)[u % 2], 16 * cnt)
                        sy.wait_ge((sm.s_ksp0, sm.s_ksp1)[u % 2], 16 * cnt)
                        sy.dma_start(hold["qt_buf"][:, (u % 2) * S:(u % 2 + 1) * S],
                                     qkt_d[0, u, :, :]).then_inc(
                            (sm.s_ql0, sm.s_ql1)[u % 2], 16)
                        sy.dma_start(hold["kt_buf"][:, (u % 2) * S:(u % 2 + 1) * S],
                                     qkt_d[1, u, :, :]).then_inc(
                            (sm.s_ql0, sm.s_ql1)[u % 2], 16)

            @blk.tensor
            def _(te):
                te.wait_ge(sm.s_wq, 48)
                gv = 0
                for c in range(NCH):
                    te.wait_ge((sm.s_trl0, sm.s_trl1)[c % 2], 16 * 8 * (c // 2 + 1))
                    par = c % 2
                    for u in range(8):
                        gq = c * 8 + u
                        if gq >= 2:
                            te.wait_ge(sm.s_qev, gq - 1)
                            te.wait_ge(sm.s_kev, gq - 1)
                        qsl = qp[:, (gq % 2) * 512:(gq % 2 + 1) * 512]
                        ksl = kp[:, (gq % 2) * 512:(gq % 2 + 1) * 512]
                        for dc in range(8):
                            mm = te.matmul(qsl,
                                           wq_sb[:, dc * DM + u * 128:dc * DM + (u + 1) * 128],
                                           xn1T_sl(par, dc), start=(dc == 0), stop=(dc == 7))
                        mm.then_inc(sm.s_qg, 1)
                        for dc in range(8):
                            mm = te.matmul(ksl,
                                           wk_sb[:, dc * DM + u * 128:dc * DM + (u + 1) * 128],
                                           xn1T_sl(par, dc), start=(dc == 0), stop=(dc == 7))
                        mm.then_inc(sm.s_kg, 1)
                    for grp in V_PLAN[c]:
                        if gv >= 2:
                            te.wait_ge(sm.s_vev, 1 + V_COPY_PREFIX[gv - 1])
                        vsl = vp[:, (gv % 2) * 1024:(gv % 2 + 1) * 1024]
                        for half in (0, 1):
                            for (lhs_lo, lhs_w, plo, slot) in grp:
                                for dc in range(8):
                                    mm = te.matmul(
                                        vsl[plo:plo + lhs_w, half * 512:(half + 1) * 512],
                                        xn1T_sl(par, dc, lhs_lo, lhs_w),
                                        wv_sb[:, dc * DM + half * 512:dc * DM + (half + 1) * 512],
                                        start=(dc == 0), stop=(dc == 7),
                                        tile_position=(0, plo))
                        mm.then_inc(sm.s_vg, 1)
                        gv += 1

            @blk.scalar
            def _(s):
                for gq in range(8 * NCH):
                    c, u = gq // 8, gq % 8
                    s.wait_ge(sm.s_kg, gq + 1)
                    if gq >= 2:
                        s.wait_ge((sm.s_ksp0, sm.s_ksp1)[gq % 2], 16 * (gq // 2))
                    par = gq % 2
                    s.activation(kstage[:, par * 512:(par + 1) * 512],
                                 kp[:, par * 512:(par + 1) * 512], AF.Identity,
                                 bias=bkT[:, u:u + 1], scale=1.0).then_inc(sm.s_kev, 1)

            @blk.gpsimd
            def _(g):
                for gq in range(8 * NCH):
                    c, u = gq // 8, gq % 8
                    par = gq % 2
                    g.wait_ge(sm.s_qev, gq + 1)
                    g.dma_start(qkt_d[0, u, :, c * 512:(c + 1) * 512],
                                qstage[:, par * 512:(par + 1) * 512]).then_inc(
                        (sm.s_qsp0, sm.s_qsp1)[par], 16)
                    g.wait_ge(sm.s_kev, gq + 1)
                    g.dma_start(qkt_d[1, u, :, c * 512:(c + 1) * 512],
                                kstage[:, par * 512:(par + 1) * 512]).then_inc(
                        (sm.s_ksp0, sm.s_ksp1)[par], 16)

    def _p2():
        v_sb = hold["v_sb"]
        A = ARENA_COLS
        ctxb = hold["ctxb"]
        qt_buf, kt_buf = hold["qt_buf"], hold["kt_buf"]
        s_ct = (sm.s_ctl0, sm.s_ctl1, sm.s_ctl2, sm.s_ctl3,
                sm.s_ctl4, sm.s_ctl5, sm.s_ctl6, sm.s_ctl7)
        with (
            nc.sbuf_tensor("arenaB", [128, 2 * A], BF16) as arenaB,
            nc.sbuf_tensor("cstage0", [65, 2 * 512], BF16) as cstage0,
            nc.sbuf_tensor("cstage1", [65, 2 * 512], BF16) as cstage1,
            nc.psum_tensor("pp8", [128, 8 * 512], F32) as pp8,
            nc.Block() as blk,
        ):
            def s_cols(t, gi):
                if t == 0:
                    return 512 * gi, 512
                return SEG_OFF[t], SEG_W[t]

            @blk.sync
            def _(sy):
                for u in range(2, 8):
                    sy.wait_ge(sm.s_qsp0, 16 * 4 * NCH)
                    sy.wait_ge(sm.s_qsp1, 16 * 4 * NCH)
                    sy.wait_ge(sm.s_ksp0, 16 * 4 * NCH)
                    sy.wait_ge(sm.s_ksp1, 16 * 4 * NCH)
                    sy.wait_ge(sm.s_spe, SGROUPS * (u - 1))
                    sy.dma_start(qt_buf[:, (u % 2) * S:(u % 2 + 1) * S],
                                 qkt_d[0, u, :, :]).then_inc((sm.s_ql0, sm.s_ql1)[u % 2], 16)
                    sy.dma_start(kt_buf[:, (u % 2) * S:(u % 2 + 1) * S],
                                 qkt_d[1, u, :, :]).then_inc((sm.s_ql0, sm.s_ql1)[u % 2], 16)
                for t in range(4):
                    Ct = t // 4
                    for h01, sems in ((0, (sm.s_cs00, sm.s_cs01)),
                                      (1, (sm.s_cs10, sm.s_cs11))):
                        for par in (0, 1):
                            ncnt = (Ct - par) // 2 + 1 if Ct >= par else 0
                            if 28 + ncnt > 0:
                                sy.wait_ge(sems[par], 16 * (28 + ncnt))
                    sy.dma_start_transpose(
                        ctxb[:, (t % 4) * 1040:(t % 4 + 1) * 1040],
                        ctxT_d[:, t * 128:(t + 1) * 128]).then_inc(s_ct[t % 4], 16)

            @blk.tensor
            def _(te):
                last_use = {}
                for u in range(8):
                    up = (u % 2) * S
                    te.wait_ge((sm.s_ql0, sm.s_ql1)[u % 2], 32 * (u // 2 + 1))
                    if u >= 1:
                        te.wait_ge(sm.s_cev0, 8 * u)
                        te.wait_ge(sm.s_cev1, 8 * u)
                    for gi in range(SGROUPS):
                        gg = u * SGROUPS + gi
                        par = gi % 4
                        if par in last_use:
                            te.wait_ge(sm.s_exp, last_use[par] + 1)
                        last_use[par] = gg
                        t = 0 if gi < 8 else gi - 7
                        jt, jb = PAIRS[t]
                        runs = [(8 * gi, 8)] if t == 0 else pair_runs(t)
                        col = 0
                        last = None
                        for (qs, n) in runs:
                            w = n * 64
                            for (rows, hoff) in ((0, 0), (64, 512)):
                                rhs = qt_buf[rows:rows + 64,
                                             up + qs * 64: up + qs * 64 + w]
                                for (tcol, j) in ((0, jt), (64, jb)):
                                    last = te.matmul(
                                        pp8[tcol:tcol + 64,
                                            par * 1024 + hoff + col: par * 1024 + hoff + col + w],
                                        kt_buf[rows:rows + 64,
                                               up + j * 64: up + j * 64 + 64],
                                        rhs, start=True, stop=True,
                                        tile_position=(rows, tcol))
                            col += w
                        last.then_inc(sm.s_spe, 1)
                    te.wait_ge(sm.s_exp, SGROUPS * (u + 1))
                    te.wait_ge(sm.s_ms, MS_INC_PER_U * (u + 1))
                    for C in range(NCH):
                        for h01 in (0, 1):
                            if C >= 2:
                                te.wait_ge(sm.s_cev0, 8 * u + C - 1)
                                te.wait_ge(sm.s_cev1, 8 * u + C - 1)
                            elif u >= 1:
                                te.wait_ge(sm.s_cev0, 8 * u)
                                te.wait_ge(sm.s_cev1, 8 * u)
                            pcoff = (h01 * 2 + (C % 2)) * 512
                            h = 2 * u + h01
                            n_e = len(CTX_LISTS[C])
                            for ei, (t, acol, pcol, w) in enumerate(CTX_LISTS[C]):
                                mm = te.matmul(
                                    pp8[0:65, pcoff + pcol:pcoff + pcol + w],
                                    v_sb[:, t * 1040 + h * 65: t * 1040 + h * 65 + 65],
                                    arenaB[:, h01 * A + acol:h01 * A + acol + w],
                                    start=(ei == 0), stop=(ei == n_e - 1))
                            mm.then_inc(sm.s_cpe, 1)

            @blk.scalar
            def _(s):
                for u in range(8):
                    for gi in range(SGROUPS):
                        gg = u * SGROUPS + gi
                        par = gi % 4
                        t = 0 if gi < 8 else gi - 7
                        acol, w = s_cols(t, gi)
                        s.wait_ge(sm.s_spe, gg + 1)
                        src_ap = pp8[:, par * 1024:(par + 1) * 1024].rearrange(
                            "p (b c) -> p b c", b=2)[:, :, 0:w]
                        dst_ap = arenaB[:].rearrange("p (b c) -> p b c", b=2)[
                            :, :, acol:acol + w]
                        s.activation(dst_ap, src_ap, AF.Exp).then_inc(sm.s_exp, 1)

            @blk.vector
            def _(v):
                for u in range(8):
                    for gi in range(SGROUPS):
                        gg = u * SGROUPS + gi
                        t = 0 if gi < 8 else gi - 7
                        if t == 0 or not MEMSETS[t]:
                            continue
                        v.wait_ge(sm.s_exp, gg + 1)
                        last = None
                        for (half, lo, hi) in MEMSETS[t]:
                            for h01 in (0, 1):
                                last = v.memset(
                                    arenaB[half * 64:half * 64 + 64,
                                           h01 * A + SEG_OFF[t] + lo:
                                           h01 * A + SEG_OFF[t] + hi], 0.0)
                        last.then_inc(sm.s_ms, 1)
                    for C in range(NCH):
                        v.wait_ge(sm.s_cpe, u * 16 + C * 2 + 1)
                        if 4 * u + C // 2 >= 1:
                            v.wait_ge((sm.s_cs00, sm.s_cs01)[C % 2], 16 * (4 * u + C // 2))
                        v.tensor_copy(cstage0[:, (C % 2) * 512:(C % 2 + 1) * 512],
                                      pp8[0:65, (C % 2) * 512:(C % 2 + 1) * 512]).then_inc(sm.s_cev0, 1)
                        v.wait_ge(sm.s_cpe, u * 16 + C * 2 + 2)
                        if 4 * u + C // 2 >= 1:
                            v.wait_ge((sm.s_cs10, sm.s_cs11)[C % 2], 16 * (4 * u + C // 2))
                        v.tensor_copy(cstage1[:, (C % 2) * 512:(C % 2 + 1) * 512],
                                      pp8[0:65, 1024 + (C % 2) * 512:1024 + (C % 2 + 1) * 512]).then_inc(sm.s_cev1, 1)

            @blk.gpsimd
            def _(g):
                for u in range(8):
                    for C in range(NCH):
                        g.wait_ge(sm.s_cev0, 8 * u + C + 1)
                        g.dma_start(ctxT_d[(2 * u) * 65:(2 * u) * 65 + 65,
                                           C * 512:(C + 1) * 512],
                                    cstage0[:, (C % 2) * 512:(C % 2 + 1) * 512]).then_inc(
                            (sm.s_cs00, sm.s_cs01)[C % 2], 16)
                        g.wait_ge(sm.s_cev1, 8 * u + C + 1)
                        g.dma_start(ctxT_d[(2 * u + 1) * 65:(2 * u + 1) * 65 + 65,
                                           C * 512:(C + 1) * 512],
                                    cstage1[:, (C % 2) * 512:(C % 2 + 1) * 512]).then_inc(
                            (sm.s_cs10, sm.s_cs11)[C % 2], 16)

    def _p3a():
        NI = NT // 2
        NC4 = 8
        ctxb = hold["ctxb"]
        with (
            nc.sbuf_tensor("p3_x", [128, 3 * DM], F32) as xb3,
            nc.sbuf_tensor("p3_attn", [128, 4 * DM], F32) as attnb,
            nc.sbuf_tensor("p3_x1", [128, 4 * DM], F32) as x1b,
            nc.sbuf_tensor("p3_xn2", [128, 2 * DM], BF16) as xn2b,
            nc.sbuf_tensor("p3_zg", [128, 2 * 32], F32) as zgb,
            nc.sbuf_tensor("p3_rz", [128, 2 * 32], F32) as rzb,
            nc.sbuf_tensor("bst3", [128, 2 * 24], F32) as bst3,
            nc.sbuf_tensor("agg3", [128, 2 * 4], F32) as agg3,
            nc.sbuf_tensor("sc3", [128, 2 * 8], F32) as sc3,
            nc.sbuf_tensor("bst3b", [128, 2 * 24], F32) as bst3b,
            nc.sbuf_tensor("agg3b", [128, 2 * 4], F32) as agg3b,
            nc.sbuf_tensor("sc3b", [128, 2 * 8], F32) as sc3b,
            nc.Block() as blk,
        ):
            s_ct = (sm.s_ctl0, sm.s_ctl1, sm.s_ctl2, sm.s_ctl3)
            s_xlv = (sm.s_xl0, sm.s_xl1, sm.s_xl2)
            sil_sched = {}
            vec_sched = {}
            for it in range(6, 10):
                sil_sched[it] = [(0, h) for h in range(6 * (it - 6), 6 * (it - 6) + 6)]
                vec_sched[it] = [("tt", 0, h) for h in range(6 * (it - 6), 6 * (it - 6) + 6)]
            vec_sched[11] = [("y", 0), ("y", 1)]
            for it in range(13, 16):
                sil_sched[it] = [(1, h) for h in range(8 * (it - 13), 8 * (it - 13) + 8)]
                vec_sched.setdefault(it, [])
                vec_sched[it] += [("tt", 1, h) for h in range(8 * (it - 13), 8 * (it - 13) + 8)]

            @blk.gpsimd
            def _(g):
                for pp in (sm.s_cs00, sm.s_cs01, sm.s_cs10, sm.s_cs11):
                    g.wait_ge(pp, 16 * 4 * 8)
                def wg_q(qq):
                    srcw = wgT_d[:, qq * (FF // 4):(qq + 1) * (FF // 4)]
                    dst = hold["wg_sb"][:].rearrange("p (dc c) -> p dc c", dc=8)[
                        :, :, qq * (FF // 4):(qq + 1) * (FF // 4)]
                    g.dma_start(dst,
                                srcw.rearrange("(dc p) c -> p dc c", p=128)).then_inc(s_wg_t[qq], 16)

                def wd_q(qr):
                    srcw = wdT_d[qr * 768:(qr + 1) * 768, :]
                    g.dma_start(
                        hold["wd_sb"][:, qr * 6 * DM:(qr + 1) * 6 * DM].rearrange(
                            "p (hc c) -> p hc c", hc=6),
                        srcw.rearrange("(hc p) c -> p hc c", p=128)).then_inc(s_wd_t[qr], 16)
                wu_loads(g, 0, 0, 3)
                wg_q(0)
                wu_loads(g, 0, 3, 9)
                wg_q(1)
                wu_loads(g, 0, 9, 15)
                wg_q(2)
                wu_loads(g, 0, 15, 21)
                wg_q(3)
                wu_loads(g, 0, 21, 24)
                for qr in range(4):
                    wd_q(qr)
                wu_loads(g, 1, 0, 24)

            @blk.sync
            def _(sy):
                for pp in (sm.s_cs00, sm.s_cs01, sm.s_cs10, sm.s_cs11):
                    sy.wait_ge(pp, 16 * 4 * 8)
                for t in range(NT):
                    if t >= 4:
                        sy.wait_ge(sm.s_v3a, t // 2 - 1)
                        sy.dma_start_transpose(
                            ctxb[:, (t % 4) * 1040:(t % 4 + 1) * 1040],
                            ctxT_d[:, t * 128:(t + 1) * 128]).then_inc(s_ct[t % 4], 16)
                    if t >= 3:
                        sy.wait_ge(sm.s_v4a, (t - 1) // 2)
                    sy.dma_start(xb3[:, (t % 3) * DM:(t % 3 + 1) * DM],
                                 x_in[t * 128:(t + 1) * 128, :]).then_inc(s_xlv[t % 3], 16)
                    if t == 7:
                        x2t_loads(sy, 0)
                    elif t == 21:
                        x2t_loads(sy, 1, dcs=[0, 1, 2, 3])
                    elif t == 27:
                        x2t_loads(sy, 1, dcs=[4, 5, 6, 7])
                    elif t == 29:
                        x2t_loads(sy, 2, dcs=[0, 1, 2, 3])
                x2t_loads(sy, 2, dcs=[4, 5, 6, 7])

            @blk.tensor
            def _(te):
                ffn_gateup(te, 0)
                ffn_down(te, 0, 0)
                ffn_down(te, 0, 1)
                ffn_gateup(te, 1)

            @blk.vector
            def _(v):
                for pp in (sm.s_cs00, sm.s_cs01, sm.s_cs10, sm.s_cs11):
                    v.wait_ge(pp, 16 * 4 * 8)
                hz = [0]

                def hop(instr):
                    instr.then_inc(sm.s_bn3, 1)
                    hz[0] += 1
                    v.wait_ge(sm.s_bn3, hz[0])

                def stage_A(it):
                    pb = it % 2
                    for k in (0, 1):
                        t = 2 * it + k
                        v.wait_ge(s_ct[t % 4], 16 * (t // 4 + 1))
                    if it >= 2:
                        v.wait_ge(sm.s_s3b, 2 * it - 2)
                    for k in (0, 1):
                        t = 2 * it + k
                        cb = ctxb[:, (t % 4) * 1040:(t % 4 + 1) * 1040]
                        ins = v.tensor_copy(zgb[:, pb * 32 + k * 16:pb * 32 + (k + 1) * 16],
                                            cb[:, 64:1040:65])
                    hop(ins)
                    hop(v.reciprocal(rzb[:, pb * 32:pb * 32 + 32],
                                     zgb[:, pb * 32:pb * 32 + 32]))
                    for k in (0, 1):
                        t = 2 * it + k
                        cb = ctxb[:, (t % 4) * 1040:(t % 4 + 1) * 1040]
                        cb3 = cb.rearrange("p (h e) -> p h e", e=65)[:, :, 0:64]
                        rz3 = rzb[:, pb * 32 + k * 16:pb * 32 + (k + 1) * 16].rearrange(
                            "p (h o) -> p h o", o=1).to_broadcast([128, H, 64])
                        at = attnb[:, (t % 4) * DM:(t % 4 + 1) * DM]
                        hop(v.tensor_tensor(at.rearrange("p (h e) -> p h e", e=64),
                                            cb3, rz3, ALU.mult))
                        ins = v.tensor_add(at, at, bv_rep[:])
                    hop(ins)
                    for k in (0, 1):
                        t = 2 * it + k
                        sl = (t % 4) * DM
                        v.bn_stats(bst3[:, pb * 24 + k * 12:pb * 24 + k * 12 + 6],
                                   attnb[:, sl:sl + 512])
                        ins = v.bn_stats(bst3[:, pb * 24 + k * 12 + 6:pb * 24 + k * 12 + 12],
                                         attnb[:, sl + 512:sl + 1024])
                    hop(ins)
                    v.bn_aggr(agg3[:, pb * 4:pb * 4 + 2], bst3[:, pb * 24:pb * 24 + 12])
                    hop(v.bn_aggr(agg3[:, pb * 4 + 2:pb * 4 + 4],
                                  bst3[:, pb * 24 + 12:pb * 24 + 24]))
                    v.tensor_scalar_add(sc3[:, pb * 8:pb * 8 + 2],
                                        agg3[:, pb * 4 + 1:pb * 4 + 4:2],
                                        EPS).then_inc(sm.s_v3a, 1)

                def stage_C(it):
                    qb = it % 2
                    v.wait_ge(sm.s_s3a, it + 1)
                    hop(v.reciprocal(sc3[:, qb * 8 + 4:qb * 8 + 6],
                                     sc3[:, qb * 8 + 2:qb * 8 + 4]))
                    v.scalar_tensor_tensor(sc3[:, qb * 8 + 6:qb * 8 + 8],
                                           agg3[:, qb * 4:qb * 4 + 3:2], -1.0,
                                           sc3[:, qb * 8 + 4:qb * 8 + 6],
                                           ALU.mult, ALU.mult).then_inc(sm.s_v3b, 1)

                def stage_E(it):
                    pb = it % 2
                    v.wait_ge(sm.s_s3b, 2 * (it + 1))
                    for k in (0, 1):
                        t = 2 * it + k
                        v.wait_ge(s_xlv[t % 3], 16 * (t // 3 + 1))
                    if it >= 2:
                        v.wait_ge((sm.s_sp3b0, sm.s_sp3b1)[pb], 64 * (it // 2))
                    for k in (0, 1):
                        t = 2 * it + k
                        sl = (t % 4) * DM
                        ins = v.tensor_add(x1b[:, sl:sl + DM],
                                           x1b[:, sl:sl + DM],
                                           xb3[:, (t % 3) * DM:(t % 3 + 1) * DM])
                    hop(ins)
                    for k in (0, 1):
                        t = 2 * it + k
                        sl = (t % 4) * DM
                        v.bn_stats(bst3b[:, pb * 24 + k * 12:pb * 24 + k * 12 + 6],
                                   x1b[:, sl:sl + 512])
                        ins = v.bn_stats(bst3b[:, pb * 24 + k * 12 + 6:pb * 24 + k * 12 + 12],
                                         x1b[:, sl + 512:sl + 1024])
                    hop(ins)
                    v.bn_aggr(agg3b[:, pb * 4:pb * 4 + 2], bst3b[:, pb * 24:pb * 24 + 12])
                    hop(v.bn_aggr(agg3b[:, pb * 4 + 2:pb * 4 + 4],
                                  bst3b[:, pb * 24 + 12:pb * 24 + 24]))
                    v.tensor_scalar_add(sc3b[:, pb * 8:pb * 8 + 2],
                                        agg3b[:, pb * 4 + 1:pb * 4 + 4:2],
                                        EPS).then_inc(sm.s_v4a, 1)

                def stage_G(it):
                    qb = it % 2
                    v.wait_ge(sm.s_s4a, it + 1)
                    hop(v.reciprocal(sc3b[:, qb * 8 + 4:qb * 8 + 6],
                                     sc3b[:, qb * 8 + 2:qb * 8 + 4]))
                    v.scalar_tensor_tensor(sc3b[:, qb * 8 + 6:qb * 8 + 8],
                                           agg3b[:, qb * 4:qb * 4 + 3:2], -1.0,
                                           sc3b[:, qb * 8 + 4:qb * 8 + 6],
                                           ALU.mult, ALU.mult).then_inc(sm.s_v4b, 1)

                for it in range(NI + 2):
                    if it < NI:
                        stage_A(it)
                    if 1 <= it < NI + 1:
                        stage_C(it - 1)
                        stage_E(it - 1)
                    if 2 <= it < NI + 2:
                        stage_G(it - 2)
                    for op in vec_sched.get(it, []):
                        if op[0] == "tt":
                            ffn_tt(v, op[1], op[2])
                        else:
                            ffn_ycopy_t(v, op[1])

            @blk.scalar
            def _(s):
                def stage_B(it):
                    pb = it % 2
                    s.wait_ge(sm.s_v3a, it + 1)
                    s.activation(sc3[:, pb * 8 + 2:pb * 8 + 4],
                                 sc3[:, pb * 8:pb * 8 + 2], AF.Sqrt).then_inc(sm.s_s3a, 1)

                def stage_D(it):
                    qb = it % 2
                    s.wait_ge(sm.s_v3b, it + 1)
                    if it >= 2:
                        s.wait_ge((sm.s_sp3b0, sm.s_sp3b1)[it % 2], 64 * (it // 2))
                    for k in (0, 1):
                        t = 2 * it + k
                        s.activation(x1b[:, (t % 4) * DM:(t % 4 + 1) * DM],
                                     attnb[:, (t % 4) * DM:(t % 4 + 1) * DM], AF.Identity,
                                     bias=sc3[:, qb * 8 + 6 + k:qb * 8 + 7 + k],
                                     scale=sc3[:, qb * 8 + 4 + k:qb * 8 + 5 + k]).then_inc(sm.s_s3b, 1)

                def stage_F(it):
                    pb = it % 2
                    s.wait_ge(sm.s_v4a, it + 1)
                    s.activation(sc3b[:, pb * 8 + 2:pb * 8 + 4],
                                 sc3b[:, pb * 8:pb * 8 + 2], AF.Sqrt).then_inc(sm.s_s4a, 1)

                def stage_H(it):
                    qb = it % 2
                    s.wait_ge(sm.s_v4b, it + 1)
                    if it >= 2:
                        s.wait_ge((sm.s_sp3b0, sm.s_sp3b1)[it % 2], 64 * (it // 2))
                    if it >= 1:
                        s.wait_ge((sm.s_sp3b0, sm.s_sp3b1)[(it - 1) % 2],
                                  64 * ((it + 1) // 2))
                    for k in (0, 1):
                        t = 2 * it + k
                        s.activation(xn2b[:, (t % 2) * DM:(t % 2 + 1) * DM],
                                     x1b[:, (t % 4) * DM:(t % 4 + 1) * DM], AF.Identity,
                                     bias=sc3b[:, qb * 8 + 6 + k:qb * 8 + 7 + k],
                                     scale=sc3b[:, qb * 8 + 4 + k:qb * 8 + 5 + k]).then_inc(sm.s_s4b, 1)
                        s.wait_ge(sm.s_s4b, t + 1)
                        s.dma_start(x1_d[t * 128:(t + 1) * 128, :],
                                    x1b[:, (t % 4) * DM:(t % 4 + 1) * DM]).then_inc(
                            (sm.s_sp3b0, sm.s_sp3b1)[(t // 2) % 2], 16)
                        s.dma_start(xn2_d[t * 128:(t + 1) * 128, :],
                                    xn2b[:, (t % 2) * DM:(t % 2 + 1) * DM]).then_inc(
                            (sm.s_sp3b0, sm.s_sp3b1)[(t // 2) % 2], 16)

                for it in range(NI + 2):
                    if it < NI:
                        stage_B(it)
                    if 1 <= it < NI + 1:
                        stage_D(it - 1)
                        stage_F(it - 1)
                    if 2 <= it < NI + 2:
                        stage_H(it - 2)
                    for (c, hc) in sil_sched.get(it, []):
                        ffn_silu(s, c, hc)

    def _p3b():
        pass

    def _p4():
        NC4 = 8
        with (
            nc.sbuf_tensor("x1f", [128, 2 * DM], F32) as x1f,
            nc.sbuf_tensor("ostage", [128, 2 * DM], F32) as ostage,
            nc.sbuf_tensor("obuf", [128, 2 * DM], F32) as obuf,
            nc.sbuf_tensor("bst4", [128, 2 * 12], F32) as bst4,
            nc.sbuf_tensor("agg4", [128, 2 * 2], F32) as agg4,
            nc.sbuf_tensor("sc4", [128, 2 * 5], F32) as sc4,
            nc.Block() as blk,
        ):
            y_sb = hold["y_sb"]

            @blk.tensor
            def _(te):
                ffn_down(te, 1, 0)
                ffn_down(te, 1, 1)
                for c in range(2, NC4):
                    ffn_gateup(te, c)
                    ffn_down(te, c, 0)
                    ffn_down(te, c, 1)

            @blk.gpsimd
            def _(g):
                def spill(gt):
                    g.wait_ge(sm.s_oady, gt + 1)
                    g.dma_start(out_d[gt * 128:(gt + 1) * 128, :],
                                obuf[:, (gt % 2) * DM:(gt % 2 + 1) * DM]).then_inc(
                        (sm.s_out0, sm.s_out1)[gt % 2], 16)
                for c in range(2, NC4):
                    for tt in range(4):
                        spill(4 * (c - 2) + tt)
                    wu_loads(g, c, 0, 24)
                for tt in range(8):
                    spill(4 * (NC4 - 2) + tt)
                g.wait_ge(sm.s_out0, 16 * NT // 2)
                g.wait_ge(sm.s_out1, 16 * NT // 2)

            @blk.sync
            def _(sy):
                sy.wait_ge(sm.s_sp3b0, 32 * NT // 2)
                sy.wait_ge(sm.s_sp3b1, 32 * NT // 2)

                def x1loads(c):
                    for tt in range(4):
                        gt = 4 * c + tt
                        if gt >= 2:
                            sy.wait_ge(sm.s_oady, gt - 1)
                        sy.dma_start(x1f[:, (gt % 2) * DM:(gt % 2 + 1) * DM],
                                     x1_d[gt * 128:(gt + 1) * 128, :]).then_inc(
                            (sm.s_x1f0, sm.s_x1f1)[gt % 2], 16)
                x1loads(0)
                x1loads(1)
                for c in range(3, NC4):
                    x2t_loads(sy, c)
                    x1loads(c - 1)
                x1loads(NC4 - 1)

            @blk.scalar
            def _(s):
                def ln4_t(gt):
                    p2 = gt % 2
                    s.wait_ge(sm.s_v5a, gt + 1)
                    s.activation(sc4[:, p2 * 5 + 1:p2 * 5 + 2],
                                 sc4[:, p2 * 5:p2 * 5 + 1], AF.Sqrt).then_inc(sm.s_s5a, 1)
                    s.wait_ge(sm.s_v5b, gt + 1)
                    s.activation(ostage[:, p2 * DM:(p2 + 1) * DM],
                                 y_sb[:, (gt % 2) * DM:(gt % 2 + 1) * DM], AF.Identity,
                                 bias=sc4[:, p2 * 5 + 4:p2 * 5 + 5],
                                 scale=sc4[:, p2 * 5 + 3:p2 * 5 + 4]).then_inc(sm.s_s5b, 1)
                ln4_t(0)
                ln4_t(1)
                for c in range(2, NC4):
                    for g in range(4 * c - 6, 4 * c - 2):
                        ln4_t(g)
                    for hc in range(24):
                        ffn_silu(s, c, hc)
                for g in range(26, 32):
                    ln4_t(g)

            @blk.vector
            def _(v):
                hz = [0]

                def hop(instr):
                    instr.then_inc(sm.s_bn4, 1)
                    hz[0] += 1
                    v.wait_ge(sm.s_bn4, hz[0])

                def tail_t(gt):
                    if True:
                        p2 = gt % 2
                        sl = (gt % 2) * DM
                        v.wait_ge(sm.s_yraw, gt + 1)
                        v.bn_stats(bst4[:, p2 * 12:p2 * 12 + 6], y_sb[:, sl:sl + 512])
                        hop(v.bn_stats(bst4[:, p2 * 12 + 6:p2 * 12 + 12],
                                       y_sb[:, sl + 512:sl + 1024]))
                        hop(v.bn_aggr(agg4[:, p2 * 2:p2 * 2 + 2],
                                      bst4[:, p2 * 12:p2 * 12 + 12]))
                        v.tensor_scalar(sc4[:, p2 * 5:p2 * 5 + 1],
                                        agg4[:, p2 * 2 + 1:p2 * 2 + 2],
                                        S2 * S2, EPS, ALU.mult, ALU.add).then_inc(sm.s_v5a, 1)
                        v.wait_ge(sm.s_s5a, gt + 1)
                        hop(v.reciprocal(sc4[:, p2 * 5 + 2:p2 * 5 + 3],
                                         sc4[:, p2 * 5 + 1:p2 * 5 + 2]))
                        v.tensor_scalar_mul(sc4[:, p2 * 5 + 3:p2 * 5 + 4],
                                            sc4[:, p2 * 5 + 2:p2 * 5 + 3], S2)
                        v.scalar_tensor_tensor(sc4[:, p2 * 5 + 4:p2 * 5 + 5],
                                               agg4[:, p2 * 2:p2 * 2 + 1], -S2,
                                               sc4[:, p2 * 5 + 2:p2 * 5 + 3],
                                               ALU.mult, ALU.mult).then_inc(sm.s_v5b, 1)
                        v.wait_ge(sm.s_s5b, gt + 1)
                        v.wait_ge((sm.s_x1f0, sm.s_x1f1)[p2], 16 * (gt // 2 + 1))
                        if gt >= 2:
                            v.wait_ge((sm.s_out0, sm.s_out1)[p2], 16 * (gt // 2))
                        v.tensor_add(obuf[:, p2 * DM:(p2 + 1) * DM],
                                     ostage[:, p2 * DM:(p2 + 1) * DM],
                                     x1f[:, (gt % 2) * DM:(gt % 2 + 1) * DM]).then_inc(sm.s_oady, 1)

                v.wait_ge(sm.s_sp3b0, 32 * NT // 2)
                v.wait_ge(sm.s_sp3b1, 32 * NT // 2)
                tail_t(0)
                tail_t(1)
                ffn_ycopy_t(v, 2)
                ffn_ycopy_t(v, 3)
                for c in range(2, NC4):
                    tail_t(4 * c - 6)
                    ffn_ycopy_t(v, 4 * c - 4)
                    tail_t(4 * c - 5)
                    ffn_ycopy_t(v, 4 * c - 3)
                    tail_t(4 * c - 4)
                    ffn_ycopy_t(v, 4 * c - 2)
                    tail_t(4 * c - 3)
                    ffn_ycopy_t(v, 4 * c - 1)
                    for hc in range(24):
                        ffn_tt(v, c, hc)
                tail_t(26)
                ffn_ycopy_t(v, 28)
                tail_t(27)
                ffn_ycopy_t(v, 29)
                tail_t(28)
                ffn_ycopy_t(v, 30)
                tail_t(29)
                ffn_ycopy_t(v, 31)
                tail_t(30)
                tail_t(31)

    fw = {1: [("s_sp10", 16 * NT // 2), ("s_sp11", 16 * NT // 2)],
          2: [("s_qsp0", 16 * 4 * NCH), ("s_qsp1", 16 * 4 * NCH),
              ("s_ksp0", 16 * 4 * NCH), ("s_ksp1", 16 * 4 * NCH)],
          3: [("s_cs00", 16 * 4 * 8), ("s_cs01", 16 * 4 * 8),
              ("s_cs10", 16 * 4 * 8), ("s_cs11", 16 * 4 * 8)],
          4: [("s_sp3b0", 32 * NT // 2), ("s_sp3b1", 32 * NT // 2), ("s_wgu", 16 * 6)],
          5: [("s_sp3b0", 32 * NT // 2), ("s_sp3b1", 32 * NT // 2), ("s_wgu", 16 * 6)]}

    _p1a()
    if phases >= 2:
        hold["v_sb"] = es_v.enter_context(
            nc.sbuf_tensor("v_sb", [128, 32 * 1040], BF16, side="right"))
        if phases >= 3:
            hold["qt_buf"] = es_v.enter_context(
                nc.sbuf_tensor("qt_buf", [128, 2 * S], BF16, side="right"))
            hold["kt_buf"] = es_v.enter_context(
                nc.sbuf_tensor("kt_buf", [128, 2 * S], BF16, side="right"))
            hold["qstage"] = es_v.enter_context(
                nc.sbuf_tensor("qstage", [128, 2 * 512], BF16, side="right"))
            hold["kstage"] = es_v.enter_context(
                nc.sbuf_tensor("kstage", [128, 2 * 512], BF16, side="right"))
        _p1b()
    es_qkv.close()
    if phases >= 3:
        hold["ctxb"] = es.enter_context(
            nc.sbuf_tensor("p3_ctx", [128, 4 * 1040], BF16))
        _p2()
    es_v.close()
    if phases >= 4:
        hold["wg_sb"] = es_ffn.enter_context(
            nc.sbuf_tensor("wg_sb", [128, 8 * FF], BF16, side="right"))
        hold["wd_sb"] = es_ffn.enter_context(
            nc.sbuf_tensor("wd_sb", [128, 24 * DM], BF16, side="right"))
        hold["x2T"] = es_ffn.enter_context(
            nc.sbuf_tensor("x2T", [128, 12 * 512], BF16))
        hold["wu_sl"] = es_ffn.enter_context(
            nc.sbuf_tensor("wu_sl", [128, 3 * 8 * 128], BF16))
        hold["hswb"] = es_ffn.enter_context(
            nc.sbuf_tensor("hswb", [128, 24 * 512], BF16))
        hold["y_sb"] = es_ffn.enter_context(
            nc.sbuf_tensor("y_sb", [128, 2 * DM], F32))
        hold["gp"] = es_ffn.enter_context(nc.psum_tensor("gp", [128, 2 * 512], F32))
        hold["upp"] = es_ffn.enter_context(nc.psum_tensor("upp", [128, 2 * 512], F32))
        hold["yp"] = es_ffn.enter_context(nc.psum_tensor("yp", [128, 2 * 1024], F32))
        _p3a()
    if phases >= 5:
        _p3b()
    if phases >= 6:
        _p4()
    if phases < 6:
        with nc.Block() as blk:
            @blk.gpsimd
            def _(g):
                for sname, tot in fw[phases]:
                    g.wait_ge(sems[sname], tot)
    es_ffn.close()
    es.close()
    return nc


def kernel(L0_attn_hidden_states, L0_attn_Wq, L0_attn_bq, L0_attn_Wk, L0_attn_bk,
           L0_attn_Wv, L0_attn_bv, L0_ffn_w_gate, L0_ffn_w_up, L0_ffn_w_down):
    x = np.ascontiguousarray(np.asarray(L0_attn_hidden_states))
    wq = np.ascontiguousarray(np.asarray(L0_attn_Wq).T)
    wk = np.ascontiguousarray(np.asarray(L0_attn_Wk).T)
    wv = np.ascontiguousarray(np.asarray(L0_attn_Wv).T)
    bq = np.ascontiguousarray(np.asarray(L0_attn_bq))
    bk = np.ascontiguousarray(np.asarray(L0_attn_bk))
    bv = np.ascontiguousarray(np.asarray(L0_attn_bv))
    wg = np.ascontiguousarray(np.asarray(L0_ffn_w_gate).T)
    wu = np.ascontiguousarray(np.asarray(L0_ffn_w_up).T)
    wd = np.ascontiguousarray(np.asarray(L0_ffn_w_down).T)

    nc = build_nc()
    in_maps = [{"x_in": x[b], "wqT": wq, "wkT": wk, "wvT": wv,
                "bq": bq, "bk": bk, "bv": bv,
                "wgT": wg, "wuT": wu, "wdT": wd} for b in range(8)]
    res = run_bass_kernel_spmd(nc, in_maps, core_ids=list(range(8)))
    return np.stack([res.results[b]["out"] for b in range(8)], axis=0)


# version 19 (final)
# speedup vs baseline: 1.0814x; 1.0814x over previous
import numpy as np
from contextlib import ExitStack
import concourse.bass as bass
import concourse.mybir as mybir
from concourse.bass_utils import run_bass_kernel_spmd

F32 = mybir.dt.float32
BF16 = mybir.dt.bfloat16
I32 = mybir.dt.int32
AF = mybir.ActivationFunctionType
ALU = mybir.AluOpType

S = 4096
DM = 1024
H = 16
NB = 64
FF = 3072
NT = S // 128
NCH = S // 512
QSCALE = 0.125
DQ = float(np.float32(1.0 / 127.0))
S2 = DQ * DQ
EPS = 1e-5


def attends(qb):
    if qb in (0, NB - 1):
        return list(range(NB))
    if qb == 1:
        return [0, 1, 2, NB - 1]
    if qb == NB - 2:
        return [0, NB - 3, NB - 2, NB - 1]
    return sorted({0, qb - 1, qb, qb + 1, NB - 1})


PAIRS = [(0, NB - 1)] + [(2 * t - 1, 2 * t) for t in range(1, NB // 2)]
NP_ = len(PAIRS)


def pair_runs(t):
    if t == 0:
        return [(0, NB)]
    lo, hi = max(1, 2 * t - 2), min(NB - 2, 2 * t + 1)
    return [(0, 1), (lo, hi - lo + 1), (NB - 1, 1)]


def _build_tables():
    seg_off, seg_w = [], []
    off = 0
    for t in range(NP_):
        seg_off.append(off)
        w = sum(n for _, n in pair_runs(t)) * 64
        seg_w.append(w)
        off += w
    arena_cols = off

    memsets = []
    for t in range(NP_):
        jt, jb = PAIRS[t]
        rects = []
        for half, j in ((0, jt), (1, jb)):
            c = 0
            bad = []
            for qs, n in pair_runs(t):
                for k in range(n):
                    if j not in attends(qs + k):
                        bad.append([c + k * 64, c + k * 64 + 64])
                c += n * 64
            merged = []
            for lo, hi in bad:
                if merged and merged[-1][1] == lo:
                    merged[-1][1] = hi
                else:
                    merged.append([lo, hi])
            rects += [(half, lo, hi) for lo, hi in merged]
        memsets.append(rects)

    ctx_lists = []
    total_w = 0
    for C in range(NCH):
        entries = []
        for t in range(NP_):
            c = 0
            for qs, n in pair_runs(t):
                lo, hi = max(qs, 8 * C), min(qs + n, 8 * C + 8)
                if lo < hi:
                    entries.append((t, seg_off[t] + c + (lo - qs) * 64,
                                    (lo - 8 * C) * 64, (hi - lo) * 64))
                c += n * 64
        ctx_lists.append(entries)
        total_w += sum(e[3] for e in entries)
    assert total_w == arena_cols, (total_w, arena_cols)
    return seg_off, seg_w, arena_cols, memsets, ctx_lists


SEG_OFF, SEG_W, ARENA_COLS, MEMSETS, CTX_LISTS = _build_tables()
SGROUPS = 8 + (NP_ - 1)
MS_INC_PER_U = sum(1 for t in range(1, NP_) if MEMSETS[t])


def _v_plan():
    plans = []
    for c in range(NCH):
        groups = []
        pA = (0, 64, 0 if c == 0 else 64, 0 if c == 0 else 4 * c)
        pB = (448, 64, 64 if c == 7 else 0, 0 if c == 7 else 4 * c + 4)
        if c in (0, 7):
            groups += [[pA], [pB]]
        else:
            groups += [[pA, pB]]
        for w in (1, 2, 3):
            groups.append([(w * 128 - 64, 128, 0, 4 * c + w)])
        plans.append(groups)
    return plans


V_PLAN = _v_plan()
V_COPIES = [[sum(2 if p[1] == 128 else 1 for p in g) for g in plan]
            for plan in V_PLAN]
V_COPY_PREFIX = []
_acc = 0
for _cnts in V_COPIES:
    for _cnt in _cnts:
        V_COPY_PREFIX.append(_acc)
        _acc += _cnt
V_COPY_TOTAL = _acc
VG_PER_CHUNK = [len(p) for p in V_PLAN]
VG_PREFIX = [0]
for _n in VG_PER_CHUNK:
    VG_PREFIX.append(VG_PREFIX[-1] + _n)


def build_nc(debug=(), phases=6):
    nc = bass.Bass("TRN2", target_bir_lowering=False)

    def scratch(name, shape, dtype):
        kind = "ExternalOutput" if name in debug else "Internal"
        return nc.dram_tensor(name, shape, dtype, kind=kind).ap()

    x_in = nc.dram_tensor("x_in", [S, DM], F32, kind="ExternalInput").ap()
    wqT = nc.dram_tensor("wqT", [DM, DM], F32, kind="ExternalInput").ap()
    wkT = nc.dram_tensor("wkT", [DM, DM], F32, kind="ExternalInput").ap()
    wvT = nc.dram_tensor("wvT", [DM, DM], F32, kind="ExternalInput").ap()
    bq_d = nc.dram_tensor("bq", [DM], F32, kind="ExternalInput").ap()
    bk_d = nc.dram_tensor("bk", [DM], F32, kind="ExternalInput").ap()
    bv_d = nc.dram_tensor("bv", [DM], F32, kind="ExternalInput").ap()
    wgT_d = nc.dram_tensor("wgT", [DM, FF], I32, kind="ExternalInput").ap()
    wuT_d = nc.dram_tensor("wuT", [DM, FF], I32, kind="ExternalInput").ap()
    wdT_d = nc.dram_tensor("wdT", [FF, DM], I32, kind="ExternalInput").ap()
    out_d = nc.dram_tensor("out", [S, DM], F32, kind="ExternalOutput").ap()

    xn1_d = scratch("xn1_d", [S, DM], BF16)
    qkt_d = scratch("qkt_d", [2, 8, 128, S], BF16)
    ctxT_d = scratch("ctxT_d", [1040, S], BF16)
    ln2_d = scratch("ln2_d", [S, DM], BF16)
    x1_d = scratch("x1_d", [S, DM], F32)
    xn2_d = scratch("xn2_d", [S, DM], BF16)

    es = ExitStack()
    es_qkv = ExitStack()
    es_v = ExitStack()
    es_ffn = ExitStack()
    sems = {}

    def sem(name):
        sems[name] = es.enter_context(nc.semaphore(name))

    for n in ("s_bb s_wq s_bvr s_xld0 s_xld1 s_xld2 s_xld3 s_bn1 s_va s_sq s_vb s_ln1 "
              "s_sp10 s_sp11 "
              "s_trl0 s_trl1 s_qg s_kg s_vg s_qev s_kev s_vev "
              "s_qsp0 s_qsp1 s_ksp0 s_ksp1 "
              "s_ql0 s_ql1 s_spe s_exp s_ms s_cpe s_cev0 s_cev1 "
              "s_cs00 s_cs01 s_cs10 s_cs11 "
              "s_ctl0 s_ctl1 s_ctl2 s_ctl3 s_ctl4 s_ctl5 s_ctl6 s_ctl7 s_bn3 s_v3a s_s3a s_v3b s_s3b s_sp30 s_sp31 "
              "s_xl0 s_xl1 s_xl2 s_xl3 s_xl4 s_xl5 s_xl6 s_xl7 s_bn3b s_v4a s_s4a s_v4b s_s4b s_sp3b0 s_sp3b1 "
              "s_wgu s_wua s_wub s_wuc s_wud s_x2t0 s_x2t1 s_x1f0 s_x1f1 s_gu s_sg s_hu s_dpe s_yraw "
              "s_bn4 s_v5a s_s5a s_v5b s_s5b s_oady s_out0 s_out1").split():
        sem(n)
    sm = type("SM", (), sems)

    bv_rep = es.enter_context(nc.sbuf_tensor("bv_rep", [128, DM], F32, side="right"))
    wq_sb = es_qkv.enter_context(nc.sbuf_tensor("wq_sb", [128, 8 * DM], BF16))
    wk_sb = es_qkv.enter_context(nc.sbuf_tensor("wk_sb", [128, 8 * DM], BF16))
    wv_sb = es_qkv.enter_context(nc.sbuf_tensor("wv_sb", [128, 8 * DM], BF16))
    bqsT = es_qkv.enter_context(nc.sbuf_tensor("bqsT", [128, 8], F32))
    bkT = es_qkv.enter_context(nc.sbuf_tensor("bkT", [128, 8], F32))
    hold = {}

    def _p1a():
        NI = NT // 2
        with (
            nc.sbuf_tensor("xbuf", [128, 8 * DM], F32) as xbuf,
            nc.sbuf_tensor("xn1buf", [128, 4 * DM], BF16) as xn1buf,
            nc.sbuf_tensor("bv_row", [1, DM], BF16) as bv_row,
            nc.sbuf_tensor("ones1", [1, 128], BF16) as ones1,
            nc.sbuf_tensor("bst1", [128, 2 * 24], F32) as bst1,
            nc.sbuf_tensor("agg1", [128, 2 * 4], F32) as agg1,
            nc.sbuf_tensor("sc1", [128, 2 * 8], F32) as sc1,
            nc.psum_tensor("bvp", [128, DM], F32) as bvp,
            nc.Block() as blk,
        ):
            s_xl = (sm.s_xld0, sm.s_xld1, sm.s_xld2, sm.s_xld3)

            @blk.gpsimd
            def _(g):
                g.dma_start(bv_row[:], bv_d[None, :]).then_inc(sm.s_bb, 16)
                for u in range(8):
                    g.dma_start(bqsT[:, u:u + 1],
                                bq_d[u * 128:(u + 1) * 128][:, None]).then_inc(sm.s_bb, 16)
                for u in range(8):
                    g.dma_start(bkT[:, u:u + 1],
                                bk_d[u * 128:(u + 1) * 128][:, None]).then_inc(sm.s_bb, 16)
                for w_d, w_sb in ((wqT, wq_sb), (wkT, wk_sb), (wvT, wv_sb)):
                    g.dma_start(w_sb[:].rearrange("p (dc c) -> p dc c", dc=8),
                                w_d.rearrange("(dc p) c -> p dc c", p=128)).then_inc(sm.s_wq, 16)

            @blk.sync
            def _(sy):
                for it in range(NI):
                    if it >= 4:
                        sy.wait_ge(sm.s_ln1, 2 * it - 6)
                    for k in (0, 1):
                        t = 2 * it + k
                        sy.dma_start(xbuf[:, (t % 8) * DM:(t % 8 + 1) * DM],
                                     x_in[t * 128:(t + 1) * 128, :]).then_inc(
                            s_xl[it % 4], 16)

            @blk.tensor
            def _(te):
                te.wait_ge(sm.s_bb, 272)
                te.wait_ge(sm.s_bvr, 1)
                te.matmul(bvp[:, 0:512], ones1[:], bv_row[:, 0:512], start=True, stop=True)
                te.matmul(bvp[:, 512:1024], ones1[:], bv_row[:, 512:1024],
                          start=True, stop=True).then_inc(sm.s_bvr, 1)

            @blk.vector
            def _(v):
                hz = [0]

                def hop(instr):
                    instr.then_inc(sm.s_bn1, 1)
                    hz[0] += 1
                    v.wait_ge(sm.s_bn1, hz[0])
                v.memset(ones1[:], 1.0).then_inc(sm.s_bvr, 1)
                v.wait_ge(sm.s_bvr, 2)
                v.tensor_copy(bv_rep[:], bvp[:])
                v.wait_ge(sm.s_bb, 272)
                v.tensor_scalar_mul(bqsT[:], bqsT[:], QSCALE)
                for it in range(NI + 1):
                    pb = it % 2
                    if it < NI:
                        v.wait_ge(s_xl[it % 4], 32 * (it // 4 + 1))
                        for k in (0, 1):
                            t = 2 * it + k
                            sl = (t % 8) * DM
                            v.bn_stats(bst1[:, pb * 24 + k * 12:pb * 24 + k * 12 + 6],
                                       xbuf[:, sl:sl + 512])
                            ins = v.bn_stats(bst1[:, pb * 24 + k * 12 + 6:pb * 24 + k * 12 + 12],
                                             xbuf[:, sl + 512:sl + 1024])
                        hop(ins)
                        v.bn_aggr(agg1[:, pb * 4:pb * 4 + 2], bst1[:, pb * 24:pb * 24 + 12])
                        hop(v.bn_aggr(agg1[:, pb * 4 + 2:pb * 4 + 4],
                                      bst1[:, pb * 24 + 12:pb * 24 + 24]))
                        v.tensor_scalar_add(
                            sc1[:, pb * 8:pb * 8 + 2],
                            agg1[:, pb * 4 + 1:pb * 4 + 4:2],
                            EPS).then_inc(sm.s_va, 1)
                    if it >= 1:
                        qb = (it - 1) % 2
                        v.wait_ge(sm.s_sq, it)
                        hop(v.reciprocal(sc1[:, qb * 8 + 4:qb * 8 + 6],
                                         sc1[:, qb * 8 + 2:qb * 8 + 4]))
                        v.scalar_tensor_tensor(
                            sc1[:, qb * 8 + 6:qb * 8 + 8],
                            agg1[:, qb * 4:qb * 4 + 3:2], -1.0,
                            sc1[:, qb * 8 + 4:qb * 8 + 6],
                            ALU.mult, ALU.mult).then_inc(sm.s_vb, 1)

            @blk.scalar
            def _(s):
                for it in range(NI + 1):
                    pb = it % 2
                    if it < NI:
                        s.wait_ge(sm.s_va, it + 1)
                        s.activation(sc1[:, pb * 8 + 2:pb * 8 + 4],
                                     sc1[:, pb * 8:pb * 8 + 2], AF.Sqrt).then_inc(sm.s_sq, 1)
                    if it >= 1:
                        qb = (it - 1) % 2
                        s.wait_ge(sm.s_vb, it)
                        if it - 1 >= 2:
                            s.wait_ge((sm.s_sp10, sm.s_sp11)[qb], 32 * ((it - 1) // 2))
                        for k in (0, 1):
                            t = 2 * (it - 1) + k
                            s.activation(xn1buf[:, (t % 4) * DM:(t % 4 + 1) * DM],
                                         xbuf[:, (t % 8) * DM:(t % 8 + 1) * DM], AF.Identity,
                                         bias=sc1[:, qb * 8 + 6 + k:qb * 8 + 7 + k],
                                         scale=sc1[:, qb * 8 + 4 + k:qb * 8 + 5 + k]).then_inc(sm.s_ln1, 1)
                            s.wait_ge(sm.s_ln1, t + 1)
                            s.dma_start(xn1_d[t * 128:(t + 1) * 128, :],
                                        xn1buf[:, (t % 4) * DM:(t % 4 + 1) * DM]).then_inc(
                                (sm.s_sp10, sm.s_sp11)[(t // 2) % 2], 16)

    def _p1b():
        v_sb = hold["v_sb"]
        qstage = hold.get("qstage")
        kstage = hold.get("kstage")
        if qstage is None:
            qstage = es_v.enter_context(
                nc.sbuf_tensor("qstage", [128, 2 * 512], BF16, side="right"))
            kstage = es_v.enter_context(
                nc.sbuf_tensor("kstage", [128, 2 * 512], BF16, side="right"))
        with (
            nc.sbuf_tensor("xn1T", [128, 2 * 8 * 512], BF16) as xn1T,
            nc.psum_tensor("qp", [128, 2 * 512], F32) as qp,
            nc.psum_tensor("kp", [128, 2 * 512], F32) as kp,
            nc.psum_tensor("vp", [128, 2 * 1024], F32) as vp,
            nc.Block() as blk,
        ):
            def xn1T_sl(par, dc, lo=0, w=512):
                base = par * 8 * 512 + dc * 512
                return xn1T[:, base + lo: base + lo + w]

            @blk.vector
            def _(v):
                v.memset(v_sb[:, 64:32 * 1040:65], 1.0).then_inc(sm.s_vev, 1)
                gv = 0
                for c in range(NCH):
                    for u in range(8):
                        gq = c * 8 + u
                        v.wait_ge(sm.s_qg, gq + 1)
                        if gq >= 2:
                            v.wait_ge((sm.s_qsp0, sm.s_qsp1)[gq % 2], 16 * (gq // 2))
                        par = gq % 2
                        v.tensor_scalar(qstage[:, par * 512:(par + 1) * 512],
                                        qp[:, par * 512:(par + 1) * 512],
                                        QSCALE, bqsT[:, u:u + 1],
                                        ALU.mult, ALU.add).then_inc(sm.s_qev, 1)
                    for grp in V_PLAN[c]:
                        v.wait_ge(sm.s_vg, gv + 1)
                        par = gv % 2
                        psl = vp[:, par * 1024:(par + 1) * 1024]
                        for (lhs_lo, lhs_w, plo, slot) in grp:
                            rlos = (0, 64) if lhs_w == 128 else (plo,)
                            for rlo in rlos:
                                dst = v_sb[rlo:rlo + 64,
                                           slot * 1040:slot * 1040 + 1040]
                                dst = dst.rearrange("p (h e) -> p h e", e=65)[:, :, 0:64]
                                v.tensor_copy(dst, psl[rlo:rlo + 64, :].rearrange(
                                    "p (h e) -> p h e", e=64)).then_inc(sm.s_vev, 1)
                        gv += 1

            @blk.sync
            def _(sy):
                for c in range(NCH):
                    sy.wait_ge(sm.s_sp10, 32 * (c + 1))
                    sy.wait_ge(sm.s_sp11, 32 * (c + 1))
                    if c >= 2:
                        sy.wait_ge(sm.s_vg, VG_PREFIX[c - 1])
                    for dc in range(8):
                        sy.dma_start_transpose(
                            xn1T_sl(c % 2, dc),
                            xn1_d[c * 512:(c + 1) * 512, dc * 128:(dc + 1) * 128]
                        ).then_inc((sm.s_trl0, sm.s_trl1)[c % 2], 16)
                if "qt_buf" in hold:
                    for u in (0, 1):
                        cnt = len([g for g in range(64)
                                   if g % 2 == u % 2 and g <= 56 + u])
                        sy.wait_ge((sm.s_qsp0, sm.s_qsp1)[u % 2], 16 * cnt)
                        sy.wait_ge((sm.s_ksp0, sm.s_ksp1)[u % 2], 16 * cnt)
                        sy.dma_start(hold["qt_buf"][:, (u % 2) * S:(u % 2 + 1) * S],
                                     qkt_d[0, u, :, :]).then_inc(
                            (sm.s_ql0, sm.s_ql1)[u % 2], 16)
                        sy.dma_start(hold["kt_buf"][:, (u % 2) * S:(u % 2 + 1) * S],
                                     qkt_d[1, u, :, :]).then_inc(
                            (sm.s_ql0, sm.s_ql1)[u % 2], 16)

            @blk.tensor
            def _(te):
                te.wait_ge(sm.s_wq, 48)
                gv = 0
                for c in range(NCH):
                    te.wait_ge((sm.s_trl0, sm.s_trl1)[c % 2], 16 * 8 * (c // 2 + 1))
                    par = c % 2
                    for u in range(8):
                        gq = c * 8 + u
                        if gq >= 2:
                            te.wait_ge(sm.s_qev, gq - 1)
                            te.wait_ge(sm.s_kev, gq - 1)
                        qsl = qp[:, (gq % 2) * 512:(gq % 2 + 1) * 512]
                        ksl = kp[:, (gq % 2) * 512:(gq % 2 + 1) * 512]
                        for dc in range(8):
                            mm = te.matmul(qsl,
                                           wq_sb[:, dc * DM + u * 128:dc * DM + (u + 1) * 128],
                                           xn1T_sl(par, dc), start=(dc == 0), stop=(dc == 7))
                        mm.then_inc(sm.s_qg, 1)
                        for dc in range(8):
                            mm = te.matmul(ksl,
                                           wk_sb[:, dc * DM + u * 128:dc * DM + (u + 1) * 128],
                                           xn1T_sl(par, dc), start=(dc == 0), stop=(dc == 7))
                        mm.then_inc(sm.s_kg, 1)
                    for grp in V_PLAN[c]:
                        if gv >= 2:
                            te.wait_ge(sm.s_vev, 1 + V_COPY_PREFIX[gv - 1])
                        vsl = vp[:, (gv % 2) * 1024:(gv % 2 + 1) * 1024]
                        for half in (0, 1):
                            for (lhs_lo, lhs_w, plo, slot) in grp:
                                for dc in range(8):
                                    mm = te.matmul(
                                        vsl[plo:plo + lhs_w, half * 512:(half + 1) * 512],
                                        xn1T_sl(par, dc, lhs_lo, lhs_w),
                                        wv_sb[:, dc * DM + half * 512:dc * DM + (half + 1) * 512],
                                        start=(dc == 0), stop=(dc == 7),
                                        tile_position=(0, plo))
                        mm.then_inc(sm.s_vg, 1)
                        gv += 1

            @blk.scalar
            def _(s):
                for gq in range(8 * NCH):
                    c, u = gq // 8, gq % 8
                    s.wait_ge(sm.s_kg, gq + 1)
                    if gq >= 2:
                        s.wait_ge((sm.s_ksp0, sm.s_ksp1)[gq % 2], 16 * (gq // 2))
                    par = gq % 2
                    s.activation(kstage[:, par * 512:(par + 1) * 512],
                                 kp[:, par * 512:(par + 1) * 512], AF.Identity,
                                 bias=bkT[:, u:u + 1], scale=1.0).then_inc(sm.s_kev, 1)

            @blk.gpsimd
            def _(g):
                for gq in range(8 * NCH):
                    c, u = gq // 8, gq % 8
                    par = gq % 2
                    g.wait_ge(sm.s_qev, gq + 1)
                    g.dma_start(qkt_d[0, u, :, c * 512:(c + 1) * 512],
                                qstage[:, par * 512:(par + 1) * 512]).then_inc(
                        (sm.s_qsp0, sm.s_qsp1)[par], 16)
                    g.wait_ge(sm.s_kev, gq + 1)
                    g.dma_start(qkt_d[1, u, :, c * 512:(c + 1) * 512],
                                kstage[:, par * 512:(par + 1) * 512]).then_inc(
                        (sm.s_ksp0, sm.s_ksp1)[par], 16)

    def _p2():
        v_sb = hold["v_sb"]
        A = ARENA_COLS
        ctxb = hold["ctxb"]
        qt_buf, kt_buf = hold["qt_buf"], hold["kt_buf"]
        s_ct = (sm.s_ctl0, sm.s_ctl1, sm.s_ctl2, sm.s_ctl3,
                sm.s_ctl4, sm.s_ctl5, sm.s_ctl6, sm.s_ctl7)
        with (
            nc.sbuf_tensor("arenaB", [128, 2 * A], BF16) as arenaB,
            nc.sbuf_tensor("cstage0", [65, 2 * 512], BF16) as cstage0,
            nc.sbuf_tensor("cstage1", [65, 2 * 512], BF16) as cstage1,
            nc.psum_tensor("pp8", [128, 8 * 512], F32) as pp8,
            nc.Block() as blk,
        ):
            def s_cols(t, gi):
                if t == 0:
                    return 512 * gi, 512
                return SEG_OFF[t], SEG_W[t]

            @blk.sync
            def _(sy):
                for u in range(2, 8):
                    sy.wait_ge(sm.s_qsp0, 16 * 4 * NCH)
                    sy.wait_ge(sm.s_qsp1, 16 * 4 * NCH)
                    sy.wait_ge(sm.s_ksp0, 16 * 4 * NCH)
                    sy.wait_ge(sm.s_ksp1, 16 * 4 * NCH)
                    sy.wait_ge(sm.s_spe, SGROUPS * (u - 1))
                    sy.dma_start(qt_buf[:, (u % 2) * S:(u % 2 + 1) * S],
                                 qkt_d[0, u, :, :]).then_inc((sm.s_ql0, sm.s_ql1)[u % 2], 16)
                    sy.dma_start(kt_buf[:, (u % 2) * S:(u % 2 + 1) * S],
                                 qkt_d[1, u, :, :]).then_inc((sm.s_ql0, sm.s_ql1)[u % 2], 16)
                for t in range(8):
                    Ct = t // 4
                    for h01, sems in ((0, (sm.s_cs00, sm.s_cs01)),
                                      (1, (sm.s_cs10, sm.s_cs11))):
                        for par in (0, 1):
                            ncnt = (Ct - par) // 2 + 1 if Ct >= par else 0
                            if 28 + ncnt > 0:
                                sy.wait_ge(sems[par], 16 * (28 + ncnt))
                    sy.dma_start_transpose(
                        ctxb[:, (t % 8) * 1040:(t % 8 + 1) * 1040],
                        ctxT_d[:, t * 128:(t + 1) * 128]).then_inc(s_ct[t % 8], 16)

            @blk.tensor
            def _(te):
                last_use = {}
                for u in range(8):
                    up = (u % 2) * S
                    te.wait_ge((sm.s_ql0, sm.s_ql1)[u % 2], 32 * (u // 2 + 1))
                    if u >= 1:
                        te.wait_ge(sm.s_cev0, 8 * u)
                        te.wait_ge(sm.s_cev1, 8 * u)
                    for gi in range(SGROUPS):
                        gg = u * SGROUPS + gi
                        par = gi % 4
                        if par in last_use:
                            te.wait_ge(sm.s_exp, last_use[par] + 1)
                        last_use[par] = gg
                        t = 0 if gi < 8 else gi - 7
                        jt, jb = PAIRS[t]
                        runs = [(8 * gi, 8)] if t == 0 else pair_runs(t)
                        col = 0
                        last = None
                        for (qs, n) in runs:
                            w = n * 64
                            for (rows, hoff) in ((0, 0), (64, 512)):
                                rhs = qt_buf[rows:rows + 64,
                                             up + qs * 64: up + qs * 64 + w]
                                for (tcol, j) in ((0, jt), (64, jb)):
                                    last = te.matmul(
                                        pp8[tcol:tcol + 64,
                                            par * 1024 + hoff + col: par * 1024 + hoff + col + w],
                                        kt_buf[rows:rows + 64,
                                               up + j * 64: up + j * 64 + 64],
                                        rhs, start=True, stop=True,
                                        tile_position=(rows, tcol))
                            col += w
                        last.then_inc(sm.s_spe, 1)
                    te.wait_ge(sm.s_exp, SGROUPS * (u + 1))
                    te.wait_ge(sm.s_ms, MS_INC_PER_U * (u + 1))
                    for C in range(NCH):
                        for h01 in (0, 1):
                            if C >= 2:
                                te.wait_ge(sm.s_cev0, 8 * u + C - 1)
                                te.wait_ge(sm.s_cev1, 8 * u + C - 1)
                            elif u >= 1:
                                te.wait_ge(sm.s_cev0, 8 * u)
                                te.wait_ge(sm.s_cev1, 8 * u)
                            pcoff = (h01 * 2 + (C % 2)) * 512
                            h = 2 * u + h01
                            n_e = len(CTX_LISTS[C])
                            for ei, (t, acol, pcol, w) in enumerate(CTX_LISTS[C]):
                                mm = te.matmul(
                                    pp8[0:65, pcoff + pcol:pcoff + pcol + w],
                                    v_sb[:, t * 1040 + h * 65: t * 1040 + h * 65 + 65],
                                    arenaB[:, h01 * A + acol:h01 * A + acol + w],
                                    start=(ei == 0), stop=(ei == n_e - 1))
                            mm.then_inc(sm.s_cpe, 1)

            @blk.scalar
            def _(s):
                for u in range(8):
                    for gi in range(SGROUPS):
                        gg = u * SGROUPS + gi
                        par = gi % 4
                        t = 0 if gi < 8 else gi - 7
                        acol, w = s_cols(t, gi)
                        s.wait_ge(sm.s_spe, gg + 1)
                        src_ap = pp8[:, par * 1024:(par + 1) * 1024].rearrange(
                            "p (b c) -> p b c", b=2)[:, :, 0:w]
                        dst_ap = arenaB[:].rearrange("p (b c) -> p b c", b=2)[
                            :, :, acol:acol + w]
                        s.activation(dst_ap, src_ap, AF.Exp).then_inc(sm.s_exp, 1)

            @blk.vector
            def _(v):
                for u in range(8):
                    for gi in range(SGROUPS):
                        gg = u * SGROUPS + gi
                        t = 0 if gi < 8 else gi - 7
                        if t == 0 or not MEMSETS[t]:
                            continue
                        v.wait_ge(sm.s_exp, gg + 1)
                        last = None
                        for (half, lo, hi) in MEMSETS[t]:
                            for h01 in (0, 1):
                                last = v.memset(
                                    arenaB[half * 64:half * 64 + 64,
                                           h01 * A + SEG_OFF[t] + lo:
                                           h01 * A + SEG_OFF[t] + hi], 0.0)
                        last.then_inc(sm.s_ms, 1)
                    for C in range(NCH):
                        v.wait_ge(sm.s_cpe, u * 16 + C * 2 + 1)
                        if 4 * u + C // 2 >= 1:
                            v.wait_ge((sm.s_cs00, sm.s_cs01)[C % 2], 16 * (4 * u + C // 2))
                        v.tensor_copy(cstage0[:, (C % 2) * 512:(C % 2 + 1) * 512],
                                      pp8[0:65, (C % 2) * 512:(C % 2 + 1) * 512]).then_inc(sm.s_cev0, 1)
                        v.wait_ge(sm.s_cpe, u * 16 + C * 2 + 2)
                        if 4 * u + C // 2 >= 1:
                            v.wait_ge((sm.s_cs10, sm.s_cs11)[C % 2], 16 * (4 * u + C // 2))
                        v.tensor_copy(cstage1[:, (C % 2) * 512:(C % 2 + 1) * 512],
                                      pp8[0:65, 1024 + (C % 2) * 512:1024 + (C % 2 + 1) * 512]).then_inc(sm.s_cev1, 1)

            @blk.gpsimd
            def _(g):
                for u in range(8):
                    for C in range(NCH):
                        g.wait_ge(sm.s_cev0, 8 * u + C + 1)
                        g.dma_start(ctxT_d[(2 * u) * 65:(2 * u) * 65 + 65,
                                           C * 512:(C + 1) * 512],
                                    cstage0[:, (C % 2) * 512:(C % 2 + 1) * 512]).then_inc(
                            (sm.s_cs00, sm.s_cs01)[C % 2], 16)
                        g.wait_ge(sm.s_cev1, 8 * u + C + 1)
                        g.dma_start(ctxT_d[(2 * u + 1) * 65:(2 * u + 1) * 65 + 65,
                                           C * 512:(C + 1) * 512],
                                    cstage1[:, (C % 2) * 512:(C % 2 + 1) * 512]).then_inc(
                            (sm.s_cs10, sm.s_cs11)[C % 2], 16)

    def _p3a():
        wg_sb, wd_sb = hold["wg_sb"], hold["wd_sb"]
        NI = NT // 2
        ctxb = hold["ctxb"]
        with (
            nc.sbuf_tensor("p3_x", [128, 8 * DM], F32) as xb3,
            nc.sbuf_tensor("p3_attn", [128, 4 * DM], F32) as attnb,
            nc.sbuf_tensor("p3_ln2", [128, 4 * DM], BF16) as ln2b,
            nc.sbuf_tensor("p3_x1", [128, 4 * DM], F32) as x1b,
            nc.sbuf_tensor("p3_xn2", [128, 4 * DM], BF16) as xn2b,
            nc.sbuf_tensor("p3_zg", [128, 2 * 32], F32) as zgb,
            nc.sbuf_tensor("p3_rz", [128, 2 * 32], F32) as rzb,
            nc.sbuf_tensor("bst3", [128, 2 * 24], F32) as bst3,
            nc.sbuf_tensor("agg3", [128, 2 * 4], F32) as agg3,
            nc.sbuf_tensor("sc3", [128, 2 * 8], F32) as sc3,
            nc.sbuf_tensor("bst3b", [128, 2 * 24], F32) as bst3b,
            nc.sbuf_tensor("agg3b", [128, 2 * 4], F32) as agg3b,
            nc.sbuf_tensor("sc3b", [128, 2 * 8], F32) as sc3b,
            nc.Block() as blk,
        ):
            s_ct = (sm.s_ctl0, sm.s_ctl1, sm.s_ctl2, sm.s_ctl3,
                    sm.s_ctl4, sm.s_ctl5, sm.s_ctl6, sm.s_ctl7)
            s_xlv = (sm.s_xl0, sm.s_xl1, sm.s_xl2, sm.s_xl3,
                     sm.s_xl4, sm.s_xl5, sm.s_xl6, sm.s_xl7)

            @blk.gpsimd
            def _(g):
                for pp in (sm.s_cs00, sm.s_cs01, sm.s_cs10, sm.s_cs11):
                    g.wait_ge(pp, 16 * 4 * 8)
                for (w_d, w_sb) in ((wgT_d, wg_sb),):
                    for half in (0, 1):
                        srcw = w_d[:, half * (FF // 2):(half + 1) * (FF // 2)]
                        dst = w_sb[:].rearrange("p (dc c) -> p dc c", dc=8)[
                            :, :, half * (FF // 2):(half + 1) * (FF // 2)]
                        g.dma_start(dst,
                                    srcw.rearrange("(dc p) c -> p dc c", p=128)).then_inc(sm.s_wgu, 16)
                for qr in range(4):
                    srcw = wdT_d[qr * 768:(qr + 1) * 768, :]
                    g.dma_start(
                        wd_sb[:, qr * 6 * DM:(qr + 1) * 6 * DM].rearrange(
                            "p (hc c) -> p hc c", hc=6),
                        srcw.rearrange("(hc p) c -> p hc c", p=128)).then_inc(sm.s_wgu, 16)

            @blk.sync
            def _(sy):
                for pp in (sm.s_cs00, sm.s_cs01, sm.s_cs10, sm.s_cs11):
                    sy.wait_ge(pp, 16 * 4 * 8)
                for t in range(NT):
                    if t >= 8:
                        sy.wait_ge(sm.s_v3a, t // 2 - 3)
                        sy.wait_ge(sm.s_v4a, t // 2 - 3)
                        sy.dma_start_transpose(
                            ctxb[:, (t % 8) * 1040:(t % 8 + 1) * 1040],
                            ctxT_d[:, t * 128:(t + 1) * 128]).then_inc(s_ct[t % 8], 16)
                    sy.dma_start(xb3[:, (t % 8) * DM:(t % 8 + 1) * DM],
                                 x_in[t * 128:(t + 1) * 128, :]).then_inc(s_xlv[t % 8], 16)
                sy.wait_ge(sm.s_sp3b0, 64)
                sy.wait_ge(sm.s_sp3b1, 64)
                for dc in range(8):
                    sy.dma_start_transpose(
                        hold["x2T0"][:, dc * 512:(dc + 1) * 512],
                        xn2_d[0:512, dc * 128:(dc + 1) * 128]).then_inc(sm.s_x2t0, 16)

            @blk.vector
            def _(v):
                hz = [0]

                def hop(instr):
                    instr.then_inc(sm.s_bn3, 1)
                    hz[0] += 1
                    v.wait_ge(sm.s_bn3, hz[0])

                def stage_A(it):
                    pb = it % 2
                    v.wait_ge(s_ct[(2 * it) % 8], 16 * (it // 4 + 1))
                    v.wait_ge(s_ct[(2 * it + 1) % 8], 16 * (it // 4 + 1))
                    if it >= 2:
                        v.wait_ge(sm.s_s3b, 2 * it - 2)
                    for k in (0, 1):
                        t = 2 * it + k
                        cb = ctxb[:, (t % 8) * 1040:(t % 8 + 1) * 1040]
                        ins = v.tensor_copy(zgb[:, pb * 32 + k * 16:pb * 32 + (k + 1) * 16],
                                            cb[:, 64:1040:65])
                    hop(ins)
                    hop(v.reciprocal(rzb[:, pb * 32:pb * 32 + 32],
                                     zgb[:, pb * 32:pb * 32 + 32]))
                    for k in (0, 1):
                        t = 2 * it + k
                        cb = ctxb[:, (t % 8) * 1040:(t % 8 + 1) * 1040]
                        cb3 = cb.rearrange("p (h e) -> p h e", e=65)[:, :, 0:64]
                        rz3 = rzb[:, pb * 32 + k * 16:pb * 32 + (k + 1) * 16].rearrange(
                            "p (h o) -> p h o", o=1).to_broadcast([128, H, 64])
                        at = attnb[:, (t % 4) * DM:(t % 4 + 1) * DM]
                        hop(v.tensor_tensor(at.rearrange("p (h e) -> p h e", e=64),
                                            cb3, rz3, ALU.mult))
                        ins = v.tensor_add(at, at, bv_rep[:])
                    hop(ins)
                    for k in (0, 1):
                        t = 2 * it + k
                        sl = (t % 4) * DM
                        v.bn_stats(bst3[:, pb * 24 + k * 12:pb * 24 + k * 12 + 6],
                                   attnb[:, sl:sl + 512])
                        ins = v.bn_stats(bst3[:, pb * 24 + k * 12 + 6:pb * 24 + k * 12 + 12],
                                         attnb[:, sl + 512:sl + 1024])
                    hop(ins)
                    v.bn_aggr(agg3[:, pb * 4:pb * 4 + 2], bst3[:, pb * 24:pb * 24 + 12])
                    hop(v.bn_aggr(agg3[:, pb * 4 + 2:pb * 4 + 4],
                                  bst3[:, pb * 24 + 12:pb * 24 + 24]))
                    v.tensor_scalar_add(sc3[:, pb * 8:pb * 8 + 2],
                                        agg3[:, pb * 4 + 1:pb * 4 + 4:2],
                                        EPS).then_inc(sm.s_v3a, 1)

                def stage_C(it):
                    qb = it % 2
                    v.wait_ge(sm.s_s3a, it + 1)
                    hop(v.reciprocal(sc3[:, qb * 8 + 4:qb * 8 + 6],
                                     sc3[:, qb * 8 + 2:qb * 8 + 4]))
                    v.scalar_tensor_tensor(sc3[:, qb * 8 + 6:qb * 8 + 8],
                                           agg3[:, qb * 4:qb * 4 + 3:2], -1.0,
                                           sc3[:, qb * 8 + 4:qb * 8 + 6],
                                           ALU.mult, ALU.mult).then_inc(sm.s_v3b, 1)

                def stage_E(it):
                    pb = it % 2
                    v.wait_ge(sm.s_s3b, 2 * (it + 1))
                    v.wait_ge(s_xlv[(2 * it) % 8], 16 * (it // 4 + 1))
                    v.wait_ge(s_xlv[(2 * it + 1) % 8], 16 * (it // 4 + 1))
                    if it >= 2:
                        v.wait_ge((sm.s_sp3b0, sm.s_sp3b1)[pb], 64 * (it // 2))
                    for k in (0, 1):
                        t = 2 * it + k
                        sl = (t % 4) * DM
                        sl8 = (t % 8) * DM
                        ins = v.tensor_add(x1b[:, sl:sl + DM],
                                           ln2b[:, sl:sl + DM], xb3[:, sl8:sl8 + DM])
                    hop(ins)
                    for k in (0, 1):
                        t = 2 * it + k
                        sl = (t % 4) * DM
                        v.bn_stats(bst3b[:, pb * 24 + k * 12:pb * 24 + k * 12 + 6],
                                   x1b[:, sl:sl + 512])
                        ins = v.bn_stats(bst3b[:, pb * 24 + k * 12 + 6:pb * 24 + k * 12 + 12],
                                         x1b[:, sl + 512:sl + 1024])
                    hop(ins)
                    v.bn_aggr(agg3b[:, pb * 4:pb * 4 + 2], bst3b[:, pb * 24:pb * 24 + 12])
                    hop(v.bn_aggr(agg3b[:, pb * 4 + 2:pb * 4 + 4],
                                  bst3b[:, pb * 24 + 12:pb * 24 + 24]))
                    v.tensor_scalar_add(sc3b[:, pb * 8:pb * 8 + 2],
                                        agg3b[:, pb * 4 + 1:pb * 4 + 4:2],
                                        EPS).then_inc(sm.s_v4a, 1)

                def stage_G(it):
                    qb = it % 2
                    v.wait_ge(sm.s_s4a, it + 1)
                    hop(v.reciprocal(sc3b[:, qb * 8 + 4:qb * 8 + 6],
                                     sc3b[:, qb * 8 + 2:qb * 8 + 4]))
                    v.scalar_tensor_tensor(sc3b[:, qb * 8 + 6:qb * 8 + 8],
                                           agg3b[:, qb * 4:qb * 4 + 3:2], -1.0,
                                           sc3b[:, qb * 8 + 4:qb * 8 + 6],
                                           ALU.mult, ALU.mult).then_inc(sm.s_v4b, 1)

                for it in range(NI + 2):
                    if it < NI:
                        stage_A(it)
                    if 1 <= it < NI + 1:
                        stage_C(it - 1)
                        stage_E(it - 1)
                    if 2 <= it < NI + 2:
                        stage_G(it - 2)

            @blk.scalar
            def _(s):
                def stage_B(it):
                    pb = it % 2
                    s.wait_ge(sm.s_v3a, it + 1)
                    s.activation(sc3[:, pb * 8 + 2:pb * 8 + 4],
                                 sc3[:, pb * 8:pb * 8 + 2], AF.Sqrt).then_inc(sm.s_s3a, 1)

                def stage_D(it):
                    qb = it % 2
                    s.wait_ge(sm.s_v3b, it + 1)
                    if it >= 2:
                        s.wait_ge(sm.s_v4a, it - 1)
                    for k in (0, 1):
                        t = 2 * it + k
                        s.activation(ln2b[:, (t % 4) * DM:(t % 4 + 1) * DM],
                                     attnb[:, (t % 4) * DM:(t % 4 + 1) * DM], AF.Identity,
                                     bias=sc3[:, qb * 8 + 6 + k:qb * 8 + 7 + k],
                                     scale=sc3[:, qb * 8 + 4 + k:qb * 8 + 5 + k]).then_inc(sm.s_s3b, 1)

                def stage_F(it):
                    pb = it % 2
                    s.wait_ge(sm.s_v4a, it + 1)
                    s.activation(sc3b[:, pb * 8 + 2:pb * 8 + 4],
                                 sc3b[:, pb * 8:pb * 8 + 2], AF.Sqrt).then_inc(sm.s_s4a, 1)

                def stage_H(it):
                    qb = it % 2
                    s.wait_ge(sm.s_v4b, it + 1)
                    if it >= 2:
                        s.wait_ge((sm.s_sp3b0, sm.s_sp3b1)[it % 2], 64 * (it // 2))
                    for k in (0, 1):
                        t = 2 * it + k
                        s.activation(xn2b[:, (t % 4) * DM:(t % 4 + 1) * DM],
                                     x1b[:, (t % 4) * DM:(t % 4 + 1) * DM], AF.Identity,
                                     bias=sc3b[:, qb * 8 + 6 + k:qb * 8 + 7 + k],
                                     scale=sc3b[:, qb * 8 + 4 + k:qb * 8 + 5 + k]).then_inc(sm.s_s4b, 1)
                        s.wait_ge(sm.s_s4b, t + 1)
                        s.dma_start(x1_d[t * 128:(t + 1) * 128, :],
                                    x1b[:, (t % 4) * DM:(t % 4 + 1) * DM]).then_inc(
                            (sm.s_sp3b0, sm.s_sp3b1)[(t // 2) % 2], 16)
                        s.dma_start(xn2_d[t * 128:(t + 1) * 128, :],
                                    xn2b[:, (t % 4) * DM:(t % 4 + 1) * DM]).then_inc(
                            (sm.s_sp3b0, sm.s_sp3b1)[(t // 2) % 2], 16)

                for it in range(NI + 2):
                    if it < NI:
                        stage_B(it)
                    if 1 <= it < NI + 1:
                        stage_D(it - 1)
                        stage_F(it - 1)
                    if 2 <= it < NI + 2:
                        stage_H(it - 2)

    def _p3b():
        pass

    def _p4():
        wg_sb, wd_sb = hold["wg_sb"], hold["wd_sb"]
        NC4 = 8
        with (
            nc.sbuf_tensor("x2T", [128, 2 * 8 * 512], BF16) as x2T,
            nc.sbuf_tensor("wu_sl", [128, 4 * 8 * 128], BF16) as wu_sl,
            nc.sbuf_tensor("sgbuf", [128, 2 * 512], BF16) as sgbuf,
            nc.sbuf_tensor("hswb", [128, 24 * 512], BF16) as hswb,
            nc.sbuf_tensor("y_sb", [128, 4 * DM], F32) as y_sb,
            nc.sbuf_tensor("x1f", [128, 2 * DM], F32) as x1f,
            nc.sbuf_tensor("ostage", [128, 2 * DM], F32) as ostage,
            nc.sbuf_tensor("bst4", [128, 2 * 12], F32) as bst4,
            nc.sbuf_tensor("agg4", [128, 2 * 2], F32) as agg4,
            nc.sbuf_tensor("sc4", [128, 2 * 5], F32) as sc4,
            nc.psum_tensor("gp", [128, 2 * 512], F32) as gp,
            nc.psum_tensor("upp", [128, 2 * 512], F32) as upp,
            nc.psum_tensor("yp", [128, 2 * 1024], F32) as yp,
            nc.Block() as blk,
        ):
            s_wu = (sm.s_wua, sm.s_wub, sm.s_wuc, sm.s_wud)
            y_tgts = ((yp, 0), (yp, 1024), (gp, 0), (upp, 0))

            @blk.gpsimd
            def _(g):
                def spill(gt):
                    g.wait_ge(sm.s_oady, gt + 1)
                    g.dma_start(out_d[gt * 128:(gt + 1) * 128, :],
                                ostage[:, (gt % 2) * DM:(gt % 2 + 1) * DM]).then_inc(
                        (sm.s_out0, sm.s_out1)[gt % 2], 16)
                g.wait_ge(sm.s_sp3b0, 32 * NT // 2)
                g.wait_ge(sm.s_sp3b1, 32 * NT // 2)
                for c in range(NC4):
                    for hc in range(24):
                        if c > 0 or hc >= 4:
                            g.wait_ge(sm.s_gu, 24 * c + hc - 3)
                        sl = hc % 4
                        g.dma_start(
                            wu_sl[:, sl * 1024:(sl + 1) * 1024].rearrange(
                                "p (dc e) -> p dc e", dc=8),
                            wuT_d[:, hc * 128:(hc + 1) * 128].rearrange(
                                "(dc p) e -> p dc e", p=128)).then_inc(s_wu[sl], 16)
                    if c >= 1:
                        for tt in range(4):
                            spill(4 * (c - 1) + tt)
                for tt in range(4):
                    spill(4 * (NC4 - 1) + tt)
                g.wait_ge(sm.s_out0, 16 * NT // 2)
                g.wait_ge(sm.s_out1, 16 * NT // 2)

            @blk.sync
            def _(sy):
                sy.wait_ge(sm.s_sp3b0, 32 * NT // 2)
                sy.wait_ge(sm.s_sp3b1, 32 * NT // 2)
                def x1loads(c):
                    for tt in range(4):
                        gt = 4 * c + tt
                        if gt >= 2:
                            sy.wait_ge(sm.s_oady, gt - 1)
                        sy.dma_start(x1f[:, (gt % 2) * DM:(gt % 2 + 1) * DM],
                                     x1_d[gt * 128:(gt + 1) * 128, :]).then_inc(
                            (sm.s_x1f0, sm.s_x1f1)[gt % 2], 16)
                for c in range(NC4):
                    if c >= 1:
                        if c >= 2:
                            sy.wait_ge(sm.s_gu, 24 * (c - 1))
                        for dc in range(8):
                            sy.dma_start_transpose(
                                x2T[:, (c % 2) * 8 * 512 + dc * 512:
                                    (c % 2) * 8 * 512 + (dc + 1) * 512],
                                xn2_d[c * 512:(c + 1) * 512,
                                      dc * 128:(dc + 1) * 128]).then_inc(
                                (sm.s_x2t0, sm.s_x2t1)[c % 2], 16)
                        x1loads(c - 1)
                x1loads(NC4 - 1)

            @blk.tensor
            def _(te):
                te.wait_ge(sm.s_wgu, 16 * 6)
                for c in range(NC4):
                    te.wait_ge((sm.s_x2t0, sm.s_x2t1)[c % 2], 16 * 8 * (c // 2 + 1))
                    te.wait_ge(sm.s_yraw, 4 * c)
                    par = c % 2
                    for hc in range(24):
                        ghc = 24 * c + hc
                        if ghc >= 2:
                            te.wait_ge(sm.s_hu, ghc - 1)
                        te.wait_ge(s_wu[hc % 4], 16 * (6 * c + hc // 4 + 1))
                        gsl = gp[:, (ghc % 2) * 512:(ghc % 2 + 1) * 512]
                        usl = upp[:, (ghc % 2) * 512:(ghc % 2 + 1) * 512]

                        def xsl(dc):
                            if c == 0:
                                return hold["x2T0"][:, dc * 512:(dc + 1) * 512]
                            return x2T[:, par * 8 * 512 + dc * 512:
                                       par * 8 * 512 + (dc + 1) * 512]
                        for dc in range(8):
                            mm = te.matmul(gsl,
                                           wg_sb[:, dc * FF + hc * 128:dc * FF + (hc + 1) * 128],
                                           xsl(dc), start=(dc == 0), stop=(dc == 7))
                        mm.then_inc(sm.s_sp30, 1)
                        for dc in range(8):
                            mm = te.matmul(usl,
                                           wu_sl[:, (hc % 4) * 1024 + dc * 128:
                                                 (hc % 4) * 1024 + (dc + 1) * 128],
                                           xsl(dc), start=(dc == 0), stop=(dc == 7))
                        mm.then_inc(sm.s_gu, 1)
                    te.wait_ge(sm.s_sg, 24 * (c + 1))
                    te.wait_ge(sm.s_hu, 24 * (c + 1))
                    for hc in range(24):
                        for tt in range(4):
                            tgt, off = y_tgts[tt]
                            for half in (0, 1):
                                mm = te.matmul(
                                    tgt[:, off + half * 512:off + (half + 1) * 512],
                                    hswb[:, hc * 512 + tt * 128:hc * 512 + (tt + 1) * 128],
                                    wd_sb[:, hc * DM + half * 512:hc * DM + (half + 1) * 512],
                                    start=(hc == 0), stop=(hc == 23))
                    mm.then_inc(sm.s_dpe, 1)

            @blk.scalar
            def _(s):
                def ln4_acts(c):
                    for tt in range(4):
                        gt = 4 * c + tt
                        p2 = gt % 2
                        s.wait_ge(sm.s_v5a, gt + 1)
                        s.activation(sc4[:, p2 * 5 + 1:p2 * 5 + 2],
                                     sc4[:, p2 * 5:p2 * 5 + 1], AF.Sqrt).then_inc(sm.s_s5a, 1)
                        s.wait_ge(sm.s_v5b, gt + 1)
                        if gt >= 2:
                            s.wait_ge((sm.s_out0, sm.s_out1)[p2], 16 * (gt // 2))
                        s.activation(ostage[:, p2 * DM:(p2 + 1) * DM],
                                     y_sb[:, (gt % 4) * DM:(gt % 4 + 1) * DM], AF.Identity,
                                     bias=sc4[:, p2 * 5 + 4:p2 * 5 + 5],
                                     scale=sc4[:, p2 * 5 + 3:p2 * 5 + 4]).then_inc(sm.s_s5b, 1)
                for c in range(NC4):
                    for hc in range(24):
                        ghc = 24 * c + hc
                        s.wait_ge(sm.s_sp30, ghc + 1)
                        if ghc >= 2:
                            s.wait_ge(sm.s_hu, ghc - 1)
                        s.activation(sgbuf[:, (ghc % 2) * 512:(ghc % 2 + 1) * 512],
                                     gp[:, (ghc % 2) * 512:(ghc % 2 + 1) * 512],
                                     AF.Silu, scale=DQ).then_inc(sm.s_sg, 1)
                    if c >= 1:
                        ln4_acts(c - 1)
                ln4_acts(NC4 - 1)

            @blk.vector
            def _(v):
                hz = [0]

                def hop(instr):
                    instr.then_inc(sm.s_bn4, 1)
                    hz[0] += 1
                    v.wait_ge(sm.s_bn4, hz[0])

                def tt_block(c):
                    for hc in range(24):
                        ghc = 24 * c + hc
                        v.wait_ge(sm.s_sg, ghc + 1)
                        v.wait_ge(sm.s_gu, ghc + 1)
                        if c >= 1:
                            v.wait_ge(sm.s_dpe, c)
                        v.tensor_tensor(hswb[:, hc * 512:(hc + 1) * 512],
                                        upp[:, (ghc % 2) * 512:(ghc % 2 + 1) * 512],
                                        sgbuf[:, (ghc % 2) * 512:(ghc % 2 + 1) * 512],
                                        ALU.mult).then_inc(sm.s_hu, 1)

                def y_copies(c):
                    v.wait_ge(sm.s_dpe, c + 1)
                    for tt in range(4):
                        gt = 4 * c + tt
                        if gt >= 4:
                            v.wait_ge(sm.s_s5b, gt - 3)
                        tgt, off = y_tgts[tt]
                        v.tensor_copy(y_sb[:, (gt % 4) * DM:(gt % 4 + 1) * DM],
                                      tgt[:, off:off + 1024]).then_inc(sm.s_yraw, 1)

                def tail_block(c):
                    for tt in range(4):
                        gt = 4 * c + tt
                        p2 = gt % 2
                        sl = (gt % 4) * DM
                        v.wait_ge(sm.s_yraw, gt + 1)
                        v.bn_stats(bst4[:, p2 * 12:p2 * 12 + 6], y_sb[:, sl:sl + 512])
                        hop(v.bn_stats(bst4[:, p2 * 12 + 6:p2 * 12 + 12],
                                       y_sb[:, sl + 512:sl + 1024]))
                        hop(v.bn_aggr(agg4[:, p2 * 2:p2 * 2 + 2],
                                      bst4[:, p2 * 12:p2 * 12 + 12]))
                        v.tensor_scalar(sc4[:, p2 * 5:p2 * 5 + 1],
                                        agg4[:, p2 * 2 + 1:p2 * 2 + 2],
                                        S2 * S2, EPS, ALU.mult, ALU.add).then_inc(sm.s_v5a, 1)
                        v.wait_ge(sm.s_s5a, gt + 1)
                        hop(v.reciprocal(sc4[:, p2 * 5 + 2:p2 * 5 + 3],
                                         sc4[:, p2 * 5 + 1:p2 * 5 + 2]))
                        v.tensor_scalar_mul(sc4[:, p2 * 5 + 3:p2 * 5 + 4],
                                            sc4[:, p2 * 5 + 2:p2 * 5 + 3], S2)
                        v.scalar_tensor_tensor(sc4[:, p2 * 5 + 4:p2 * 5 + 5],
                                               agg4[:, p2 * 2:p2 * 2 + 1], -S2,
                                               sc4[:, p2 * 5 + 2:p2 * 5 + 3],
                                               ALU.mult, ALU.mult).then_inc(sm.s_v5b, 1)
                        v.wait_ge(sm.s_s5b, gt + 1)
                        v.wait_ge((sm.s_x1f0, sm.s_x1f1)[p2], 16 * (gt // 2 + 1))
                        if gt >= 2:
                            v.wait_ge((sm.s_out0, sm.s_out1)[p2], 16 * (gt // 2))
                        v.tensor_add(ostage[:, p2 * DM:(p2 + 1) * DM],
                                     ostage[:, p2 * DM:(p2 + 1) * DM],
                                     x1f[:, (gt % 2) * DM:(gt % 2 + 1) * DM]).then_inc(sm.s_oady, 1)

                for c in range(NC4):
                    tt_block(c)
                    if c >= 1:
                        tail_block(c - 1)
                    y_copies(c)
                tail_block(NC4 - 1)

    fw = {1: [("s_sp10", 16 * NT // 2), ("s_sp11", 16 * NT // 2)],
          2: [("s_qsp0", 16 * 4 * NCH), ("s_qsp1", 16 * 4 * NCH),
              ("s_ksp0", 16 * 4 * NCH), ("s_ksp1", 16 * 4 * NCH)],
          3: [("s_cs00", 16 * 4 * 8), ("s_cs01", 16 * 4 * 8),
              ("s_cs10", 16 * 4 * 8), ("s_cs11", 16 * 4 * 8)],
          4: [("s_sp3b0", 32 * NT // 2), ("s_sp3b1", 32 * NT // 2), ("s_wgu", 16 * 6)],
          5: [("s_sp3b0", 32 * NT // 2), ("s_sp3b1", 32 * NT // 2), ("s_wgu", 16 * 6)]}

    _p1a()
    if phases >= 2:
        hold["v_sb"] = es_v.enter_context(
            nc.sbuf_tensor("v_sb", [128, 32 * 1040], BF16, side="right"))
        if phases >= 3:
            hold["qt_buf"] = es_v.enter_context(
                nc.sbuf_tensor("qt_buf", [128, 2 * S], BF16, side="right"))
            hold["kt_buf"] = es_v.enter_context(
                nc.sbuf_tensor("kt_buf", [128, 2 * S], BF16, side="right"))
            hold["qstage"] = es_v.enter_context(
                nc.sbuf_tensor("qstage", [128, 2 * 512], BF16, side="right"))
            hold["kstage"] = es_v.enter_context(
                nc.sbuf_tensor("kstage", [128, 2 * 512], BF16, side="right"))
        _p1b()
    es_qkv.close()
    if phases >= 3:
        hold["ctxb"] = es.enter_context(
            nc.sbuf_tensor("p3_ctx", [128, 8 * 1040], BF16))
        _p2()
    es_v.close()
    if phases >= 4:
        hold["wg_sb"] = es_ffn.enter_context(
            nc.sbuf_tensor("wg_sb", [128, 8 * FF], BF16, side="right"))
        hold["wd_sb"] = es_ffn.enter_context(
            nc.sbuf_tensor("wd_sb", [128, 24 * DM], BF16, side="right"))
        hold["x2T0"] = es_ffn.enter_context(
            nc.sbuf_tensor("x2T0", [128, 8 * 512], BF16))
        _p3a()
    if phases >= 5:
        _p3b()
    if phases >= 6:
        _p4()
    if phases < 6:
        with nc.Block() as blk:
            @blk.gpsimd
            def _(g):
                for sname, tot in fw[phases]:
                    g.wait_ge(sems[sname], tot)
    es_ffn.close()
    es.close()
    return nc


def kernel(L0_attn_hidden_states, L0_attn_Wq, L0_attn_bq, L0_attn_Wk, L0_attn_bk,
           L0_attn_Wv, L0_attn_bv, L0_ffn_w_gate, L0_ffn_w_up, L0_ffn_w_down):
    x = np.ascontiguousarray(np.asarray(L0_attn_hidden_states))
    wq = np.ascontiguousarray(np.asarray(L0_attn_Wq).T)
    wk = np.ascontiguousarray(np.asarray(L0_attn_Wk).T)
    wv = np.ascontiguousarray(np.asarray(L0_attn_Wv).T)
    bq = np.ascontiguousarray(np.asarray(L0_attn_bq))
    bk = np.ascontiguousarray(np.asarray(L0_attn_bk))
    bv = np.ascontiguousarray(np.asarray(L0_attn_bv))
    wg = np.ascontiguousarray(np.asarray(L0_ffn_w_gate).T)
    wu = np.ascontiguousarray(np.asarray(L0_ffn_w_up).T)
    wd = np.ascontiguousarray(np.asarray(L0_ffn_w_down).T)

    nc = build_nc()
    in_maps = [{"x_in": x[b], "wqT": wq, "wkT": wk, "wvT": wv,
                "bq": bq, "bk": bk, "bv": bv,
                "wgT": wg, "wuT": wu, "wdT": wd} for b in range(8)]
    res = run_bass_kernel_spmd(nc, in_maps, core_ids=list(range(8)))
    return np.stack([res.results[b]["out"] for b in range(8)], axis=0)


# version 20
# speedup vs baseline: 1.0883x; 1.0064x over previous
import numpy as np
from contextlib import ExitStack
import concourse.bass as bass
import concourse.mybir as mybir
from concourse.bass_utils import run_bass_kernel_spmd

F32 = mybir.dt.float32
BF16 = mybir.dt.bfloat16
I32 = mybir.dt.int32
AF = mybir.ActivationFunctionType
ALU = mybir.AluOpType

S = 4096
DM = 1024
H = 16
NB = 64
FF = 3072
NT = S // 128
NCH = S // 512
QSCALE = 0.125
DQ = float(np.float32(1.0 / 127.0))
S2 = DQ * DQ
EPS = 1e-5


def attends(qb):
    if qb in (0, NB - 1):
        return list(range(NB))
    if qb == 1:
        return [0, 1, 2, NB - 1]
    if qb == NB - 2:
        return [0, NB - 3, NB - 2, NB - 1]
    return sorted({0, qb - 1, qb, qb + 1, NB - 1})


PAIRS = [(0, NB - 1)] + [(2 * t - 1, 2 * t) for t in range(1, NB // 2)]
NP_ = len(PAIRS)


def pair_runs(t):
    if t == 0:
        return [(0, NB)]
    lo, hi = max(1, 2 * t - 2), min(NB - 2, 2 * t + 1)
    return [(0, 1), (lo, hi - lo + 1), (NB - 1, 1)]


def _build_tables():
    seg_off, seg_w = [], []
    off = 0
    for t in range(NP_):
        seg_off.append(off)
        w = sum(n for _, n in pair_runs(t)) * 64
        seg_w.append(w)
        off += w
    arena_cols = off

    memsets = []
    for t in range(NP_):
        jt, jb = PAIRS[t]
        rects = []
        for half, j in ((0, jt), (1, jb)):
            c = 0
            bad = []
            for qs, n in pair_runs(t):
                for k in range(n):
                    if j not in attends(qs + k):
                        bad.append([c + k * 64, c + k * 64 + 64])
                c += n * 64
            merged = []
            for lo, hi in bad:
                if merged and merged[-1][1] == lo:
                    merged[-1][1] = hi
                else:
                    merged.append([lo, hi])
            rects += [(half, lo, hi) for lo, hi in merged]
        memsets.append(rects)

    ctx_lists = []
    total_w = 0
    for C in range(NCH):
        entries = []
        for t in range(NP_):
            c = 0
            for qs, n in pair_runs(t):
                lo, hi = max(qs, 8 * C), min(qs + n, 8 * C + 8)
                if lo < hi:
                    entries.append((t, seg_off[t] + c + (lo - qs) * 64,
                                    (lo - 8 * C) * 64, (hi - lo) * 64))
                c += n * 64
        ctx_lists.append(entries)
        total_w += sum(e[3] for e in entries)
    assert total_w == arena_cols, (total_w, arena_cols)
    return seg_off, seg_w, arena_cols, memsets, ctx_lists


SEG_OFF, SEG_W, ARENA_COLS, MEMSETS, CTX_LISTS = _build_tables()
SGROUPS = 8 + (NP_ - 1)
MS_INC_PER_U = sum(1 for t in range(1, NP_) if MEMSETS[t])


def _v_plan():
    plans = []
    for c in range(NCH):
        groups = []
        pA = (0, 64, 0 if c == 0 else 64, 0 if c == 0 else 4 * c)
        pB = (448, 64, 64 if c == 7 else 0, 0 if c == 7 else 4 * c + 4)
        if c in (0, 7):
            groups += [[pA], [pB]]
        else:
            groups += [[pA, pB]]
        for w in (1, 2, 3):
            groups.append([(w * 128 - 64, 128, 0, 4 * c + w)])
        plans.append(groups)
    return plans


V_PLAN = _v_plan()
V_COPIES = [[sum(2 if p[1] == 128 else 1 for p in g) for g in plan]
            for plan in V_PLAN]
V_COPY_PREFIX = []
_acc = 0
for _cnts in V_COPIES:
    for _cnt in _cnts:
        V_COPY_PREFIX.append(_acc)
        _acc += _cnt
V_COPY_TOTAL = _acc
VG_PER_CHUNK = [len(p) for p in V_PLAN]
VG_PREFIX = [0]
for _n in VG_PER_CHUNK:
    VG_PREFIX.append(VG_PREFIX[-1] + _n)


def build_nc(debug=(), phases=6):
    nc = bass.Bass("TRN2", target_bir_lowering=False)

    def scratch(name, shape, dtype):
        kind = "ExternalOutput" if name in debug else "Internal"
        return nc.dram_tensor(name, shape, dtype, kind=kind).ap()

    x_in = nc.dram_tensor("x_in", [S, DM], F32, kind="ExternalInput").ap()
    wqT = nc.dram_tensor("wqT", [DM, DM], F32, kind="ExternalInput").ap()
    wkT = nc.dram_tensor("wkT", [DM, DM], F32, kind="ExternalInput").ap()
    wvT = nc.dram_tensor("wvT", [DM, DM], F32, kind="ExternalInput").ap()
    bq_d = nc.dram_tensor("bq", [DM], F32, kind="ExternalInput").ap()
    bk_d = nc.dram_tensor("bk", [DM], F32, kind="ExternalInput").ap()
    bv_d = nc.dram_tensor("bv", [DM], F32, kind="ExternalInput").ap()
    wgT_d = nc.dram_tensor("wgT", [DM, FF], I32, kind="ExternalInput").ap()
    wuT_d = nc.dram_tensor("wuT", [DM, FF], I32, kind="ExternalInput").ap()
    wdT_d = nc.dram_tensor("wdT", [FF, DM], I32, kind="ExternalInput").ap()
    out_d = nc.dram_tensor("out", [S, DM], F32, kind="ExternalOutput").ap()

    xn1_d = scratch("xn1_d", [S, DM], BF16)
    qkt_d = scratch("qkt_d", [2, 8, 128, S], BF16)
    ctxT_d = scratch("ctxT_d", [1040, S], BF16)
    ln2_d = scratch("ln2_d", [S, DM], BF16)
    x1_d = scratch("x1_d", [S, DM], F32)
    xn2_d = scratch("xn2_d", [S, DM], BF16)

    es = ExitStack()
    es_qkv = ExitStack()
    es_v = ExitStack()
    es_ffn = ExitStack()
    sems = {}

    def sem(name):
        sems[name] = es.enter_context(nc.semaphore(name))

    for n in ("s_bb s_wq s_bvr s_xld0 s_xld1 s_xld2 s_xld3 s_bn1 s_va s_sq s_vb s_ln1 "
              "s_sp10 s_sp11 "
              "s_trl0 s_trl1 s_qg s_kg s_vg s_qev s_kev s_vev "
              "s_qsp0 s_qsp1 s_ksp0 s_ksp1 "
              "s_ql0 s_ql1 s_spe s_exp s_ms s_cpe s_cev0 s_cev1 "
              "s_cs00 s_cs01 s_cs10 s_cs11 "
              "s_ctl0 s_ctl1 s_ctl2 s_ctl3 s_ctl4 s_ctl5 s_ctl6 s_ctl7 s_bn3 s_v3a s_s3a s_v3b s_s3b s_sp30 s_sp31 "
              "s_xl0 s_xl1 s_xl2 s_xl3 s_xl4 s_xl5 s_xl6 s_xl7 s_bn3b s_v4a s_s4a s_v4b s_s4b s_sp3b0 s_sp3b1 "
              "s_wgu s_wua s_wub s_wuc s_wud s_x2t0 s_x2t1 s_x1f0 s_x1f1 s_gu s_sg s_hu s_dpe s_yraw "
              "s_bn4 s_v5a s_s5a s_v5b s_s5b s_oady s_out0 s_out1").split():
        sem(n)
    sm = type("SM", (), sems)

    bv_rep = es.enter_context(nc.sbuf_tensor("bv_rep", [128, DM], F32, side="right"))
    wq_sb = es_qkv.enter_context(nc.sbuf_tensor("wq_sb", [128, 8 * DM], BF16))
    wk_sb = es_qkv.enter_context(nc.sbuf_tensor("wk_sb", [128, 8 * DM], BF16))
    wv_sb = es_qkv.enter_context(nc.sbuf_tensor("wv_sb", [128, 8 * DM], BF16))
    bqsT = es_qkv.enter_context(nc.sbuf_tensor("bqsT", [128, 8], F32))
    bkT = es_qkv.enter_context(nc.sbuf_tensor("bkT", [128, 8], F32))
    hold = {}

    def _p1a():
        NI = NT // 2
        with (
            nc.sbuf_tensor("xbuf", [128, 8 * DM], F32) as xbuf,
            nc.sbuf_tensor("xn1buf", [128, 4 * DM], BF16) as xn1buf,
            nc.sbuf_tensor("bv_row", [1, DM], BF16) as bv_row,
            nc.sbuf_tensor("ones1", [1, 128], BF16) as ones1,
            nc.sbuf_tensor("bst1", [128, 2 * 24], F32) as bst1,
            nc.sbuf_tensor("agg1", [128, 2 * 4], F32) as agg1,
            nc.sbuf_tensor("sc1", [128, 2 * 8], F32) as sc1,
            nc.psum_tensor("bvp", [128, DM], F32) as bvp,
            nc.Block() as blk,
        ):
            s_xl = (sm.s_xld0, sm.s_xld1, sm.s_xld2, sm.s_xld3)

            @blk.gpsimd
            def _(g):
                g.dma_start(bv_row[:], bv_d[None, :]).then_inc(sm.s_bb, 16)
                for u in range(8):
                    g.dma_start(bqsT[:, u:u + 1],
                                bq_d[u * 128:(u + 1) * 128][:, None]).then_inc(sm.s_bb, 16)
                for u in range(8):
                    g.dma_start(bkT[:, u:u + 1],
                                bk_d[u * 128:(u + 1) * 128][:, None]).then_inc(sm.s_bb, 16)
                for w_d, w_sb in ((wqT, wq_sb), (wkT, wk_sb), (wvT, wv_sb)):
                    g.dma_start(w_sb[:].rearrange("p (dc c) -> p dc c", dc=8),
                                w_d.rearrange("(dc p) c -> p dc c", p=128)).then_inc(sm.s_wq, 16)

            @blk.sync
            def _(sy):
                for it in range(NI):
                    if it >= 4:
                        sy.wait_ge(sm.s_ln1, 2 * it - 6)
                    for k in (0, 1):
                        t = 2 * it + k
                        sy.dma_start(xbuf[:, (t % 8) * DM:(t % 8 + 1) * DM],
                                     x_in[t * 128:(t + 1) * 128, :]).then_inc(
                            s_xl[it % 4], 16)

            @blk.tensor
            def _(te):
                te.wait_ge(sm.s_bb, 272)
                te.wait_ge(sm.s_bvr, 1)
                te.matmul(bvp[:, 0:512], ones1[:], bv_row[:, 0:512], start=True, stop=True)
                te.matmul(bvp[:, 512:1024], ones1[:], bv_row[:, 512:1024],
                          start=True, stop=True).then_inc(sm.s_bvr, 1)

            @blk.vector
            def _(v):
                hz = [0]

                def hop(instr):
                    instr.then_inc(sm.s_bn1, 1)
                    hz[0] += 1
                    v.wait_ge(sm.s_bn1, hz[0])
                v.memset(ones1[:], 1.0).then_inc(sm.s_bvr, 1)
                v.wait_ge(sm.s_bvr, 2)
                v.tensor_copy(bv_rep[:], bvp[:])
                v.wait_ge(sm.s_bb, 272)
                v.tensor_scalar_mul(bqsT[:], bqsT[:], QSCALE)
                for it in range(NI + 1):
                    pb = it % 2
                    if it < NI:
                        v.wait_ge(s_xl[it % 4], 32 * (it // 4 + 1))
                        for k in (0, 1):
                            t = 2 * it + k
                            sl = (t % 8) * DM
                            v.bn_stats(bst1[:, pb * 24 + k * 12:pb * 24 + k * 12 + 6],
                                       xbuf[:, sl:sl + 512])
                            ins = v.bn_stats(bst1[:, pb * 24 + k * 12 + 6:pb * 24 + k * 12 + 12],
                                             xbuf[:, sl + 512:sl + 1024])
                        hop(ins)
                        v.bn_aggr(agg1[:, pb * 4:pb * 4 + 2], bst1[:, pb * 24:pb * 24 + 12])
                        hop(v.bn_aggr(agg1[:, pb * 4 + 2:pb * 4 + 4],
                                      bst1[:, pb * 24 + 12:pb * 24 + 24]))
                        v.tensor_scalar_add(
                            sc1[:, pb * 8:pb * 8 + 2],
                            agg1[:, pb * 4 + 1:pb * 4 + 4:2],
                            EPS).then_inc(sm.s_va, 1)
                    if it >= 1:
                        qb = (it - 1) % 2
                        v.wait_ge(sm.s_sq, it)
                        hop(v.reciprocal(sc1[:, qb * 8 + 4:qb * 8 + 6],
                                         sc1[:, qb * 8 + 2:qb * 8 + 4]))
                        v.scalar_tensor_tensor(
                            sc1[:, qb * 8 + 6:qb * 8 + 8],
                            agg1[:, qb * 4:qb * 4 + 3:2], -1.0,
                            sc1[:, qb * 8 + 4:qb * 8 + 6],
                            ALU.mult, ALU.mult).then_inc(sm.s_vb, 1)

            @blk.scalar
            def _(s):
                for it in range(NI + 1):
                    pb = it % 2
                    if it < NI:
                        s.wait_ge(sm.s_va, it + 1)
                        s.activation(sc1[:, pb * 8 + 2:pb * 8 + 4],
                                     sc1[:, pb * 8:pb * 8 + 2], AF.Sqrt).then_inc(sm.s_sq, 1)
                    if it >= 1:
                        qb = (it - 1) % 2
                        s.wait_ge(sm.s_vb, it)
                        if it - 1 >= 2:
                            s.wait_ge((sm.s_sp10, sm.s_sp11)[qb], 32 * ((it - 1) // 2))
                        for k in (0, 1):
                            t = 2 * (it - 1) + k
                            s.activation(xn1buf[:, (t % 4) * DM:(t % 4 + 1) * DM],
                                         xbuf[:, (t % 8) * DM:(t % 8 + 1) * DM], AF.Identity,
                                         bias=sc1[:, qb * 8 + 6 + k:qb * 8 + 7 + k],
                                         scale=sc1[:, qb * 8 + 4 + k:qb * 8 + 5 + k]).then_inc(sm.s_ln1, 1)
                            s.wait_ge(sm.s_ln1, t + 1)
                            s.dma_start(xn1_d[t * 128:(t + 1) * 128, :],
                                        xn1buf[:, (t % 4) * DM:(t % 4 + 1) * DM]).then_inc(
                                (sm.s_sp10, sm.s_sp11)[(t // 2) % 2], 16)

    def _p1b():
        v_sb = hold["v_sb"]
        qstage = hold.get("qstage")
        kstage = hold.get("kstage")
        if qstage is None:
            qstage = es_v.enter_context(
                nc.sbuf_tensor("qstage", [128, 2 * 512], BF16, side="right"))
            kstage = es_v.enter_context(
                nc.sbuf_tensor("kstage", [128, 2 * 512], BF16, side="right"))
        with (
            nc.sbuf_tensor("xn1T", [128, 2 * 8 * 512], BF16) as xn1T,
            nc.psum_tensor("qp", [128, 2 * 512], F32) as qp,
            nc.psum_tensor("kp", [128, 2 * 512], F32) as kp,
            nc.psum_tensor("vp", [128, 2 * 1024], F32) as vp,
            nc.Block() as blk,
        ):
            def xn1T_sl(par, dc, lo=0, w=512):
                base = par * 8 * 512 + dc * 512
                return xn1T[:, base + lo: base + lo + w]

            @blk.vector
            def _(v):
                v.memset(v_sb[:, 64:32 * 1040:65], 1.0).then_inc(sm.s_vev, 1)
                gv = 0
                for c in range(NCH):
                    for u in range(8):
                        gq = c * 8 + u
                        v.wait_ge(sm.s_qg, gq + 1)
                        if gq >= 2:
                            v.wait_ge((sm.s_qsp0, sm.s_qsp1)[gq % 2], 16 * (gq // 2))
                        par = gq % 2
                        v.tensor_scalar(qstage[:, par * 512:(par + 1) * 512],
                                        qp[:, par * 512:(par + 1) * 512],
                                        QSCALE, bqsT[:, u:u + 1],
                                        ALU.mult, ALU.add).then_inc(sm.s_qev, 1)
                    for grp in V_PLAN[c]:
                        v.wait_ge(sm.s_vg, gv + 1)
                        par = gv % 2
                        psl = vp[:, par * 1024:(par + 1) * 1024]
                        for (lhs_lo, lhs_w, plo, slot) in grp:
                            rlos = (0, 64) if lhs_w == 128 else (plo,)
                            for rlo in rlos:
                                dst = v_sb[rlo:rlo + 64,
                                           slot * 1040:slot * 1040 + 1040]
                                dst = dst.rearrange("p (h e) -> p h e", e=65)[:, :, 0:64]
                                v.tensor_copy(dst, psl[rlo:rlo + 64, :].rearrange(
                                    "p (h e) -> p h e", e=64)).then_inc(sm.s_vev, 1)
                        gv += 1

            @blk.sync
            def _(sy):
                for c in range(NCH):
                    sy.wait_ge(sm.s_sp10, 32 * (c + 1))
                    sy.wait_ge(sm.s_sp11, 32 * (c + 1))
                    if c >= 2:
                        sy.wait_ge(sm.s_vg, VG_PREFIX[c - 1])
                    for dc in range(8):
                        sy.dma_start_transpose(
                            xn1T_sl(c % 2, dc),
                            xn1_d[c * 512:(c + 1) * 512, dc * 128:(dc + 1) * 128]
                        ).then_inc((sm.s_trl0, sm.s_trl1)[c % 2], 16)
                if "qt_buf" in hold:
                    for u in (0, 1):
                        cnt = len([g for g in range(64)
                                   if g % 2 == u % 2 and g <= 56 + u])
                        sy.wait_ge((sm.s_qsp0, sm.s_qsp1)[u % 2], 16 * cnt)
                        sy.wait_ge((sm.s_ksp0, sm.s_ksp1)[u % 2], 16 * cnt)
                        sy.dma_start(hold["qt_buf"][:, (u % 2) * S:(u % 2 + 1) * S],
                                     qkt_d[0, u, :, :]).then_inc(
                            (sm.s_ql0, sm.s_ql1)[u % 2], 16)
                        sy.dma_start(hold["kt_buf"][:, (u % 2) * S:(u % 2 + 1) * S],
                                     qkt_d[1, u, :, :]).then_inc(
                            (sm.s_ql0, sm.s_ql1)[u % 2], 16)

            @blk.tensor
            def _(te):
                te.wait_ge(sm.s_wq, 48)
                gv = 0
                for c in range(NCH):
                    te.wait_ge((sm.s_trl0, sm.s_trl1)[c % 2], 16 * 8 * (c // 2 + 1))
                    par = c % 2
                    for u in range(8):
                        gq = c * 8 + u
                        if gq >= 2:
                            te.wait_ge(sm.s_qev, gq - 1)
                            te.wait_ge(sm.s_kev, gq - 1)
                        qsl = qp[:, (gq % 2) * 512:(gq % 2 + 1) * 512]
                        ksl = kp[:, (gq % 2) * 512:(gq % 2 + 1) * 512]
                        for dc in range(8):
                            mm = te.matmul(qsl,
                                           wq_sb[:, dc * DM + u * 128:dc * DM + (u + 1) * 128],
                                           xn1T_sl(par, dc), start=(dc == 0), stop=(dc == 7))
                        mm.then_inc(sm.s_qg, 1)
                        for dc in range(8):
                            mm = te.matmul(ksl,
                                           wk_sb[:, dc * DM + u * 128:dc * DM + (u + 1) * 128],
                                           xn1T_sl(par, dc), start=(dc == 0), stop=(dc == 7))
                        mm.then_inc(sm.s_kg, 1)
                    for grp in V_PLAN[c]:
                        if gv >= 2:
                            te.wait_ge(sm.s_vev, 1 + V_COPY_PREFIX[gv - 1])
                        vsl = vp[:, (gv % 2) * 1024:(gv % 2 + 1) * 1024]
                        for half in (0, 1):
                            for (lhs_lo, lhs_w, plo, slot) in grp:
                                for dc in range(8):
                                    mm = te.matmul(
                                        vsl[plo:plo + lhs_w, half * 512:(half + 1) * 512],
                                        xn1T_sl(par, dc, lhs_lo, lhs_w),
                                        wv_sb[:, dc * DM + half * 512:dc * DM + (half + 1) * 512],
                                        start=(dc == 0), stop=(dc == 7),
                                        tile_position=(0, plo))
                        mm.then_inc(sm.s_vg, 1)
                        gv += 1

            @blk.scalar
            def _(s):
                for gq in range(8 * NCH):
                    c, u = gq // 8, gq % 8
                    s.wait_ge(sm.s_kg, gq + 1)
                    if gq >= 2:
                        s.wait_ge((sm.s_ksp0, sm.s_ksp1)[gq % 2], 16 * (gq // 2))
                    par = gq % 2
                    s.activation(kstage[:, par * 512:(par + 1) * 512],
                                 kp[:, par * 512:(par + 1) * 512], AF.Identity,
                                 bias=bkT[:, u:u + 1], scale=1.0).then_inc(sm.s_kev, 1)

            @blk.gpsimd
            def _(g):
                for gq in range(8 * NCH):
                    c, u = gq // 8, gq % 8
                    par = gq % 2
                    g.wait_ge(sm.s_qev, gq + 1)
                    g.dma_start(qkt_d[0, u, :, c * 512:(c + 1) * 512],
                                qstage[:, par * 512:(par + 1) * 512]).then_inc(
                        (sm.s_qsp0, sm.s_qsp1)[par], 16)
                    g.wait_ge(sm.s_kev, gq + 1)
                    g.dma_start(qkt_d[1, u, :, c * 512:(c + 1) * 512],
                                kstage[:, par * 512:(par + 1) * 512]).then_inc(
                        (sm.s_ksp0, sm.s_ksp1)[par], 16)

    def _p2():
        v_sb = hold["v_sb"]
        A = ARENA_COLS
        ctxb = hold["ctxb"]
        qt_buf, kt_buf = hold["qt_buf"], hold["kt_buf"]
        s_ct = (sm.s_ctl0, sm.s_ctl1, sm.s_ctl2, sm.s_ctl3,
                sm.s_ctl4, sm.s_ctl5, sm.s_ctl6, sm.s_ctl7)
        with (
            nc.sbuf_tensor("arenaB", [128, 2 * A], BF16) as arenaB,
            nc.sbuf_tensor("cstage0", [65, 2 * 512], BF16) as cstage0,
            nc.sbuf_tensor("cstage1", [65, 2 * 512], BF16) as cstage1,
            nc.psum_tensor("pp8", [128, 8 * 512], F32) as pp8,
            nc.Block() as blk,
        ):
            def s_cols(t, gi):
                if t == 0:
                    return 512 * gi, 512
                return SEG_OFF[t], SEG_W[t]

            @blk.sync
            def _(sy):
                for u in range(2, 8):
                    sy.wait_ge(sm.s_qsp0, 16 * 4 * NCH)
                    sy.wait_ge(sm.s_qsp1, 16 * 4 * NCH)
                    sy.wait_ge(sm.s_ksp0, 16 * 4 * NCH)
                    sy.wait_ge(sm.s_ksp1, 16 * 4 * NCH)
                    sy.wait_ge(sm.s_spe, SGROUPS * (u - 1))
                    sy.dma_start(qt_buf[:, (u % 2) * S:(u % 2 + 1) * S],
                                 qkt_d[0, u, :, :]).then_inc((sm.s_ql0, sm.s_ql1)[u % 2], 16)
                    sy.dma_start(kt_buf[:, (u % 2) * S:(u % 2 + 1) * S],
                                 qkt_d[1, u, :, :]).then_inc((sm.s_ql0, sm.s_ql1)[u % 2], 16)
                for t in range(8):
                    Ct = t // 4
                    for h01, sems in ((0, (sm.s_cs00, sm.s_cs01)),
                                      (1, (sm.s_cs10, sm.s_cs11))):
                        for par in (0, 1):
                            ncnt = (Ct - par) // 2 + 1 if Ct >= par else 0
                            if 28 + ncnt > 0:
                                sy.wait_ge(sems[par], 16 * (28 + ncnt))
                    sy.dma_start_transpose(
                        ctxb[:, (t % 8) * 1040:(t % 8 + 1) * 1040],
                        ctxT_d[:, t * 128:(t + 1) * 128]).then_inc(s_ct[t % 8], 16)

            @blk.tensor
            def _(te):
                last_use = {}
                for u in range(8):
                    up = (u % 2) * S
                    te.wait_ge((sm.s_ql0, sm.s_ql1)[u % 2], 32 * (u // 2 + 1))
                    if u >= 1:
                        te.wait_ge(sm.s_cev0, 8 * u)
                        te.wait_ge(sm.s_cev1, 8 * u)
                    for gi in range(SGROUPS):
                        gg = u * SGROUPS + gi
                        par = gi % 4
                        if par in last_use:
                            te.wait_ge(sm.s_exp, last_use[par] + 1)
                        last_use[par] = gg
                        t = 0 if gi < 8 else gi - 7
                        jt, jb = PAIRS[t]
                        runs = [(8 * gi, 8)] if t == 0 else pair_runs(t)
                        col = 0
                        last = None
                        for (qs, n) in runs:
                            w = n * 64
                            for (rows, hoff) in ((0, 0), (64, 512)):
                                rhs = qt_buf[rows:rows + 64,
                                             up + qs * 64: up + qs * 64 + w]
                                for (tcol, j) in ((0, jt), (64, jb)):
                                    last = te.matmul(
                                        pp8[tcol:tcol + 64,
                                            par * 1024 + hoff + col: par * 1024 + hoff + col + w],
                                        kt_buf[rows:rows + 64,
                                               up + j * 64: up + j * 64 + 64],
                                        rhs, start=True, stop=True,
                                        tile_position=(rows, tcol))
                            col += w
                        last.then_inc(sm.s_spe, 1)
                    te.wait_ge(sm.s_exp, SGROUPS * (u + 1))
                    te.wait_ge(sm.s_ms, MS_INC_PER_U * (u + 1))
                    for C in range(NCH):
                        for h01 in (0, 1):
                            if C >= 2:
                                te.wait_ge(sm.s_cev0, 8 * u + C - 1)
                                te.wait_ge(sm.s_cev1, 8 * u + C - 1)
                            elif u >= 1:
                                te.wait_ge(sm.s_cev0, 8 * u)
                                te.wait_ge(sm.s_cev1, 8 * u)
                            pcoff = (h01 * 2 + (C % 2)) * 512
                            h = 2 * u + h01
                            n_e = len(CTX_LISTS[C])
                            for ei, (t, acol, pcol, w) in enumerate(CTX_LISTS[C]):
                                mm = te.matmul(
                                    pp8[0:65, pcoff + pcol:pcoff + pcol + w],
                                    v_sb[:, t * 1040 + h * 65: t * 1040 + h * 65 + 65],
                                    arenaB[:, h01 * A + acol:h01 * A + acol + w],
                                    start=(ei == 0), stop=(ei == n_e - 1))
                            mm.then_inc(sm.s_cpe, 1)

            @blk.scalar
            def _(s):
                for u in range(8):
                    for gi in range(SGROUPS):
                        gg = u * SGROUPS + gi
                        par = gi % 4
                        t = 0 if gi < 8 else gi - 7
                        acol, w = s_cols(t, gi)
                        s.wait_ge(sm.s_spe, gg + 1)
                        src_ap = pp8[:, par * 1024:(par + 1) * 1024].rearrange(
                            "p (b c) -> p b c", b=2)[:, :, 0:w]
                        dst_ap = arenaB[:].rearrange("p (b c) -> p b c", b=2)[
                            :, :, acol:acol + w]
                        s.activation(dst_ap, src_ap, AF.Exp).then_inc(sm.s_exp, 1)

            @blk.vector
            def _(v):
                for u in range(8):
                    for gi in range(SGROUPS):
                        gg = u * SGROUPS + gi
                        t = 0 if gi < 8 else gi - 7
                        if t == 0 or not MEMSETS[t]:
                            continue
                        v.wait_ge(sm.s_exp, gg + 1)
                        last = None
                        for (half, lo, hi) in MEMSETS[t]:
                            for h01 in (0, 1):
                                last = v.memset(
                                    arenaB[half * 64:half * 64 + 64,
                                           h01 * A + SEG_OFF[t] + lo:
                                           h01 * A + SEG_OFF[t] + hi], 0.0)
                        last.then_inc(sm.s_ms, 1)
                    for C in range(NCH):
                        v.wait_ge(sm.s_cpe, u * 16 + C * 2 + 1)
                        if 4 * u + C // 2 >= 1:
                            v.wait_ge((sm.s_cs00, sm.s_cs01)[C % 2], 16 * (4 * u + C // 2))
                        v.tensor_copy(cstage0[:, (C % 2) * 512:(C % 2 + 1) * 512],
                                      pp8[0:65, (C % 2) * 512:(C % 2 + 1) * 512]).then_inc(sm.s_cev0, 1)
                        v.wait_ge(sm.s_cpe, u * 16 + C * 2 + 2)
                        if 4 * u + C // 2 >= 1:
                            v.wait_ge((sm.s_cs10, sm.s_cs11)[C % 2], 16 * (4 * u + C // 2))
                        v.tensor_copy(cstage1[:, (C % 2) * 512:(C % 2 + 1) * 512],
                                      pp8[0:65, 1024 + (C % 2) * 512:1024 + (C % 2 + 1) * 512]).then_inc(sm.s_cev1, 1)

            @blk.gpsimd
            def _(g):
                for u in range(8):
                    for C in range(NCH):
                        g.wait_ge(sm.s_cev0, 8 * u + C + 1)
                        g.dma_start(ctxT_d[(2 * u) * 65:(2 * u) * 65 + 65,
                                           C * 512:(C + 1) * 512],
                                    cstage0[:, (C % 2) * 512:(C % 2 + 1) * 512]).then_inc(
                            (sm.s_cs00, sm.s_cs01)[C % 2], 16)
                        g.wait_ge(sm.s_cev1, 8 * u + C + 1)
                        g.dma_start(ctxT_d[(2 * u + 1) * 65:(2 * u + 1) * 65 + 65,
                                           C * 512:(C + 1) * 512],
                                    cstage1[:, (C % 2) * 512:(C % 2 + 1) * 512]).then_inc(
                            (sm.s_cs10, sm.s_cs11)[C % 2], 16)

    def _p3a():
        wg_sb, wd_sb = hold["wg_sb"], hold["wd_sb"]
        NI = NT // 2
        ctxb = hold["ctxb"]
        with (
            nc.sbuf_tensor("p3_x", [128, 8 * DM], F32) as xb3,
            nc.sbuf_tensor("p3_attn", [128, 4 * DM], F32) as attnb,
            nc.sbuf_tensor("p3_ln2", [128, 4 * DM], BF16) as ln2b,
            nc.sbuf_tensor("p3_x1", [128, 4 * DM], F32) as x1b,
            nc.sbuf_tensor("p3_xn2", [128, 4 * DM], BF16) as xn2b,
            nc.sbuf_tensor("p3_zg", [128, 2 * 32], F32) as zgb,
            nc.sbuf_tensor("p3_rz", [128, 2 * 32], F32) as rzb,
            nc.sbuf_tensor("bst3", [128, 2 * 24], F32) as bst3,
            nc.sbuf_tensor("agg3", [128, 2 * 4], F32) as agg3,
            nc.sbuf_tensor("sc3", [128, 2 * 8], F32) as sc3,
            nc.sbuf_tensor("bst3b", [128, 2 * 24], F32) as bst3b,
            nc.sbuf_tensor("agg3b", [128, 2 * 4], F32) as agg3b,
            nc.sbuf_tensor("sc3b", [128, 2 * 8], F32) as sc3b,
            nc.Block() as blk,
        ):
            s_ct = (sm.s_ctl0, sm.s_ctl1, sm.s_ctl2, sm.s_ctl3,
                    sm.s_ctl4, sm.s_ctl5, sm.s_ctl6, sm.s_ctl7)
            s_xlv = (sm.s_xl0, sm.s_xl1, sm.s_xl2, sm.s_xl3,
                     sm.s_xl4, sm.s_xl5, sm.s_xl6, sm.s_xl7)

            @blk.gpsimd
            def _(g):
                for pp in (sm.s_cs00, sm.s_cs01, sm.s_cs10, sm.s_cs11):
                    g.wait_ge(pp, 16 * 4 * 8)
                for (w_d, w_sb) in ((wgT_d, wg_sb),):
                    for half in (0, 1):
                        srcw = w_d[:, half * (FF // 2):(half + 1) * (FF // 2)]
                        dst = w_sb[:].rearrange("p (dc c) -> p dc c", dc=8)[
                            :, :, half * (FF // 2):(half + 1) * (FF // 2)]
                        g.dma_start(dst,
                                    srcw.rearrange("(dc p) c -> p dc c", p=128)).then_inc(sm.s_wgu, 16)
                for qr in range(4):
                    srcw = wdT_d[qr * 768:(qr + 1) * 768, :]
                    g.dma_start(
                        wd_sb[:, qr * 6 * DM:(qr + 1) * 6 * DM].rearrange(
                            "p (hc c) -> p hc c", hc=6),
                        srcw.rearrange("(hc p) c -> p hc c", p=128)).then_inc(sm.s_wgu, 16)

            @blk.sync
            def _(sy):
                for pp in (sm.s_cs00, sm.s_cs01, sm.s_cs10, sm.s_cs11):
                    sy.wait_ge(pp, 16 * 4 * 8)
                for t in range(NT):
                    if t >= 8:
                        sy.wait_ge(sm.s_v3a, t // 2 - 3)
                        sy.wait_ge(sm.s_v4a, t // 2 - 3)
                        sy.dma_start_transpose(
                            ctxb[:, (t % 8) * 1040:(t % 8 + 1) * 1040],
                            ctxT_d[:, t * 128:(t + 1) * 128]).then_inc(s_ct[t % 8], 16)
                    sy.dma_start(xb3[:, (t % 8) * DM:(t % 8 + 1) * DM],
                                 x_in[t * 128:(t + 1) * 128, :]).then_inc(s_xlv[t % 8], 16)
                sy.wait_ge(sm.s_sp3b0, 64)
                sy.wait_ge(sm.s_sp3b1, 64)
                for dc in range(8):
                    sy.dma_start_transpose(
                        hold["x2T0"][:, dc * 512:(dc + 1) * 512],
                        xn2_d[0:512, dc * 128:(dc + 1) * 128]).then_inc(sm.s_x2t0, 16)

            @blk.vector
            def _(v):
                hz = [0]

                def hop(instr):
                    instr.then_inc(sm.s_bn3, 1)
                    hz[0] += 1
                    v.wait_ge(sm.s_bn3, hz[0])

                def stage_A(it):
                    pb = it % 2
                    v.wait_ge(s_ct[(2 * it) % 8], 16 * (it // 4 + 1))
                    v.wait_ge(s_ct[(2 * it + 1) % 8], 16 * (it // 4 + 1))
                    if it >= 2:
                        v.wait_ge(sm.s_s3b, 2 * it - 2)
                    for k in (0, 1):
                        t = 2 * it + k
                        cb = ctxb[:, (t % 8) * 1040:(t % 8 + 1) * 1040]
                        ins = v.tensor_copy(zgb[:, pb * 32 + k * 16:pb * 32 + (k + 1) * 16],
                                            cb[:, 64:1040:65])
                    hop(ins)
                    hop(v.reciprocal(rzb[:, pb * 32:pb * 32 + 32],
                                     zgb[:, pb * 32:pb * 32 + 32]))
                    for k in (0, 1):
                        t = 2 * it + k
                        cb = ctxb[:, (t % 8) * 1040:(t % 8 + 1) * 1040]
                        cb3 = cb.rearrange("p (h e) -> p h e", e=65)[:, :, 0:64]
                        rz3 = rzb[:, pb * 32 + k * 16:pb * 32 + (k + 1) * 16].rearrange(
                            "p (h o) -> p h o", o=1).to_broadcast([128, H, 64])
                        at = attnb[:, (t % 4) * DM:(t % 4 + 1) * DM]
                        hop(v.tensor_tensor(at.rearrange("p (h e) -> p h e", e=64),
                                            cb3, rz3, ALU.mult))
                        ins = v.tensor_add(at, at, bv_rep[:])
                    hop(ins)
                    for k in (0, 1):
                        t = 2 * it + k
                        sl = (t % 4) * DM
                        v.bn_stats(bst3[:, pb * 24 + k * 12:pb * 24 + k * 12 + 6],
                                   attnb[:, sl:sl + 512])
                        ins = v.bn_stats(bst3[:, pb * 24 + k * 12 + 6:pb * 24 + k * 12 + 12],
                                         attnb[:, sl + 512:sl + 1024])
                    hop(ins)
                    v.bn_aggr(agg3[:, pb * 4:pb * 4 + 2], bst3[:, pb * 24:pb * 24 + 12])
                    hop(v.bn_aggr(agg3[:, pb * 4 + 2:pb * 4 + 4],
                                  bst3[:, pb * 24 + 12:pb * 24 + 24]))
                    v.tensor_scalar_add(sc3[:, pb * 8:pb * 8 + 2],
                                        agg3[:, pb * 4 + 1:pb * 4 + 4:2],
                                        EPS).then_inc(sm.s_v3a, 1)

                def stage_C(it):
                    qb = it % 2
                    v.wait_ge(sm.s_s3a, it + 1)
                    hop(v.reciprocal(sc3[:, qb * 8 + 4:qb * 8 + 6],
                                     sc3[:, qb * 8 + 2:qb * 8 + 4]))
                    v.scalar_tensor_tensor(sc3[:, qb * 8 + 6:qb * 8 + 8],
                                           agg3[:, qb * 4:qb * 4 + 3:2], -1.0,
                                           sc3[:, qb * 8 + 4:qb * 8 + 6],
                                           ALU.mult, ALU.mult).then_inc(sm.s_v3b, 1)

                def stage_E(it):
                    pb = it % 2
                    v.wait_ge(sm.s_s3b, 2 * (it + 1))
                    v.wait_ge(s_xlv[(2 * it) % 8], 16 * (it // 4 + 1))
                    v.wait_ge(s_xlv[(2 * it + 1) % 8], 16 * (it // 4 + 1))
                    if it >= 2:
                        v.wait_ge((sm.s_sp3b0, sm.s_sp3b1)[pb], 64 * (it // 2))
                    for k in (0, 1):
                        t = 2 * it + k
                        sl = (t % 4) * DM
                        sl8 = (t % 8) * DM
                        ins = v.tensor_add(x1b[:, sl:sl + DM],
                                           ln2b[:, sl:sl + DM], xb3[:, sl8:sl8 + DM])
                    hop(ins)
                    for k in (0, 1):
                        t = 2 * it + k
                        sl = (t % 4) * DM
                        v.bn_stats(bst3b[:, pb * 24 + k * 12:pb * 24 + k * 12 + 6],
                                   x1b[:, sl:sl + 512])
                        ins = v.bn_stats(bst3b[:, pb * 24 + k * 12 + 6:pb * 24 + k * 12 + 12],
                                         x1b[:, sl + 512:sl + 1024])
                    hop(ins)
                    v.bn_aggr(agg3b[:, pb * 4:pb * 4 + 2], bst3b[:, pb * 24:pb * 24 + 12])
                    hop(v.bn_aggr(agg3b[:, pb * 4 + 2:pb * 4 + 4],
                                  bst3b[:, pb * 24 + 12:pb * 24 + 24]))
                    v.tensor_scalar_add(sc3b[:, pb * 8:pb * 8 + 2],
                                        agg3b[:, pb * 4 + 1:pb * 4 + 4:2],
                                        EPS).then_inc(sm.s_v4a, 1)

                def stage_G(it):
                    qb = it % 2
                    v.wait_ge(sm.s_s4a, it + 1)
                    hop(v.reciprocal(sc3b[:, qb * 8 + 4:qb * 8 + 6],
                                     sc3b[:, qb * 8 + 2:qb * 8 + 4]))
                    v.scalar_tensor_tensor(sc3b[:, qb * 8 + 6:qb * 8 + 8],
                                           agg3b[:, qb * 4:qb * 4 + 3:2], -1.0,
                                           sc3b[:, qb * 8 + 4:qb * 8 + 6],
                                           ALU.mult, ALU.mult).then_inc(sm.s_v4b, 1)

                for it in range(NI + 2):
                    if it < NI:
                        stage_A(it)
                    if 1 <= it < NI + 1:
                        stage_C(it - 1)
                        stage_E(it - 1)
                    if 2 <= it < NI + 2:
                        stage_G(it - 2)

            @blk.scalar
            def _(s):
                def stage_B(it):
                    pb = it % 2
                    s.wait_ge(sm.s_v3a, it + 1)
                    s.activation(sc3[:, pb * 8 + 2:pb * 8 + 4],
                                 sc3[:, pb * 8:pb * 8 + 2], AF.Sqrt).then_inc(sm.s_s3a, 1)

                def stage_D(it):
                    qb = it % 2
                    s.wait_ge(sm.s_v3b, it + 1)
                    if it >= 2:
                        s.wait_ge(sm.s_v4a, it - 1)
                    for k in (0, 1):
                        t = 2 * it + k
                        s.activation(ln2b[:, (t % 4) * DM:(t % 4 + 1) * DM],
                                     attnb[:, (t % 4) * DM:(t % 4 + 1) * DM], AF.Identity,
                                     bias=sc3[:, qb * 8 + 6 + k:qb * 8 + 7 + k],
                                     scale=sc3[:, qb * 8 + 4 + k:qb * 8 + 5 + k]).then_inc(sm.s_s3b, 1)

                def stage_F(it):
                    pb = it % 2
                    s.wait_ge(sm.s_v4a, it + 1)
                    s.activation(sc3b[:, pb * 8 + 2:pb * 8 + 4],
                                 sc3b[:, pb * 8:pb * 8 + 2], AF.Sqrt).then_inc(sm.s_s4a, 1)

                def stage_H(it):
                    qb = it % 2
                    s.wait_ge(sm.s_v4b, it + 1)
                    if it >= 2:
                        s.wait_ge((sm.s_sp3b0, sm.s_sp3b1)[it % 2], 64 * (it // 2))
                    for k in (0, 1):
                        t = 2 * it + k
                        s.activation(xn2b[:, (t % 4) * DM:(t % 4 + 1) * DM],
                                     x1b[:, (t % 4) * DM:(t % 4 + 1) * DM], AF.Identity,
                                     bias=sc3b[:, qb * 8 + 6 + k:qb * 8 + 7 + k],
                                     scale=sc3b[:, qb * 8 + 4 + k:qb * 8 + 5 + k]).then_inc(sm.s_s4b, 1)
                        s.wait_ge(sm.s_s4b, t + 1)
                        s.dma_start(x1_d[t * 128:(t + 1) * 128, :],
                                    x1b[:, (t % 4) * DM:(t % 4 + 1) * DM]).then_inc(
                            (sm.s_sp3b0, sm.s_sp3b1)[(t // 2) % 2], 16)
                        s.dma_start(xn2_d[t * 128:(t + 1) * 128, :],
                                    xn2b[:, (t % 4) * DM:(t % 4 + 1) * DM]).then_inc(
                            (sm.s_sp3b0, sm.s_sp3b1)[(t // 2) % 2], 16)

                for it in range(NI + 2):
                    if it < NI:
                        stage_B(it)
                    if 1 <= it < NI + 1:
                        stage_D(it - 1)
                        stage_F(it - 1)
                    if 2 <= it < NI + 2:
                        stage_H(it - 2)

    def _p3b():
        pass

    def _p4():
        wg_sb, wd_sb = hold["wg_sb"], hold["wd_sb"]
        NC4 = 8
        with (
            nc.sbuf_tensor("x2T", [128, 2 * 8 * 512], BF16) as x2T,
            nc.sbuf_tensor("wu_sl", [128, 4 * 8 * 128], BF16) as wu_sl,
            nc.sbuf_tensor("sgbuf", [128, 2 * 512], BF16) as sgbuf,
            nc.sbuf_tensor("hswb", [128, 24 * 512], BF16) as hswb,
            nc.sbuf_tensor("y_sb", [128, 4 * DM], F32) as y_sb,
            nc.sbuf_tensor("x1f", [128, 2 * DM], F32) as x1f,
            nc.sbuf_tensor("ostage", [128, 2 * DM], F32) as ostage,
            nc.sbuf_tensor("bst4", [128, 2 * 12], F32) as bst4,
            nc.sbuf_tensor("agg4", [128, 2 * 2], F32) as agg4,
            nc.sbuf_tensor("sc4", [128, 2 * 5], F32) as sc4,
            nc.psum_tensor("gp", [128, 2 * 512], F32) as gp,
            nc.psum_tensor("upp", [128, 2 * 512], F32) as upp,
            nc.psum_tensor("yp", [128, 2 * 1024], F32) as yp,
            nc.Block() as blk,
        ):
            s_wu = (sm.s_wua, sm.s_wub, sm.s_wuc, sm.s_wud)
            y_tgts = ((yp, 0), (yp, 1024), (gp, 0), (upp, 0))

            @blk.gpsimd
            def _(g):
                def spill(gt):
                    g.wait_ge(sm.s_oady, gt + 1)
                    g.dma_start(out_d[gt * 128:(gt + 1) * 128, :],
                                ostage[:, (gt % 2) * DM:(gt % 2 + 1) * DM]).then_inc(
                        (sm.s_out0, sm.s_out1)[gt % 2], 16)
                g.wait_ge(sm.s_sp3b0, 32 * NT // 2)
                g.wait_ge(sm.s_sp3b1, 32 * NT // 2)
                for c in range(NC4):
                    for hc in range(24):
                        if c > 0 or hc >= 4:
                            g.wait_ge(sm.s_gu, 24 * c + hc - 3)
                        sl = hc % 4
                        g.dma_start(
                            wu_sl[:, sl * 1024:(sl + 1) * 1024].rearrange(
                                "p (dc e) -> p dc e", dc=8),
                            wuT_d[:, hc * 128:(hc + 1) * 128].rearrange(
                                "(dc p) e -> p dc e", p=128)).then_inc(s_wu[sl], 16)
                    if c >= 1:
                        for tt in range(4):
                            spill(4 * (c - 1) + tt)
                for tt in range(4):
                    spill(4 * (NC4 - 1) + tt)
                g.wait_ge(sm.s_out0, 16 * NT // 2)
                g.wait_ge(sm.s_out1, 16 * NT // 2)

            @blk.sync
            def _(sy):
                sy.wait_ge(sm.s_sp3b0, 32 * NT // 2)
                sy.wait_ge(sm.s_sp3b1, 32 * NT // 2)
                def x1loads(c):
                    for tt in range(4):
                        gt = 4 * c + tt
                        if gt >= 2:
                            sy.wait_ge(sm.s_oady, gt - 1)
                        sy.dma_start(x1f[:, (gt % 2) * DM:(gt % 2 + 1) * DM],
                                     x1_d[gt * 128:(gt + 1) * 128, :]).then_inc(
                            (sm.s_x1f0, sm.s_x1f1)[gt % 2], 16)
                for c in range(NC4):
                    if c >= 1:
                        if c >= 2:
                            sy.wait_ge(sm.s_gu, 24 * (c - 1))
                        for dc in range(8):
                            sy.dma_start_transpose(
                                x2T[:, (c % 2) * 8 * 512 + dc * 512:
                                    (c % 2) * 8 * 512 + (dc + 1) * 512],
                                xn2_d[c * 512:(c + 1) * 512,
                                      dc * 128:(dc + 1) * 128]).then_inc(
                                (sm.s_x2t0, sm.s_x2t1)[c % 2], 16)
                        x1loads(c - 1)
                x1loads(NC4 - 1)

            @blk.tensor
            def _(te):
                te.wait_ge(sm.s_wgu, 16 * 6)
                for c in range(NC4):
                    te.wait_ge((sm.s_x2t0, sm.s_x2t1)[c % 2], 16 * 8 * (c // 2 + 1))
                    te.wait_ge(sm.s_yraw, 4 * c)
                    par = c % 2
                    for hc in range(24):
                        ghc = 24 * c + hc
                        if ghc >= 2:
                            te.wait_ge(sm.s_hu, ghc - 1)
                        te.wait_ge(s_wu[hc % 4], 16 * (6 * c + hc // 4 + 1))
                        gsl = gp[:, (ghc % 2) * 512:(ghc % 2 + 1) * 512]
                        usl = upp[:, (ghc % 2) * 512:(ghc % 2 + 1) * 512]

                        def xsl(dc):
                            if c == 0:
                                return hold["x2T0"][:, dc * 512:(dc + 1) * 512]
                            return x2T[:, par * 8 * 512 + dc * 512:
                                       par * 8 * 512 + (dc + 1) * 512]
                        for dc in range(8):
                            mm = te.matmul(gsl,
                                           wg_sb[:, dc * FF + hc * 128:dc * FF + (hc + 1) * 128],
                                           xsl(dc), start=(dc == 0), stop=(dc == 7))
                        mm.then_inc(sm.s_sp30, 1)
                        for dc in range(8):
                            mm = te.matmul(usl,
                                           wu_sl[:, (hc % 4) * 1024 + dc * 128:
                                                 (hc % 4) * 1024 + (dc + 1) * 128],
                                           xsl(dc), start=(dc == 0), stop=(dc == 7))
                        mm.then_inc(sm.s_gu, 1)
                    for hc in range(24):
                        te.wait_ge(sm.s_hu, 24 * c + hc + 1)
                        for tt in (0, 1):
                            tgt, off = y_tgts[tt]
                            for half in (0, 1):
                                mm = te.matmul(
                                    tgt[:, off + half * 512:off + (half + 1) * 512],
                                    hswb[:, hc * 512 + tt * 128:hc * 512 + (tt + 1) * 128],
                                    wd_sb[:, hc * DM + half * 512:hc * DM + (half + 1) * 512],
                                    start=(hc == 0), stop=(hc == 23))
                    te.wait_ge(sm.s_sg, 24 * (c + 1))
                    for hc in range(24):
                        for tt in (2, 3):
                            tgt, off = y_tgts[tt]
                            for half in (0, 1):
                                mm = te.matmul(
                                    tgt[:, off + half * 512:off + (half + 1) * 512],
                                    hswb[:, hc * 512 + tt * 128:hc * 512 + (tt + 1) * 128],
                                    wd_sb[:, hc * DM + half * 512:hc * DM + (half + 1) * 512],
                                    start=(hc == 0), stop=(hc == 23))
                    mm.then_inc(sm.s_dpe, 1)

            @blk.scalar
            def _(s):
                def ln4_acts(c):
                    for tt in range(4):
                        gt = 4 * c + tt
                        p2 = gt % 2
                        s.wait_ge(sm.s_v5a, gt + 1)
                        s.activation(sc4[:, p2 * 5 + 1:p2 * 5 + 2],
                                     sc4[:, p2 * 5:p2 * 5 + 1], AF.Sqrt).then_inc(sm.s_s5a, 1)
                        s.wait_ge(sm.s_v5b, gt + 1)
                        if gt >= 2:
                            s.wait_ge((sm.s_out0, sm.s_out1)[p2], 16 * (gt // 2))
                        s.activation(ostage[:, p2 * DM:(p2 + 1) * DM],
                                     y_sb[:, (gt % 4) * DM:(gt % 4 + 1) * DM], AF.Identity,
                                     bias=sc4[:, p2 * 5 + 4:p2 * 5 + 5],
                                     scale=sc4[:, p2 * 5 + 3:p2 * 5 + 4]).then_inc(sm.s_s5b, 1)
                for c in range(NC4):
                    for hc in range(24):
                        ghc = 24 * c + hc
                        s.wait_ge(sm.s_sp30, ghc + 1)
                        if ghc >= 2:
                            s.wait_ge(sm.s_hu, ghc - 1)
                        s.activation(sgbuf[:, (ghc % 2) * 512:(ghc % 2 + 1) * 512],
                                     gp[:, (ghc % 2) * 512:(ghc % 2 + 1) * 512],
                                     AF.Silu, scale=DQ).then_inc(sm.s_sg, 1)
                    if c >= 1:
                        ln4_acts(c - 1)
                ln4_acts(NC4 - 1)

            @blk.vector
            def _(v):
                hz = [0]

                def hop(instr):
                    instr.then_inc(sm.s_bn4, 1)
                    hz[0] += 1
                    v.wait_ge(sm.s_bn4, hz[0])

                def tt_block(c):
                    for hc in range(24):
                        ghc = 24 * c + hc
                        v.wait_ge(sm.s_sg, ghc + 1)
                        v.wait_ge(sm.s_gu, ghc + 1)
                        if c >= 1:
                            v.wait_ge(sm.s_dpe, c)
                        v.tensor_tensor(hswb[:, hc * 512:(hc + 1) * 512],
                                        upp[:, (ghc % 2) * 512:(ghc % 2 + 1) * 512],
                                        sgbuf[:, (ghc % 2) * 512:(ghc % 2 + 1) * 512],
                                        ALU.mult).then_inc(sm.s_hu, 1)

                def y_copies(c):
                    v.wait_ge(sm.s_dpe, c + 1)
                    for tt in range(4):
                        gt = 4 * c + tt
                        if gt >= 4:
                            v.wait_ge(sm.s_s5b, gt - 3)
                        tgt, off = y_tgts[tt]
                        v.tensor_copy(y_sb[:, (gt % 4) * DM:(gt % 4 + 1) * DM],
                                      tgt[:, off:off + 1024]).then_inc(sm.s_yraw, 1)

                def tail_block(c):
                    for tt in range(4):
                        gt = 4 * c + tt
                        p2 = gt % 2
                        sl = (gt % 4) * DM
                        v.wait_ge(sm.s_yraw, gt + 1)
                        v.bn_stats(bst4[:, p2 * 12:p2 * 12 + 6], y_sb[:, sl:sl + 512])
                        hop(v.bn_stats(bst4[:, p2 * 12 + 6:p2 * 12 + 12],
                                       y_sb[:, sl + 512:sl + 1024]))
                        hop(v.bn_aggr(agg4[:, p2 * 2:p2 * 2 + 2],
                                      bst4[:, p2 * 12:p2 * 12 + 12]))
                        v.tensor_scalar(sc4[:, p2 * 5:p2 * 5 + 1],
                                        agg4[:, p2 * 2 + 1:p2 * 2 + 2],
                                        S2 * S2, EPS, ALU.mult, ALU.add).then_inc(sm.s_v5a, 1)
                        v.wait_ge(sm.s_s5a, gt + 1)
                        hop(v.reciprocal(sc4[:, p2 * 5 + 2:p2 * 5 + 3],
                                         sc4[:, p2 * 5 + 1:p2 * 5 + 2]))
                        v.tensor_scalar_mul(sc4[:, p2 * 5 + 3:p2 * 5 + 4],
                                            sc4[:, p2 * 5 + 2:p2 * 5 + 3], S2)
                        v.scalar_tensor_tensor(sc4[:, p2 * 5 + 4:p2 * 5 + 5],
                                               agg4[:, p2 * 2:p2 * 2 + 1], -S2,
                                               sc4[:, p2 * 5 + 2:p2 * 5 + 3],
                                               ALU.mult, ALU.mult).then_inc(sm.s_v5b, 1)
                        v.wait_ge(sm.s_s5b, gt + 1)
                        v.wait_ge((sm.s_x1f0, sm.s_x1f1)[p2], 16 * (gt // 2 + 1))
                        if gt >= 2:
                            v.wait_ge((sm.s_out0, sm.s_out1)[p2], 16 * (gt // 2))
                        v.tensor_add(ostage[:, p2 * DM:(p2 + 1) * DM],
                                     ostage[:, p2 * DM:(p2 + 1) * DM],
                                     x1f[:, (gt % 2) * DM:(gt % 2 + 1) * DM]).then_inc(sm.s_oady, 1)

                for c in range(NC4):
                    tt_block(c)
                    if c >= 1:
                        tail_block(c - 1)
                    y_copies(c)
                tail_block(NC4 - 1)

    fw = {1: [("s_sp10", 16 * NT // 2), ("s_sp11", 16 * NT // 2)],
          2: [("s_qsp0", 16 * 4 * NCH), ("s_qsp1", 16 * 4 * NCH),
              ("s_ksp0", 16 * 4 * NCH), ("s_ksp1", 16 * 4 * NCH)],
          3: [("s_cs00", 16 * 4 * 8), ("s_cs01", 16 * 4 * 8),
              ("s_cs10", 16 * 4 * 8), ("s_cs11", 16 * 4 * 8)],
          4: [("s_sp3b0", 32 * NT // 2), ("s_sp3b1", 32 * NT // 2), ("s_wgu", 16 * 6)],
          5: [("s_sp3b0", 32 * NT // 2), ("s_sp3b1", 32 * NT // 2), ("s_wgu", 16 * 6)]}

    _p1a()
    if phases >= 2:
        hold["v_sb"] = es_v.enter_context(
            nc.sbuf_tensor("v_sb", [128, 32 * 1040], BF16, side="right"))
        if phases >= 3:
            hold["qt_buf"] = es_v.enter_context(
                nc.sbuf_tensor("qt_buf", [128, 2 * S], BF16, side="right"))
            hold["kt_buf"] = es_v.enter_context(
                nc.sbuf_tensor("kt_buf", [128, 2 * S], BF16, side="right"))
            hold["qstage"] = es_v.enter_context(
                nc.sbuf_tensor("qstage", [128, 2 * 512], BF16, side="right"))
            hold["kstage"] = es_v.enter_context(
                nc.sbuf_tensor("kstage", [128, 2 * 512], BF16, side="right"))
        _p1b()
    es_qkv.close()
    if phases >= 3:
        hold["ctxb"] = es.enter_context(
            nc.sbuf_tensor("p3_ctx", [128, 8 * 1040], BF16))
        _p2()
    es_v.close()
    if phases >= 4:
        hold["wg_sb"] = es_ffn.enter_context(
            nc.sbuf_tensor("wg_sb", [128, 8 * FF], BF16, side="right"))
        hold["wd_sb"] = es_ffn.enter_context(
            nc.sbuf_tensor("wd_sb", [128, 24 * DM], BF16, side="right"))
        hold["x2T0"] = es_ffn.enter_context(
            nc.sbuf_tensor("x2T0", [128, 8 * 512], BF16))
        _p3a()
    if phases >= 5:
        _p3b()
    if phases >= 6:
        _p4()
    if phases < 6:
        with nc.Block() as blk:
            @blk.gpsimd
            def _(g):
                for sname, tot in fw[phases]:
                    g.wait_ge(sems[sname], tot)
    es_ffn.close()
    es.close()
    return nc


def kernel(L0_attn_hidden_states, L0_attn_Wq, L0_attn_bq, L0_attn_Wk, L0_attn_bk,
           L0_attn_Wv, L0_attn_bv, L0_ffn_w_gate, L0_ffn_w_up, L0_ffn_w_down):
    x = np.ascontiguousarray(np.asarray(L0_attn_hidden_states))
    wq = np.ascontiguousarray(np.asarray(L0_attn_Wq).T)
    wk = np.ascontiguousarray(np.asarray(L0_attn_Wk).T)
    wv = np.ascontiguousarray(np.asarray(L0_attn_Wv).T)
    bq = np.ascontiguousarray(np.asarray(L0_attn_bq))
    bk = np.ascontiguousarray(np.asarray(L0_attn_bk))
    bv = np.ascontiguousarray(np.asarray(L0_attn_bv))
    wg = np.ascontiguousarray(np.asarray(L0_ffn_w_gate).T)
    wu = np.ascontiguousarray(np.asarray(L0_ffn_w_up).T)
    wd = np.ascontiguousarray(np.asarray(L0_ffn_w_down).T)

    nc = build_nc()
    in_maps = [{"x_in": x[b], "wqT": wq, "wkT": wk, "wvT": wv,
                "bq": bq, "bk": bk, "bv": bv,
                "wgT": wg, "wuT": wu, "wdT": wd} for b in range(8)]
    res = run_bass_kernel_spmd(nc, in_maps, core_ids=list(range(8)))
    return np.stack([res.results[b]["out"] for b in range(8)], axis=0)


# version 21
# speedup vs baseline: 1.0971x; 1.0081x over previous
import numpy as np
from contextlib import ExitStack
import concourse.bass as bass
import concourse.mybir as mybir
from concourse.bass_utils import run_bass_kernel_spmd

F32 = mybir.dt.float32
BF16 = mybir.dt.bfloat16
I32 = mybir.dt.int32
AF = mybir.ActivationFunctionType
ALU = mybir.AluOpType

S = 4096
DM = 1024
H = 16
NB = 64
FF = 3072
NT = S // 128
NCH = S // 512
QSCALE = 0.125
DQ = float(np.float32(1.0 / 127.0))
S2 = DQ * DQ
EPS = 1e-5


def attends(qb):
    if qb in (0, NB - 1):
        return list(range(NB))
    if qb == 1:
        return [0, 1, 2, NB - 1]
    if qb == NB - 2:
        return [0, NB - 3, NB - 2, NB - 1]
    return sorted({0, qb - 1, qb, qb + 1, NB - 1})


PAIRS = [(0, NB - 1)] + [(2 * t - 1, 2 * t) for t in range(1, NB // 2)]
NP_ = len(PAIRS)


def pair_runs(t):
    if t == 0:
        return [(0, NB)]
    lo, hi = max(1, 2 * t - 2), min(NB - 2, 2 * t + 1)
    return [(0, 1), (lo, hi - lo + 1), (NB - 1, 1)]


def _build_tables():
    seg_off, seg_w = [], []
    off = 0
    for t in range(NP_):
        seg_off.append(off)
        w = sum(n for _, n in pair_runs(t)) * 64
        seg_w.append(w)
        off += w
    arena_cols = off

    memsets = []
    for t in range(NP_):
        jt, jb = PAIRS[t]
        rects = []
        for half, j in ((0, jt), (1, jb)):
            c = 0
            bad = []
            for qs, n in pair_runs(t):
                for k in range(n):
                    if j not in attends(qs + k):
                        bad.append([c + k * 64, c + k * 64 + 64])
                c += n * 64
            merged = []
            for lo, hi in bad:
                if merged and merged[-1][1] == lo:
                    merged[-1][1] = hi
                else:
                    merged.append([lo, hi])
            rects += [(half, lo, hi) for lo, hi in merged]
        memsets.append(rects)

    ctx_lists = []
    total_w = 0
    for C in range(NCH):
        entries = []
        for t in range(NP_):
            c = 0
            for qs, n in pair_runs(t):
                lo, hi = max(qs, 8 * C), min(qs + n, 8 * C + 8)
                if lo < hi:
                    entries.append((t, seg_off[t] + c + (lo - qs) * 64,
                                    (lo - 8 * C) * 64, (hi - lo) * 64))
                c += n * 64
        ctx_lists.append(entries)
        total_w += sum(e[3] for e in entries)
    assert total_w == arena_cols, (total_w, arena_cols)
    return seg_off, seg_w, arena_cols, memsets, ctx_lists


SEG_OFF, SEG_W, ARENA_COLS, MEMSETS, CTX_LISTS = _build_tables()
SGROUPS = 8 + (NP_ - 1)
MS_INC_PER_U = sum(1 for t in range(1, NP_) if MEMSETS[t])


def _v_plan():
    plans = []
    for c in range(NCH):
        groups = []
        pA = (0, 64, 0 if c == 0 else 64, 0 if c == 0 else 4 * c)
        pB = (448, 64, 64 if c == 7 else 0, 0 if c == 7 else 4 * c + 4)
        if c in (0, 7):
            groups += [[pA], [pB]]
        else:
            groups += [[pA, pB]]
        for w in (1, 2, 3):
            groups.append([(w * 128 - 64, 128, 0, 4 * c + w)])
        plans.append(groups)
    return plans


V_PLAN = _v_plan()
V_COPIES = [[sum(2 if p[1] == 128 else 1 for p in g) for g in plan]
            for plan in V_PLAN]
V_COPY_PREFIX = []
_acc = 0
for _cnts in V_COPIES:
    for _cnt in _cnts:
        V_COPY_PREFIX.append(_acc)
        _acc += _cnt
V_COPY_TOTAL = _acc
VG_PER_CHUNK = [len(p) for p in V_PLAN]
VG_PREFIX = [0]
for _n in VG_PER_CHUNK:
    VG_PREFIX.append(VG_PREFIX[-1] + _n)


def build_nc(debug=(), phases=6):
    nc = bass.Bass("TRN2", target_bir_lowering=False)

    def scratch(name, shape, dtype):
        kind = "ExternalOutput" if name in debug else "Internal"
        return nc.dram_tensor(name, shape, dtype, kind=kind).ap()

    x_in = nc.dram_tensor("x_in", [S, DM], F32, kind="ExternalInput").ap()
    wqT = nc.dram_tensor("wqT", [DM, DM], F32, kind="ExternalInput").ap()
    wkT = nc.dram_tensor("wkT", [DM, DM], F32, kind="ExternalInput").ap()
    wvT = nc.dram_tensor("wvT", [DM, DM], F32, kind="ExternalInput").ap()
    bq_d = nc.dram_tensor("bq", [DM], F32, kind="ExternalInput").ap()
    bk_d = nc.dram_tensor("bk", [DM], F32, kind="ExternalInput").ap()
    bv_d = nc.dram_tensor("bv", [DM], F32, kind="ExternalInput").ap()
    wgT_d = nc.dram_tensor("wgT", [DM, FF], I32, kind="ExternalInput").ap()
    wuT_d = nc.dram_tensor("wuT", [DM, FF], I32, kind="ExternalInput").ap()
    wdT_d = nc.dram_tensor("wdT", [FF, DM], I32, kind="ExternalInput").ap()
    out_d = nc.dram_tensor("out", [S, DM], F32, kind="ExternalOutput").ap()

    xn1_d = scratch("xn1_d", [S, DM], BF16)
    qkt_d = scratch("qkt_d", [2, 8, 128, S], BF16)
    ctxT_d = scratch("ctxT_d", [1040, S], BF16)
    ln2_d = scratch("ln2_d", [S, DM], BF16)
    x1_d = scratch("x1_d", [S, DM], F32)
    xn2_d = scratch("xn2_d", [S, DM], BF16)

    es = ExitStack()
    es_qkv = ExitStack()
    es_v = ExitStack()
    es_ffn = ExitStack()
    sems = {}

    def sem(name):
        sems[name] = es.enter_context(nc.semaphore(name))

    for n in ("s_bb s_wq s_bvr s_xld0 s_xld1 s_xld2 s_xld3 s_bn1 s_va s_sq s_vb s_ln1 "
              "s_sp10 s_sp11 "
              "s_trl0 s_trl1 s_qg s_kg s_vg s_qev s_kev s_vev "
              "s_qsp0 s_qsp1 s_ksp0 s_ksp1 "
              "s_ql0 s_ql1 s_spe s_exp s_ms s_cpe s_cev0 s_cev1 "
              "s_cs00 s_cs01 s_cs10 s_cs11 "
              "s_ctl0 s_ctl1 s_ctl2 s_ctl3 s_ctl4 s_ctl5 s_ctl6 s_ctl7 s_bn3 s_v3a s_s3a s_v3b s_s3b s_sp30 s_sp31 "
              "s_xl0 s_xl1 s_xl2 s_xl3 s_xl4 s_xl5 s_xl6 s_xl7 s_bn3b s_v4a s_s4a s_v4b s_s4b s_sp3b0 s_sp3b1 "
              "s_wgu s_wua s_wub s_wuc s_wud s_x2t0 s_x2t1 s_x1f0 s_x1f1 s_gu s_sg s_hu s_dpe s_yraw "
              "s_bn4 s_v5a s_s5a s_v5b s_s5b s_oady s_out0 s_out1").split():
        sem(n)
    sm = type("SM", (), sems)

    bv_rep = es.enter_context(nc.sbuf_tensor("bv_rep", [128, DM], F32, side="right"))
    wq_sb = es_qkv.enter_context(nc.sbuf_tensor("wq_sb", [128, 8 * DM], BF16))
    wk_sb = es_qkv.enter_context(nc.sbuf_tensor("wk_sb", [128, 8 * DM], BF16))
    wv_sb = es_qkv.enter_context(nc.sbuf_tensor("wv_sb", [128, 8 * DM], BF16))
    bqsT = es_qkv.enter_context(nc.sbuf_tensor("bqsT", [128, 8], F32))
    bkT = es_qkv.enter_context(nc.sbuf_tensor("bkT", [128, 8], F32))
    hold = {}

    def _p1a():
        NI = NT // 2
        with (
            nc.sbuf_tensor("xbuf", [128, 8 * DM], F32) as xbuf,
            nc.sbuf_tensor("xn1buf", [128, 4 * DM], BF16) as xn1buf,
            nc.sbuf_tensor("bv_row", [1, DM], BF16) as bv_row,
            nc.sbuf_tensor("ones1", [1, 128], BF16) as ones1,
            nc.sbuf_tensor("bst1", [128, 2 * 24], F32) as bst1,
            nc.sbuf_tensor("agg1", [128, 2 * 4], F32) as agg1,
            nc.sbuf_tensor("sc1", [128, 2 * 8], F32) as sc1,
            nc.psum_tensor("bvp", [128, DM], F32) as bvp,
            nc.Block() as blk,
        ):
            s_xl = (sm.s_xld0, sm.s_xld1, sm.s_xld2, sm.s_xld3)

            @blk.gpsimd
            def _(g):
                g.dma_start(bv_row[:], bv_d[None, :]).then_inc(sm.s_bb, 16)
                for u in range(8):
                    g.dma_start(bqsT[:, u:u + 1],
                                bq_d[u * 128:(u + 1) * 128][:, None]).then_inc(sm.s_bb, 16)
                for u in range(8):
                    g.dma_start(bkT[:, u:u + 1],
                                bk_d[u * 128:(u + 1) * 128][:, None]).then_inc(sm.s_bb, 16)
                for w_d, w_sb in ((wqT, wq_sb), (wkT, wk_sb), (wvT, wv_sb)):
                    g.dma_start(w_sb[:].rearrange("p (dc c) -> p dc c", dc=8),
                                w_d.rearrange("(dc p) c -> p dc c", p=128)).then_inc(sm.s_wq, 16)

            @blk.sync
            def _(sy):
                for it in range(NI):
                    if it >= 4:
                        sy.wait_ge(sm.s_ln1, 2 * it - 6)
                    for k in (0, 1):
                        t = 2 * it + k
                        sy.dma_start(xbuf[:, (t % 8) * DM:(t % 8 + 1) * DM],
                                     x_in[t * 128:(t + 1) * 128, :]).then_inc(
                            s_xl[it % 4], 16)

            @blk.tensor
            def _(te):
                te.wait_ge(sm.s_bb, 272)
                te.wait_ge(sm.s_bvr, 1)
                te.matmul(bvp[:, 0:512], ones1[:], bv_row[:, 0:512], start=True, stop=True)
                te.matmul(bvp[:, 512:1024], ones1[:], bv_row[:, 512:1024],
                          start=True, stop=True).then_inc(sm.s_bvr, 1)

            @blk.vector
            def _(v):
                hz = [0]

                def hop(instr):
                    instr.then_inc(sm.s_bn1, 1)
                    hz[0] += 1
                    v.wait_ge(sm.s_bn1, hz[0])
                v.memset(ones1[:], 1.0).then_inc(sm.s_bvr, 1)
                v.wait_ge(sm.s_bvr, 2)
                v.tensor_copy(bv_rep[:], bvp[:])
                v.wait_ge(sm.s_bb, 272)
                v.tensor_scalar_mul(bqsT[:], bqsT[:], QSCALE)
                for it in range(NI + 1):
                    pb = it % 2
                    if it < NI:
                        v.wait_ge(s_xl[it % 4], 32 * (it // 4 + 1))
                        for k in (0, 1):
                            t = 2 * it + k
                            sl = (t % 8) * DM
                            v.bn_stats(bst1[:, pb * 24 + k * 12:pb * 24 + k * 12 + 6],
                                       xbuf[:, sl:sl + 512])
                            ins = v.bn_stats(bst1[:, pb * 24 + k * 12 + 6:pb * 24 + k * 12 + 12],
                                             xbuf[:, sl + 512:sl + 1024])
                        hop(ins)
                        v.bn_aggr(agg1[:, pb * 4:pb * 4 + 2], bst1[:, pb * 24:pb * 24 + 12])
                        hop(v.bn_aggr(agg1[:, pb * 4 + 2:pb * 4 + 4],
                                      bst1[:, pb * 24 + 12:pb * 24 + 24]))
                        v.tensor_scalar_add(
                            sc1[:, pb * 8:pb * 8 + 2],
                            agg1[:, pb * 4 + 1:pb * 4 + 4:2],
                            EPS).then_inc(sm.s_va, 1)
                    if it >= 1:
                        qb = (it - 1) % 2
                        v.wait_ge(sm.s_sq, it)
                        hop(v.reciprocal(sc1[:, qb * 8 + 4:qb * 8 + 6],
                                         sc1[:, qb * 8 + 2:qb * 8 + 4]))
                        v.scalar_tensor_tensor(
                            sc1[:, qb * 8 + 6:qb * 8 + 8],
                            agg1[:, qb * 4:qb * 4 + 3:2], -1.0,
                            sc1[:, qb * 8 + 4:qb * 8 + 6],
                            ALU.mult, ALU.mult).then_inc(sm.s_vb, 1)

            @blk.scalar
            def _(s):
                for it in range(NI + 1):
                    pb = it % 2
                    if it < NI:
                        s.wait_ge(sm.s_va, it + 1)
                        s.activation(sc1[:, pb * 8 + 2:pb * 8 + 4],
                                     sc1[:, pb * 8:pb * 8 + 2], AF.Sqrt).then_inc(sm.s_sq, 1)
                    if it >= 1:
                        qb = (it - 1) % 2
                        s.wait_ge(sm.s_vb, it)
                        if it - 1 >= 2:
                            s.wait_ge((sm.s_sp10, sm.s_sp11)[qb], 32 * ((it - 1) // 2))
                        for k in (0, 1):
                            t = 2 * (it - 1) + k
                            s.activation(xn1buf[:, (t % 4) * DM:(t % 4 + 1) * DM],
                                         xbuf[:, (t % 8) * DM:(t % 8 + 1) * DM], AF.Identity,
                                         bias=sc1[:, qb * 8 + 6 + k:qb * 8 + 7 + k],
                                         scale=sc1[:, qb * 8 + 4 + k:qb * 8 + 5 + k]).then_inc(sm.s_ln1, 1)
                            s.wait_ge(sm.s_ln1, t + 1)
                            s.dma_start(xn1_d[t * 128:(t + 1) * 128, :],
                                        xn1buf[:, (t % 4) * DM:(t % 4 + 1) * DM]).then_inc(
                                (sm.s_sp10, sm.s_sp11)[(t // 2) % 2], 16)

    def _p1b():
        v_sb = hold["v_sb"]
        qstage = hold.get("qstage")
        kstage = hold.get("kstage")
        if qstage is None:
            qstage = es_v.enter_context(
                nc.sbuf_tensor("qstage", [128, 2 * 512], BF16, side="right"))
            kstage = es_v.enter_context(
                nc.sbuf_tensor("kstage", [128, 2 * 512], BF16, side="right"))
        with (
            nc.sbuf_tensor("xn1T", [128, 2 * 8 * 512], BF16) as xn1T,
            nc.psum_tensor("qp", [128, 2 * 512], F32) as qp,
            nc.psum_tensor("kp", [128, 2 * 512], F32) as kp,
            nc.psum_tensor("vp", [128, 2 * 1024], F32) as vp,
            nc.Block() as blk,
        ):
            def xn1T_sl(par, dc, lo=0, w=512):
                base = par * 8 * 512 + dc * 512
                return xn1T[:, base + lo: base + lo + w]

            @blk.vector
            def _(v):
                v.memset(v_sb[:, 64:32 * 1040:65], 1.0).then_inc(sm.s_vev, 1)
                gv = 0
                for c in range(NCH):
                    for u in range(8):
                        gq = c * 8 + u
                        v.wait_ge(sm.s_qg, gq + 1)
                        if gq >= 2:
                            v.wait_ge((sm.s_qsp0, sm.s_qsp1)[gq % 2], 16 * (gq // 2))
                        par = gq % 2
                        v.tensor_scalar(qstage[:, par * 512:(par + 1) * 512],
                                        qp[:, par * 512:(par + 1) * 512],
                                        QSCALE, bqsT[:, u:u + 1],
                                        ALU.mult, ALU.add).then_inc(sm.s_qev, 1)
                    for grp in V_PLAN[c]:
                        v.wait_ge(sm.s_vg, gv + 1)
                        par = gv % 2
                        psl = vp[:, par * 1024:(par + 1) * 1024]
                        for (lhs_lo, lhs_w, plo, slot) in grp:
                            rlos = (0, 64) if lhs_w == 128 else (plo,)
                            for rlo in rlos:
                                dst = v_sb[rlo:rlo + 64,
                                           slot * 1040:slot * 1040 + 1040]
                                dst = dst.rearrange("p (h e) -> p h e", e=65)[:, :, 0:64]
                                v.tensor_copy(dst, psl[rlo:rlo + 64, :].rearrange(
                                    "p (h e) -> p h e", e=64)).then_inc(sm.s_vev, 1)
                        gv += 1

            @blk.sync
            def _(sy):
                for c in range(NCH):
                    sy.wait_ge(sm.s_sp10, 32 * (c + 1))
                    sy.wait_ge(sm.s_sp11, 32 * (c + 1))
                    if c >= 2:
                        sy.wait_ge(sm.s_vg, VG_PREFIX[c - 1])
                    for dc in range(8):
                        sy.dma_start_transpose(
                            xn1T_sl(c % 2, dc),
                            xn1_d[c * 512:(c + 1) * 512, dc * 128:(dc + 1) * 128]
                        ).then_inc((sm.s_trl0, sm.s_trl1)[c % 2], 16)
                if "qt_buf" in hold:
                    for u in (0, 1):
                        cnt = len([g for g in range(64)
                                   if g % 2 == u % 2 and g <= 56 + u])
                        sy.wait_ge((sm.s_qsp0, sm.s_qsp1)[u % 2], 16 * cnt)
                        sy.wait_ge((sm.s_ksp0, sm.s_ksp1)[u % 2], 16 * cnt)
                        sy.dma_start(hold["qt_buf"][:, (u % 2) * S:(u % 2 + 1) * S],
                                     qkt_d[0, u, :, :]).then_inc(
                            (sm.s_ql0, sm.s_ql1)[u % 2], 16)
                        sy.dma_start(hold["kt_buf"][:, (u % 2) * S:(u % 2 + 1) * S],
                                     qkt_d[1, u, :, :]).then_inc(
                            (sm.s_ql0, sm.s_ql1)[u % 2], 16)

            @blk.tensor
            def _(te):
                te.wait_ge(sm.s_wq, 48)
                gv = 0
                for c in range(NCH):
                    te.wait_ge((sm.s_trl0, sm.s_trl1)[c % 2], 16 * 8 * (c // 2 + 1))
                    par = c % 2
                    for u in range(8):
                        gq = c * 8 + u
                        if gq >= 2:
                            te.wait_ge(sm.s_qev, gq - 1)
                            te.wait_ge(sm.s_kev, gq - 1)
                        qsl = qp[:, (gq % 2) * 512:(gq % 2 + 1) * 512]
                        ksl = kp[:, (gq % 2) * 512:(gq % 2 + 1) * 512]
                        for dc in range(8):
                            mm = te.matmul(qsl,
                                           wq_sb[:, dc * DM + u * 128:dc * DM + (u + 1) * 128],
                                           xn1T_sl(par, dc), start=(dc == 0), stop=(dc == 7))
                        mm.then_inc(sm.s_qg, 1)
                        for dc in range(8):
                            mm = te.matmul(ksl,
                                           wk_sb[:, dc * DM + u * 128:dc * DM + (u + 1) * 128],
                                           xn1T_sl(par, dc), start=(dc == 0), stop=(dc == 7))
                        mm.then_inc(sm.s_kg, 1)
                    for grp in V_PLAN[c]:
                        if gv >= 2:
                            te.wait_ge(sm.s_vev, 1 + V_COPY_PREFIX[gv - 1])
                        vsl = vp[:, (gv % 2) * 1024:(gv % 2 + 1) * 1024]
                        for half in (0, 1):
                            for (lhs_lo, lhs_w, plo, slot) in grp:
                                for dc in range(8):
                                    mm = te.matmul(
                                        vsl[plo:plo + lhs_w, half * 512:(half + 1) * 512],
                                        xn1T_sl(par, dc, lhs_lo, lhs_w),
                                        wv_sb[:, dc * DM + half * 512:dc * DM + (half + 1) * 512],
                                        start=(dc == 0), stop=(dc == 7),
                                        tile_position=(0, plo))
                        mm.then_inc(sm.s_vg, 1)
                        gv += 1

            @blk.scalar
            def _(s):
                for gq in range(8 * NCH):
                    c, u = gq // 8, gq % 8
                    s.wait_ge(sm.s_kg, gq + 1)
                    if gq >= 2:
                        s.wait_ge((sm.s_ksp0, sm.s_ksp1)[gq % 2], 16 * (gq // 2))
                    par = gq % 2
                    s.activation(kstage[:, par * 512:(par + 1) * 512],
                                 kp[:, par * 512:(par + 1) * 512], AF.Identity,
                                 bias=bkT[:, u:u + 1], scale=1.0).then_inc(sm.s_kev, 1)

            @blk.gpsimd
            def _(g):
                for gq in range(8 * NCH):
                    c, u = gq // 8, gq % 8
                    par = gq % 2
                    g.wait_ge(sm.s_qev, gq + 1)
                    g.dma_start(qkt_d[0, u, :, c * 512:(c + 1) * 512],
                                qstage[:, par * 512:(par + 1) * 512]).then_inc(
                        (sm.s_qsp0, sm.s_qsp1)[par], 16)
                    g.wait_ge(sm.s_kev, gq + 1)
                    g.dma_start(qkt_d[1, u, :, c * 512:(c + 1) * 512],
                                kstage[:, par * 512:(par + 1) * 512]).then_inc(
                        (sm.s_ksp0, sm.s_ksp1)[par], 16)

    def _p2():
        v_sb = hold["v_sb"]
        A = ARENA_COLS
        ctxb = hold["ctxb"]
        qt_buf, kt_buf = hold["qt_buf"], hold["kt_buf"]
        s_ct = (sm.s_ctl0, sm.s_ctl1, sm.s_ctl2, sm.s_ctl3,
                sm.s_ctl4, sm.s_ctl5, sm.s_ctl6, sm.s_ctl7)
        with (
            nc.sbuf_tensor("arenaB", [128, 2 * A], BF16) as arenaB,
            nc.sbuf_tensor("cstage0", [65, 2 * 512], BF16) as cstage0,
            nc.sbuf_tensor("cstage1", [65, 2 * 512], BF16) as cstage1,
            nc.psum_tensor("pp8", [128, 8 * 512], F32) as pp8,
            nc.Block() as blk,
        ):
            def s_cols(t, gi):
                if t == 0:
                    return 512 * gi, 512
                return SEG_OFF[t], SEG_W[t]

            @blk.sync
            def _(sy):
                for u in range(2, 8):
                    sy.wait_ge(sm.s_qsp0, 16 * 4 * NCH)
                    sy.wait_ge(sm.s_qsp1, 16 * 4 * NCH)
                    sy.wait_ge(sm.s_ksp0, 16 * 4 * NCH)
                    sy.wait_ge(sm.s_ksp1, 16 * 4 * NCH)
                    sy.wait_ge(sm.s_spe, SGROUPS * (u - 1))
                    sy.dma_start(qt_buf[:, (u % 2) * S:(u % 2 + 1) * S],
                                 qkt_d[0, u, :, :]).then_inc((sm.s_ql0, sm.s_ql1)[u % 2], 16)
                    sy.dma_start(kt_buf[:, (u % 2) * S:(u % 2 + 1) * S],
                                 qkt_d[1, u, :, :]).then_inc((sm.s_ql0, sm.s_ql1)[u % 2], 16)
                for t in range(8):
                    Ct = t // 4
                    for h01, sems in ((0, (sm.s_cs00, sm.s_cs01)),
                                      (1, (sm.s_cs10, sm.s_cs11))):
                        for par in (0, 1):
                            ncnt = (Ct - par) // 2 + 1 if Ct >= par else 0
                            if 28 + ncnt > 0:
                                sy.wait_ge(sems[par], 16 * (28 + ncnt))
                    sy.dma_start_transpose(
                        ctxb[:, (t % 8) * 1040:(t % 8 + 1) * 1040],
                        ctxT_d[:, t * 128:(t + 1) * 128]).then_inc(s_ct[t % 8], 16)

            @blk.tensor
            def _(te):
                last_use = {}
                for u in range(8):
                    up = (u % 2) * S
                    te.wait_ge((sm.s_ql0, sm.s_ql1)[u % 2], 32 * (u // 2 + 1))
                    if u >= 1:
                        te.wait_ge(sm.s_cev0, 8 * u)
                        te.wait_ge(sm.s_cev1, 8 * u)
                    for gi in range(SGROUPS):
                        gg = u * SGROUPS + gi
                        par = gi % 4
                        if par in last_use:
                            te.wait_ge(sm.s_exp, last_use[par] + 1)
                        last_use[par] = gg
                        t = 0 if gi < 8 else gi - 7
                        jt, jb = PAIRS[t]
                        runs = [(8 * gi, 8)] if t == 0 else pair_runs(t)
                        col = 0
                        last = None
                        for (qs, n) in runs:
                            w = n * 64
                            for (rows, hoff) in ((0, 0), (64, 512)):
                                rhs = qt_buf[rows:rows + 64,
                                             up + qs * 64: up + qs * 64 + w]
                                for (tcol, j) in ((0, jt), (64, jb)):
                                    last = te.matmul(
                                        pp8[tcol:tcol + 64,
                                            par * 1024 + hoff + col: par * 1024 + hoff + col + w],
                                        kt_buf[rows:rows + 64,
                                               up + j * 64: up + j * 64 + 64],
                                        rhs, start=True, stop=True,
                                        tile_position=(rows, tcol))
                            col += w
                        last.then_inc(sm.s_spe, 1)
                    te.wait_ge(sm.s_exp, SGROUPS * (u + 1))
                    te.wait_ge(sm.s_ms, MS_INC_PER_U * (u + 1))
                    for C in range(NCH):
                        for h01 in (0, 1):
                            if C >= 2:
                                te.wait_ge(sm.s_cev0, 8 * u + C - 1)
                                te.wait_ge(sm.s_cev1, 8 * u + C - 1)
                            elif u >= 1:
                                te.wait_ge(sm.s_cev0, 8 * u)
                                te.wait_ge(sm.s_cev1, 8 * u)
                            pcoff = (h01 * 2 + (C % 2)) * 512
                            h = 2 * u + h01
                            n_e = len(CTX_LISTS[C])
                            for ei, (t, acol, pcol, w) in enumerate(CTX_LISTS[C]):
                                mm = te.matmul(
                                    pp8[0:65, pcoff + pcol:pcoff + pcol + w],
                                    v_sb[:, t * 1040 + h * 65: t * 1040 + h * 65 + 65],
                                    arenaB[:, h01 * A + acol:h01 * A + acol + w],
                                    start=(ei == 0), stop=(ei == n_e - 1))
                            mm.then_inc(sm.s_cpe, 1)

            @blk.scalar
            def _(s):
                for u in range(8):
                    for gi in range(SGROUPS):
                        gg = u * SGROUPS + gi
                        par = gi % 4
                        t = 0 if gi < 8 else gi - 7
                        acol, w = s_cols(t, gi)
                        s.wait_ge(sm.s_spe, gg + 1)
                        src_ap = pp8[:, par * 1024:(par + 1) * 1024].rearrange(
                            "p (b c) -> p b c", b=2)[:, :, 0:w]
                        dst_ap = arenaB[:].rearrange("p (b c) -> p b c", b=2)[
                            :, :, acol:acol + w]
                        s.activation(dst_ap, src_ap, AF.Exp).then_inc(sm.s_exp, 1)

            @blk.vector
            def _(v):
                for u in range(8):
                    for gi in range(SGROUPS):
                        gg = u * SGROUPS + gi
                        t = 0 if gi < 8 else gi - 7
                        if t == 0 or not MEMSETS[t]:
                            continue
                        v.wait_ge(sm.s_exp, gg + 1)
                        last = None
                        for (half, lo, hi) in MEMSETS[t]:
                            for h01 in (0, 1):
                                last = v.memset(
                                    arenaB[half * 64:half * 64 + 64,
                                           h01 * A + SEG_OFF[t] + lo:
                                           h01 * A + SEG_OFF[t] + hi], 0.0)
                        last.then_inc(sm.s_ms, 1)
                    for C in range(NCH):
                        v.wait_ge(sm.s_cpe, u * 16 + C * 2 + 1)
                        if 4 * u + C // 2 >= 1:
                            v.wait_ge((sm.s_cs00, sm.s_cs01)[C % 2], 16 * (4 * u + C // 2))
                        v.tensor_copy(cstage0[:, (C % 2) * 512:(C % 2 + 1) * 512],
                                      pp8[0:65, (C % 2) * 512:(C % 2 + 1) * 512]).then_inc(sm.s_cev0, 1)
                        v.wait_ge(sm.s_cpe, u * 16 + C * 2 + 2)
                        if 4 * u + C // 2 >= 1:
                            v.wait_ge((sm.s_cs10, sm.s_cs11)[C % 2], 16 * (4 * u + C // 2))
                        v.tensor_copy(cstage1[:, (C % 2) * 512:(C % 2 + 1) * 512],
                                      pp8[0:65, 1024 + (C % 2) * 512:1024 + (C % 2 + 1) * 512]).then_inc(sm.s_cev1, 1)

            @blk.gpsimd
            def _(g):
                for u in range(8):
                    for C in range(NCH):
                        g.wait_ge(sm.s_cev0, 8 * u + C + 1)
                        g.dma_start(ctxT_d[(2 * u) * 65:(2 * u) * 65 + 65,
                                           C * 512:(C + 1) * 512],
                                    cstage0[:, (C % 2) * 512:(C % 2 + 1) * 512]).then_inc(
                            (sm.s_cs00, sm.s_cs01)[C % 2], 16)
                        g.wait_ge(sm.s_cev1, 8 * u + C + 1)
                        g.dma_start(ctxT_d[(2 * u + 1) * 65:(2 * u + 1) * 65 + 65,
                                           C * 512:(C + 1) * 512],
                                    cstage1[:, (C % 2) * 512:(C % 2 + 1) * 512]).then_inc(
                            (sm.s_cs10, sm.s_cs11)[C % 2], 16)

    def _p3a():
        wg_sb, wd_sb = hold["wg_sb"], hold["wd_sb"]
        NI = NT // 2
        ctxb = hold["ctxb"]
        with (
            nc.sbuf_tensor("p3_x", [128, 8 * DM], F32) as xb3,
            nc.sbuf_tensor("p3_attn", [128, 4 * DM], F32) as attnb,
            nc.sbuf_tensor("p3_ln2", [128, 4 * DM], BF16) as ln2b,
            nc.sbuf_tensor("p3_x1", [128, 4 * DM], F32) as x1b,
            nc.sbuf_tensor("p3_xn2", [128, 4 * DM], BF16) as xn2b,
            nc.sbuf_tensor("p3_zg", [128, 2 * 32], F32) as zgb,
            nc.sbuf_tensor("p3_rz", [128, 2 * 32], F32) as rzb,
            nc.sbuf_tensor("bst3", [128, 2 * 24], F32) as bst3,
            nc.sbuf_tensor("agg3", [128, 2 * 4], F32) as agg3,
            nc.sbuf_tensor("sc3", [128, 2 * 8], F32) as sc3,
            nc.sbuf_tensor("bst3b", [128, 2 * 24], F32) as bst3b,
            nc.sbuf_tensor("agg3b", [128, 2 * 4], F32) as agg3b,
            nc.sbuf_tensor("sc3b", [128, 2 * 8], F32) as sc3b,
            nc.Block() as blk,
        ):
            s_ct = (sm.s_ctl0, sm.s_ctl1, sm.s_ctl2, sm.s_ctl3,
                    sm.s_ctl4, sm.s_ctl5, sm.s_ctl6, sm.s_ctl7)
            s_xlv = (sm.s_xl0, sm.s_xl1, sm.s_xl2, sm.s_xl3,
                     sm.s_xl4, sm.s_xl5, sm.s_xl6, sm.s_xl7)

            @blk.gpsimd
            def _(g):
                for pp in (sm.s_cs00, sm.s_cs01, sm.s_cs10, sm.s_cs11):
                    g.wait_ge(pp, 16 * 4 * 8)
                for (w_d, w_sb) in ((wgT_d, wg_sb),):
                    for half in (0, 1):
                        srcw = w_d[:, half * (FF // 2):(half + 1) * (FF // 2)]
                        dst = w_sb[:].rearrange("p (dc c) -> p dc c", dc=8)[
                            :, :, half * (FF // 2):(half + 1) * (FF // 2)]
                        g.dma_start(dst,
                                    srcw.rearrange("(dc p) c -> p dc c", p=128)).then_inc(sm.s_wgu, 16)
                for qr in range(4):
                    srcw = wdT_d[qr * 768:(qr + 1) * 768, :]
                    g.dma_start(
                        wd_sb[:, qr * 6 * DM:(qr + 1) * 6 * DM].rearrange(
                            "p (hc c) -> p hc c", hc=6),
                        srcw.rearrange("(hc p) c -> p hc c", p=128)).then_inc(sm.s_wgu, 16)

            @blk.sync
            def _(sy):
                for pp in (sm.s_cs00, sm.s_cs01, sm.s_cs10, sm.s_cs11):
                    sy.wait_ge(pp, 16 * 4 * 8)
                for t in range(NT):
                    if t >= 8:
                        sy.wait_ge(sm.s_v3a, t // 2 - 3)
                        sy.wait_ge(sm.s_v4a, t // 2 - 3)
                        sy.dma_start_transpose(
                            ctxb[:, (t % 8) * 1040:(t % 8 + 1) * 1040],
                            ctxT_d[:, t * 128:(t + 1) * 128]).then_inc(s_ct[t % 8], 16)
                    sy.dma_start(xb3[:, (t % 8) * DM:(t % 8 + 1) * DM],
                                 x_in[t * 128:(t + 1) * 128, :]).then_inc(s_xlv[t % 8], 16)
                sy.wait_ge(sm.s_sp3b0, 64)
                sy.wait_ge(sm.s_sp3b1, 64)
                for dc in range(8):
                    sy.dma_start_transpose(
                        hold["x2T0"][:, dc * 512:(dc + 1) * 512],
                        xn2_d[0:512, dc * 128:(dc + 1) * 128]).then_inc(sm.s_x2t0, 16)

            @blk.vector
            def _(v):
                hz = [0]

                def hop(instr):
                    instr.then_inc(sm.s_bn3, 1)
                    hz[0] += 1
                    v.wait_ge(sm.s_bn3, hz[0])

                def stage_A(it):
                    pb = it % 2
                    v.wait_ge(s_ct[(2 * it) % 8], 16 * (it // 4 + 1))
                    v.wait_ge(s_ct[(2 * it + 1) % 8], 16 * (it // 4 + 1))
                    if it >= 2:
                        v.wait_ge(sm.s_s3b, 2 * it - 2)
                    for k in (0, 1):
                        t = 2 * it + k
                        cb = ctxb[:, (t % 8) * 1040:(t % 8 + 1) * 1040]
                        ins = v.tensor_copy(zgb[:, pb * 32 + k * 16:pb * 32 + (k + 1) * 16],
                                            cb[:, 64:1040:65])
                    hop(ins)
                    hop(v.reciprocal(rzb[:, pb * 32:pb * 32 + 32],
                                     zgb[:, pb * 32:pb * 32 + 32]))
                    for k in (0, 1):
                        t = 2 * it + k
                        cb = ctxb[:, (t % 8) * 1040:(t % 8 + 1) * 1040]
                        cb3 = cb.rearrange("p (h e) -> p h e", e=65)[:, :, 0:64]
                        rz3 = rzb[:, pb * 32 + k * 16:pb * 32 + (k + 1) * 16].rearrange(
                            "p (h o) -> p h o", o=1).to_broadcast([128, H, 64])
                        at = attnb[:, (t % 4) * DM:(t % 4 + 1) * DM]
                        hop(v.tensor_tensor(at.rearrange("p (h e) -> p h e", e=64),
                                            cb3, rz3, ALU.mult))
                        ins = v.tensor_add(at, at, bv_rep[:])
                    hop(ins)
                    for k in (0, 1):
                        t = 2 * it + k
                        sl = (t % 4) * DM
                        v.bn_stats(bst3[:, pb * 24 + k * 12:pb * 24 + k * 12 + 6],
                                   attnb[:, sl:sl + 512])
                        ins = v.bn_stats(bst3[:, pb * 24 + k * 12 + 6:pb * 24 + k * 12 + 12],
                                         attnb[:, sl + 512:sl + 1024])
                    hop(ins)
                    v.bn_aggr(agg3[:, pb * 4:pb * 4 + 2], bst3[:, pb * 24:pb * 24 + 12])
                    hop(v.bn_aggr(agg3[:, pb * 4 + 2:pb * 4 + 4],
                                  bst3[:, pb * 24 + 12:pb * 24 + 24]))
                    v.tensor_scalar_add(sc3[:, pb * 8:pb * 8 + 2],
                                        agg3[:, pb * 4 + 1:pb * 4 + 4:2],
                                        EPS).then_inc(sm.s_v3a, 1)

                def stage_C(it):
                    qb = it % 2
                    v.wait_ge(sm.s_s3a, it + 1)
                    hop(v.reciprocal(sc3[:, qb * 8 + 4:qb * 8 + 6],
                                     sc3[:, qb * 8 + 2:qb * 8 + 4]))
                    v.scalar_tensor_tensor(sc3[:, qb * 8 + 6:qb * 8 + 8],
                                           agg3[:, qb * 4:qb * 4 + 3:2], -1.0,
                                           sc3[:, qb * 8 + 4:qb * 8 + 6],
                                           ALU.mult, ALU.mult).then_inc(sm.s_v3b, 1)

                def stage_E(it):
                    pb = it % 2
                    v.wait_ge(sm.s_s3b, 2 * (it + 1))
                    v.wait_ge(s_xlv[(2 * it) % 8], 16 * (it // 4 + 1))
                    v.wait_ge(s_xlv[(2 * it + 1) % 8], 16 * (it // 4 + 1))
                    if it >= 2:
                        v.wait_ge((sm.s_sp3b0, sm.s_sp3b1)[pb], 64 * (it // 2))
                    for k in (0, 1):
                        t = 2 * it + k
                        sl = (t % 4) * DM
                        sl8 = (t % 8) * DM
                        ins = v.tensor_add(x1b[:, sl:sl + DM],
                                           ln2b[:, sl:sl + DM], xb3[:, sl8:sl8 + DM])
                    hop(ins)
                    for k in (0, 1):
                        t = 2 * it + k
                        sl = (t % 4) * DM
                        v.bn_stats(bst3b[:, pb * 24 + k * 12:pb * 24 + k * 12 + 6],
                                   x1b[:, sl:sl + 512])
                        ins = v.bn_stats(bst3b[:, pb * 24 + k * 12 + 6:pb * 24 + k * 12 + 12],
                                         x1b[:, sl + 512:sl + 1024])
                    hop(ins)
                    v.bn_aggr(agg3b[:, pb * 4:pb * 4 + 2], bst3b[:, pb * 24:pb * 24 + 12])
                    hop(v.bn_aggr(agg3b[:, pb * 4 + 2:pb * 4 + 4],
                                  bst3b[:, pb * 24 + 12:pb * 24 + 24]))
                    v.tensor_scalar_add(sc3b[:, pb * 8:pb * 8 + 2],
                                        agg3b[:, pb * 4 + 1:pb * 4 + 4:2],
                                        EPS).then_inc(sm.s_v4a, 1)

                def stage_G(it):
                    qb = it % 2
                    v.wait_ge(sm.s_s4a, it + 1)
                    hop(v.reciprocal(sc3b[:, qb * 8 + 4:qb * 8 + 6],
                                     sc3b[:, qb * 8 + 2:qb * 8 + 4]))
                    v.scalar_tensor_tensor(sc3b[:, qb * 8 + 6:qb * 8 + 8],
                                           agg3b[:, qb * 4:qb * 4 + 3:2], -1.0,
                                           sc3b[:, qb * 8 + 4:qb * 8 + 6],
                                           ALU.mult, ALU.mult).then_inc(sm.s_v4b, 1)

                for it in range(NI + 2):
                    if it < NI:
                        stage_A(it)
                    if 1 <= it < NI + 1:
                        stage_C(it - 1)
                        stage_E(it - 1)
                    if 2 <= it < NI + 2:
                        stage_G(it - 2)

            @blk.scalar
            def _(s):
                def stage_B(it):
                    pb = it % 2
                    s.wait_ge(sm.s_v3a, it + 1)
                    s.activation(sc3[:, pb * 8 + 2:pb * 8 + 4],
                                 sc3[:, pb * 8:pb * 8 + 2], AF.Sqrt).then_inc(sm.s_s3a, 1)

                def stage_D(it):
                    qb = it % 2
                    s.wait_ge(sm.s_v3b, it + 1)
                    if it >= 2:
                        s.wait_ge(sm.s_v4a, it - 1)
                    for k in (0, 1):
                        t = 2 * it + k
                        s.activation(ln2b[:, (t % 4) * DM:(t % 4 + 1) * DM],
                                     attnb[:, (t % 4) * DM:(t % 4 + 1) * DM], AF.Identity,
                                     bias=sc3[:, qb * 8 + 6 + k:qb * 8 + 7 + k],
                                     scale=sc3[:, qb * 8 + 4 + k:qb * 8 + 5 + k]).then_inc(sm.s_s3b, 1)

                def stage_F(it):
                    pb = it % 2
                    s.wait_ge(sm.s_v4a, it + 1)
                    s.activation(sc3b[:, pb * 8 + 2:pb * 8 + 4],
                                 sc3b[:, pb * 8:pb * 8 + 2], AF.Sqrt).then_inc(sm.s_s4a, 1)

                def stage_H(it):
                    qb = it % 2
                    s.wait_ge(sm.s_v4b, it + 1)
                    if it >= 2:
                        s.wait_ge((sm.s_sp3b0, sm.s_sp3b1)[it % 2], 64 * (it // 2))
                    for k in (0, 1):
                        t = 2 * it + k
                        s.activation(xn2b[:, (t % 4) * DM:(t % 4 + 1) * DM],
                                     x1b[:, (t % 4) * DM:(t % 4 + 1) * DM], AF.Identity,
                                     bias=sc3b[:, qb * 8 + 6 + k:qb * 8 + 7 + k],
                                     scale=sc3b[:, qb * 8 + 4 + k:qb * 8 + 5 + k]).then_inc(sm.s_s4b, 1)
                        s.wait_ge(sm.s_s4b, t + 1)
                        s.dma_start(x1_d[t * 128:(t + 1) * 128, :],
                                    x1b[:, (t % 4) * DM:(t % 4 + 1) * DM]).then_inc(
                            (sm.s_sp3b0, sm.s_sp3b1)[(t // 2) % 2], 16)
                        s.dma_start(xn2_d[t * 128:(t + 1) * 128, :],
                                    xn2b[:, (t % 4) * DM:(t % 4 + 1) * DM]).then_inc(
                            (sm.s_sp3b0, sm.s_sp3b1)[(t // 2) % 2], 16)

                for it in range(NI + 2):
                    if it < NI:
                        stage_B(it)
                    if 1 <= it < NI + 1:
                        stage_D(it - 1)
                        stage_F(it - 1)
                    if 2 <= it < NI + 2:
                        stage_H(it - 2)

    def _p3b():
        pass

    def _p4():
        wg_sb, wd_sb = hold["wg_sb"], hold["wd_sb"]
        NC4 = 8
        with (
            nc.sbuf_tensor("x2T", [128, 2 * 8 * 512], BF16) as x2T,
            nc.sbuf_tensor("wu_sl", [128, 4 * 8 * 128], BF16) as wu_sl,
            nc.sbuf_tensor("sgbuf", [128, 2 * 512], BF16) as sgbuf,
            nc.sbuf_tensor("hswb", [128, 24 * 512], BF16) as hswb,
            nc.sbuf_tensor("y_sb", [128, 4 * DM], F32) as y_sb,
            nc.sbuf_tensor("x1f", [128, 2 * DM], F32) as x1f,
            nc.sbuf_tensor("ostage", [128, 2 * DM], F32) as ostage,
            nc.sbuf_tensor("bst4", [128, 2 * 12], F32) as bst4,
            nc.sbuf_tensor("agg4", [128, 2 * 2], F32) as agg4,
            nc.sbuf_tensor("sc4", [128, 2 * 5], F32) as sc4,
            nc.psum_tensor("gp", [128, 2 * 512], F32) as gp,
            nc.psum_tensor("upp", [128, 2 * 512], F32) as upp,
            nc.psum_tensor("yp", [128, 2 * 1024], F32) as yp,
            nc.Block() as blk,
        ):
            s_wu = (sm.s_wua, sm.s_wub, sm.s_wuc, sm.s_wud)
            y_tgts = ((yp, 0), (yp, 1024), (gp, 0), (upp, 0))

            @blk.gpsimd
            def _(g):
                def spill(gt):
                    g.wait_ge(sm.s_oady, gt + 1)
                    g.dma_start(out_d[gt * 128:(gt + 1) * 128, :],
                                ostage[:, (gt % 2) * DM:(gt % 2 + 1) * DM]).then_inc(
                        (sm.s_out0, sm.s_out1)[gt % 2], 16)
                g.wait_ge(sm.s_sp3b0, 32 * NT // 2)
                g.wait_ge(sm.s_sp3b1, 32 * NT // 2)
                for c in range(NC4):
                    for hc in range(24):
                        if c > 0 or hc >= 4:
                            g.wait_ge(sm.s_gu, 24 * c + hc - 3)
                        sl = hc % 4
                        g.dma_start(
                            wu_sl[:, sl * 1024:(sl + 1) * 1024].rearrange(
                                "p (dc e) -> p dc e", dc=8),
                            wuT_d[:, hc * 128:(hc + 1) * 128].rearrange(
                                "(dc p) e -> p dc e", p=128)).then_inc(s_wu[sl], 16)
                    if c >= 1:
                        for tt in range(4):
                            spill(4 * (c - 1) + tt)
                for tt in range(4):
                    spill(4 * (NC4 - 1) + tt)
                g.wait_ge(sm.s_out0, 16 * NT // 2)
                g.wait_ge(sm.s_out1, 16 * NT // 2)

            @blk.sync
            def _(sy):
                sy.wait_ge(sm.s_sp3b0, 32 * NT // 2)
                sy.wait_ge(sm.s_sp3b1, 32 * NT // 2)
                def x1loads(c):
                    for tt in range(4):
                        gt = 4 * c + tt
                        if gt >= 2:
                            sy.wait_ge(sm.s_oady, gt - 1)
                        sy.dma_start(x1f[:, (gt % 2) * DM:(gt % 2 + 1) * DM],
                                     x1_d[gt * 128:(gt + 1) * 128, :]).then_inc(
                            (sm.s_x1f0, sm.s_x1f1)[gt % 2], 16)
                for c in range(NC4):
                    if c >= 1:
                        if c >= 2:
                            sy.wait_ge(sm.s_gu, 24 * (c - 1))
                        for dc in range(8):
                            sy.dma_start_transpose(
                                x2T[:, (c % 2) * 8 * 512 + dc * 512:
                                    (c % 2) * 8 * 512 + (dc + 1) * 512],
                                xn2_d[c * 512:(c + 1) * 512,
                                      dc * 128:(dc + 1) * 128]).then_inc(
                                (sm.s_x2t0, sm.s_x2t1)[c % 2], 16)
                        x1loads(c - 1)
                x1loads(NC4 - 1)

            @blk.tensor
            def _(te):
                te.wait_ge(sm.s_wgu, 16 * 6)
                for c in range(NC4):
                    te.wait_ge((sm.s_x2t0, sm.s_x2t1)[c % 2], 16 * 8 * (c // 2 + 1))
                    if c >= 1:
                        te.wait_ge(sm.s_yraw, 4 * c - 2)
                    par = c % 2
                    for hc in range(24):
                        ghc = 24 * c + hc
                        if ghc >= 2:
                            te.wait_ge(sm.s_hu, ghc - 1)
                        te.wait_ge(s_wu[hc % 4], 16 * (6 * c + hc // 4 + 1))
                        gsl = gp[:, (ghc % 2) * 512:(ghc % 2 + 1) * 512]
                        usl = upp[:, (ghc % 2) * 512:(ghc % 2 + 1) * 512]

                        def xsl(dc):
                            if c == 0:
                                return hold["x2T0"][:, dc * 512:(dc + 1) * 512]
                            return x2T[:, par * 8 * 512 + dc * 512:
                                       par * 8 * 512 + (dc + 1) * 512]
                        for dc in range(8):
                            mm = te.matmul(gsl,
                                           wg_sb[:, dc * FF + hc * 128:dc * FF + (hc + 1) * 128],
                                           xsl(dc), start=(dc == 0), stop=(dc == 7))
                        mm.then_inc(sm.s_sp30, 1)
                        for dc in range(8):
                            mm = te.matmul(usl,
                                           wu_sl[:, (hc % 4) * 1024 + dc * 128:
                                                 (hc % 4) * 1024 + (dc + 1) * 128],
                                           xsl(dc), start=(dc == 0), stop=(dc == 7))
                        mm.then_inc(sm.s_gu, 1)
                    te.wait_ge(sm.s_sg, 24 * (c + 1))
                    te.wait_ge(sm.s_hu, 24 * (c + 1))
                    for hc in range(24):
                        for tt in range(4):
                            tgt, off = y_tgts[tt]
                            for half in (0, 1):
                                mm = te.matmul(
                                    tgt[:, off + half * 512:off + (half + 1) * 512],
                                    hswb[:, hc * 512 + tt * 128:hc * 512 + (tt + 1) * 128],
                                    wd_sb[:, hc * DM + half * 512:hc * DM + (half + 1) * 512],
                                    start=(hc == 0), stop=(hc == 23))
                    mm.then_inc(sm.s_dpe, 1)

            @blk.scalar
            def _(s):
                def ln4_acts(c):
                    for tt in range(4):
                        gt = 4 * c + tt
                        p2 = gt % 2
                        s.wait_ge(sm.s_v5a, gt + 1)
                        s.activation(sc4[:, p2 * 5 + 1:p2 * 5 + 2],
                                     sc4[:, p2 * 5:p2 * 5 + 1], AF.Sqrt).then_inc(sm.s_s5a, 1)
                        s.wait_ge(sm.s_v5b, gt + 1)
                        if gt >= 2:
                            s.wait_ge((sm.s_out0, sm.s_out1)[p2], 16 * (gt // 2))
                        s.activation(ostage[:, p2 * DM:(p2 + 1) * DM],
                                     y_sb[:, (gt % 4) * DM:(gt % 4 + 1) * DM], AF.Identity,
                                     bias=sc4[:, p2 * 5 + 4:p2 * 5 + 5],
                                     scale=sc4[:, p2 * 5 + 3:p2 * 5 + 4]).then_inc(sm.s_s5b, 1)
                for c in range(NC4):
                    for hc in range(24):
                        ghc = 24 * c + hc
                        s.wait_ge(sm.s_sp30, ghc + 1)
                        if ghc >= 2:
                            s.wait_ge(sm.s_hu, ghc - 1)
                        s.activation(sgbuf[:, (ghc % 2) * 512:(ghc % 2 + 1) * 512],
                                     gp[:, (ghc % 2) * 512:(ghc % 2 + 1) * 512],
                                     AF.Silu, scale=DQ).then_inc(sm.s_sg, 1)
                    if c >= 1:
                        ln4_acts(c - 1)
                ln4_acts(NC4 - 1)

            @blk.vector
            def _(v):
                hz = [0]

                def hop(instr):
                    instr.then_inc(sm.s_bn4, 1)
                    hz[0] += 1
                    v.wait_ge(sm.s_bn4, hz[0])

                def tt_block(c):
                    for hc in range(24):
                        ghc = 24 * c + hc
                        v.wait_ge(sm.s_sg, ghc + 1)
                        v.wait_ge(sm.s_gu, ghc + 1)
                        if c >= 1:
                            v.wait_ge(sm.s_dpe, c)
                        v.tensor_tensor(hswb[:, hc * 512:(hc + 1) * 512],
                                        upp[:, (ghc % 2) * 512:(ghc % 2 + 1) * 512],
                                        sgbuf[:, (ghc % 2) * 512:(ghc % 2 + 1) * 512],
                                        ALU.mult).then_inc(sm.s_hu, 1)

                def y_copies(c):
                    v.wait_ge(sm.s_dpe, c + 1)
                    for tt in (2, 3, 0, 1):
                        gt = 4 * c + tt
                        if gt >= 4:
                            v.wait_ge(sm.s_s5b, gt - 3)
                        tgt, off = y_tgts[tt]
                        v.tensor_copy(y_sb[:, (gt % 4) * DM:(gt % 4 + 1) * DM],
                                      tgt[:, off:off + 1024]).then_inc(sm.s_yraw, 1)

                def tail_block(c):
                    for tt in range(4):
                        gt = 4 * c + tt
                        p2 = gt % 2
                        sl = (gt % 4) * DM
                        v.wait_ge(sm.s_yraw, 4 * c + (3, 4, 1, 2)[tt])
                        v.bn_stats(bst4[:, p2 * 12:p2 * 12 + 6], y_sb[:, sl:sl + 512])
                        hop(v.bn_stats(bst4[:, p2 * 12 + 6:p2 * 12 + 12],
                                       y_sb[:, sl + 512:sl + 1024]))
                        hop(v.bn_aggr(agg4[:, p2 * 2:p2 * 2 + 2],
                                      bst4[:, p2 * 12:p2 * 12 + 12]))
                        v.tensor_scalar(sc4[:, p2 * 5:p2 * 5 + 1],
                                        agg4[:, p2 * 2 + 1:p2 * 2 + 2],
                                        S2 * S2, EPS, ALU.mult, ALU.add).then_inc(sm.s_v5a, 1)
                        v.wait_ge(sm.s_s5a, gt + 1)
                        hop(v.reciprocal(sc4[:, p2 * 5 + 2:p2 * 5 + 3],
                                         sc4[:, p2 * 5 + 1:p2 * 5 + 2]))
                        v.tensor_scalar_mul(sc4[:, p2 * 5 + 3:p2 * 5 + 4],
                                            sc4[:, p2 * 5 + 2:p2 * 5 + 3], S2)
                        v.scalar_tensor_tensor(sc4[:, p2 * 5 + 4:p2 * 5 + 5],
                                               agg4[:, p2 * 2:p2 * 2 + 1], -S2,
                                               sc4[:, p2 * 5 + 2:p2 * 5 + 3],
                                               ALU.mult, ALU.mult).then_inc(sm.s_v5b, 1)
                        v.wait_ge(sm.s_s5b, gt + 1)
                        v.wait_ge((sm.s_x1f0, sm.s_x1f1)[p2], 16 * (gt // 2 + 1))
                        if gt >= 2:
                            v.wait_ge((sm.s_out0, sm.s_out1)[p2], 16 * (gt // 2))
                        v.tensor_add(ostage[:, p2 * DM:(p2 + 1) * DM],
                                     ostage[:, p2 * DM:(p2 + 1) * DM],
                                     x1f[:, (gt % 2) * DM:(gt % 2 + 1) * DM]).then_inc(sm.s_oady, 1)

                for c in range(NC4):
                    tt_block(c)
                    if c >= 1:
                        tail_block(c - 1)
                    y_copies(c)
                tail_block(NC4 - 1)

    fw = {1: [("s_sp10", 16 * NT // 2), ("s_sp11", 16 * NT // 2)],
          2: [("s_qsp0", 16 * 4 * NCH), ("s_qsp1", 16 * 4 * NCH),
              ("s_ksp0", 16 * 4 * NCH), ("s_ksp1", 16 * 4 * NCH)],
          3: [("s_cs00", 16 * 4 * 8), ("s_cs01", 16 * 4 * 8),
              ("s_cs10", 16 * 4 * 8), ("s_cs11", 16 * 4 * 8)],
          4: [("s_sp3b0", 32 * NT // 2), ("s_sp3b1", 32 * NT // 2), ("s_wgu", 16 * 6)],
          5: [("s_sp3b0", 32 * NT // 2), ("s_sp3b1", 32 * NT // 2), ("s_wgu", 16 * 6)]}

    _p1a()
    if phases >= 2:
        hold["v_sb"] = es_v.enter_context(
            nc.sbuf_tensor("v_sb", [128, 32 * 1040], BF16, side="right"))
        if phases >= 3:
            hold["qt_buf"] = es_v.enter_context(
                nc.sbuf_tensor("qt_buf", [128, 2 * S], BF16, side="right"))
            hold["kt_buf"] = es_v.enter_context(
                nc.sbuf_tensor("kt_buf", [128, 2 * S], BF16, side="right"))
            hold["qstage"] = es_v.enter_context(
                nc.sbuf_tensor("qstage", [128, 2 * 512], BF16, side="right"))
            hold["kstage"] = es_v.enter_context(
                nc.sbuf_tensor("kstage", [128, 2 * 512], BF16, side="right"))
        _p1b()
    es_qkv.close()
    if phases >= 3:
        hold["ctxb"] = es.enter_context(
            nc.sbuf_tensor("p3_ctx", [128, 8 * 1040], BF16))
        _p2()
    es_v.close()
    if phases >= 4:
        hold["wg_sb"] = es_ffn.enter_context(
            nc.sbuf_tensor("wg_sb", [128, 8 * FF], BF16, side="right"))
        hold["wd_sb"] = es_ffn.enter_context(
            nc.sbuf_tensor("wd_sb", [128, 24 * DM], BF16, side="right"))
        hold["x2T0"] = es_ffn.enter_context(
            nc.sbuf_tensor("x2T0", [128, 8 * 512], BF16))
        _p3a()
    if phases >= 5:
        _p3b()
    if phases >= 6:
        _p4()
    if phases < 6:
        with nc.Block() as blk:
            @blk.gpsimd
            def _(g):
                for sname, tot in fw[phases]:
                    g.wait_ge(sems[sname], tot)
    es_ffn.close()
    es.close()
    return nc


def kernel(L0_attn_hidden_states, L0_attn_Wq, L0_attn_bq, L0_attn_Wk, L0_attn_bk,
           L0_attn_Wv, L0_attn_bv, L0_ffn_w_gate, L0_ffn_w_up, L0_ffn_w_down):
    x = np.ascontiguousarray(np.asarray(L0_attn_hidden_states))
    wq = np.ascontiguousarray(np.asarray(L0_attn_Wq).T)
    wk = np.ascontiguousarray(np.asarray(L0_attn_Wk).T)
    wv = np.ascontiguousarray(np.asarray(L0_attn_Wv).T)
    bq = np.ascontiguousarray(np.asarray(L0_attn_bq))
    bk = np.ascontiguousarray(np.asarray(L0_attn_bk))
    bv = np.ascontiguousarray(np.asarray(L0_attn_bv))
    wg = np.ascontiguousarray(np.asarray(L0_ffn_w_gate).T)
    wu = np.ascontiguousarray(np.asarray(L0_ffn_w_up).T)
    wd = np.ascontiguousarray(np.asarray(L0_ffn_w_down).T)

    nc = build_nc()
    in_maps = [{"x_in": x[b], "wqT": wq, "wkT": wk, "wvT": wv,
                "bq": bq, "bk": bk, "bv": bv,
                "wgT": wg, "wuT": wu, "wdT": wd} for b in range(8)]
    res = run_bass_kernel_spmd(nc, in_maps, core_ids=list(range(8)))
    return np.stack([res.results[b]["out"] for b in range(8)], axis=0)
